# version 12
# baseline (speedup 1.0000x reference)
import math
import numpy as np

import concourse.bass as bass
import concourse.mybir as mybir
from concourse.bass_utils import run_bass_kernel_spmd

F32 = mybir.dt.float32
BF16 = mybir.dt.bfloat16
AF = mybir.ActivationFunctionType
ALU = mybir.AluOpType

D = 2048
NH = 16
T = 1024
NCH = 8
C = 128
NHC = 64
EPS = 1e-5
W2SCALE = 1.0 / math.sqrt(8192.0)

ENGS = ("pe", "act", "dve", "pool", "sp")


def _slopes():
    start = 2.0 ** (-(2.0 ** (-(math.log2(NH) - 3))))
    s = np.array([start * start ** i for i in range(NH)], dtype=np.float64)
    return np.exp(-s)


def _popcount_parity(a):
    v = a.copy()
    out = np.zeros_like(v)
    while v.any():
        out ^= (v & 1)
        v >>= 1
    return out


def host_constants(core_id):
    import ml_dtypes
    r = _slopes()
    t0 = (core_id % 2) * T
    t0p = (1 - core_id % 2) * T
    freq = (1.0 / 1024.0) ** np.linspace(0.0, 1.0, 32)
    t = np.arange(t0, t0 + T, dtype=np.float64)
    theta = t[None, :] * freq[:, None]
    cF = np.ones((128, T), dtype=np.float64)
    sF = np.zeros((128, T), dtype=np.float64)
    cF[0:32] = np.cos(theta); cF[64:96] = np.cos(theta)
    sF[0:32] = np.sin(theta); sF[64:96] = np.sin(theta)
    tp = np.arange(t0p, t0p + T, dtype=np.float64)
    thetap = tp[None, :] * freq[:, None]
    cFp = np.ones((128, T), dtype=np.float64)
    sFp = np.zeros((128, T), dtype=np.float64)
    cFp[0:32] = np.cos(thetap); cFp[64:96] = np.cos(thetap)
    sFp[0:32] = np.sin(thetap); sFp[64:96] = np.sin(thetap)
    P = np.zeros((128, 128), dtype=np.float64)
    for d in range(64):
        P[d, d + 64] = 1.0
        P[d + 64, d] = -1.0
    PT = P.T.copy()
    idx = np.arange(C, dtype=np.float64)
    diff = idx[None, :] - idx[:, None]
    dmaskT = np.zeros((C, NH * C))
    qdecT = np.zeros((C, NH * C))
    kd = np.zeros((C, NH))
    for h in range(NH):
        dmaskT[:, h * C:(h + 1) * C] = np.where(diff >= 0, r[h] ** np.maximum(diff, 0.0), 0.0)
        qdecT[:, h * C:(h + 1) * C] = (r[h] ** (idx + 1.0))[None, :]
        kd[:, h] = r[h] ** (C - 1.0 - idx)
    kd2 = np.zeros((C, NH * NCH))
    for h in range(NH):
        for c in range(NCH):
            kd2[:, h * NCH + c] = (r[h] ** (C - 1.0 - idx)) * (r[h] ** (C * (NCH - 1 - c)))
    gate = 1.0 if (core_id % 2) == 1 else 0.0
    rcg = np.zeros((C, NH * NCH))
    for h in range(NH):
        for cl in range(NCH):
            rcg[:, h * NCH + cl] = (r[h] ** (C * cl)) * gate
    i = np.arange(128)
    H128 = np.where(_popcount_parity(i[:, None] & i[None, :]) == 0, 1.0, -1.0)
    ident = np.eye(128)
    return {
        "cF": cF.astype(np.float32), "sF": sF.astype(np.float32),
        "cFp": cFp.astype(np.float32), "sFp": sFp.astype(np.float32),
        "PT": PT.astype(ml_dtypes.bfloat16), "dmaskT": dmaskT.astype(np.float32),
        "qdecT": qdecT.astype(np.float32), "kd": kd.astype(np.float32),
        "kd2": kd2.astype(np.float32),
        "rcg": rcg.astype(np.float32),
        "H128": H128.astype(ml_dtypes.bfloat16), "identity": ident.astype(ml_dtypes.bfloat16),
    }


def tile_lhsT(W):
    K, M = W.shape
    nk, nm = K // 128, M // 128
    t = W.reshape(nk, 128, nm, 128).transpose(2, 1, 0, 3)
    return np.ascontiguousarray(t.reshape(nm, 128, nk * 128))


def bitrev6(x):
    r = 0
    for b in range(6):
        r |= ((x >> b) & 1) << (5 - b)
    return r


def fwht_schedule(order):
    bits = [32, 16, 8, 4, 2, 1]
    lev = [-1] * 64
    sched = [[] for _ in range(64)]
    for mi, m in enumerate(order):
        lev[m] = 0
        changed = True
        while changed:
            changed = False
            for l in range(6):
                b = bits[l]
                for c in range(64):
                    if (c & b) == 0 and lev[c] == l and lev[c | b] == l:
                        sched[mi].append((c, c | b))
                        lev[c] = l + 1
                        lev[c | b] = l + 1
                        changed = True
    assert all(v == 6 for v in lev)
    return sched


class Plan:

    STRICT = ("dve", "act", "pool")

    def __init__(self, nc):
        self.nc = nc
        self.ops = {e: [] for e in ENGS}
        self.sems = {}
        self.cnt = {}
        self.hw = {e: {} for e in ENGS}
        for e in self.STRICT:
            self.s(e)

    def s(self, name):
        if name not in self.sems:
            self.sems[name] = self.nc.alloc_semaphore(name)
            self.cnt[name] = 0
        return name

    def v(self, name):
        return self.cnt.get(name, 0)

    def emit(self, eng, fn, waits=(), inc=None, amt=1):
        real_waits = []
        for (n, val) in waits:
            if val is None or val <= 0:
                continue
            self.s(n)
            if self.hw[eng].get(n, 0) >= val:
                continue
            self.hw[eng][n] = val
            real_waits.append((self.sems[n], val))
        target = None
        dma_sem = None
        if eng in self.STRICT and (inc is None or inc == eng):
            self.cnt[eng] += 1
            target = self.cnt[eng]
            self.ops[eng].append((tuple(real_waits), fn, self.sems[eng], 1, True))
            return target
        if inc is not None:
            self.s(inc)
            self.cnt[inc] += amt
            target = self.cnt[inc]
            dma_sem = self.sems[inc]
        self.ops[eng].append((tuple(real_waits), fn, dma_sem, amt, False))
        return target

    def dma(self, eng, out, in_, waits=(), inc=None):
        w = list(waits)
        if inc is not None and self.v(inc) > 0:
            w.append((inc, self.v(inc)))
        if eng in self.STRICT:
            w.append((eng, self.v(eng)))
        return self.emit(eng, (lambda e, o=out, i=in_: e.dma_start(out=o, in_=i)),
                         w, inc, 16)

    def realize(self):
        nc = self.nc
        plan = self

        def replay(stream):
            def f(e):
                selfsem = plan.sems.get(stream) if stream in plan.STRICT else None
                self_idx = 0
                for (waits, fn, sem, amt, is_self) in plan.ops[stream]:
                    for (s, v) in waits:
                        e.wait_ge(s, v)
                    if is_self and self_idx > 0:
                        e.wait_ge(selfsem, self_idx)
                    inst = fn(e)
                    if sem is not None:
                        inst.then_inc(sem, amt)
                    if is_self:
                        self_idx += 1
            return f

        with nc.Block() as blk:
            blk.tensor(replay("pe"))
            blk.scalar(replay("act"))
            blk.vector(replay("dve"))
            blk.gpsimd(replay("pool"))
            blk.sync(replay("sp"))


def build_program(taps=(), use_collective=True, upto="all"):
    from contextlib import ExitStack
    nc = bass.Bass("TRN2", target_bir_lowering=False)
    p = Plan(nc)
    r_heads = _slopes()

    x_h = nc.dram_tensor("x", [D, T], F32, kind="ExternalInput")
    wq_h = nc.dram_tensor("wq_t", [16, 128, 2048], F32, kind="ExternalInput")
    wk_h = nc.dram_tensor("wk_t", [16, 128, 2048], F32, kind="ExternalInput")
    wv_h = nc.dram_tensor("wv_t", [16, 128, 2048], F32, kind="ExternalInput")
    wo_h = nc.dram_tensor("wo_t", [16, 128, 2048], F32, kind="ExternalInput")
    w1_h = nc.dram_tensor("w1_t", [64, 128, 2048], F32, kind="ExternalInput")
    w3_h = nc.dram_tensor("w3_t", [64, 128, 2048], F32, kind="ExternalInput")
    w2_h = nc.dram_tensor("w2_t", [4, 64, 128, 512], F32, kind="ExternalInput")
    rmsc_h = nc.dram_tensor("rms_cols", [128, 16], F32, kind="ExternalInput")
    cF_h = nc.dram_tensor("cF", [128, T], F32, kind="ExternalInput")
    sF_h = nc.dram_tensor("sF", [128, T], F32, kind="ExternalInput")
    xp2_h = nc.dram_tensor("xp2", [D, T], F32, kind="ExternalInput")
    cFp_h = nc.dram_tensor("cFp", [128, T], F32, kind="ExternalInput")
    sFp_h = nc.dram_tensor("sFp", [128, T], F32, kind="ExternalInput")
    PT_h = nc.dram_tensor("PT", [128, 128], BF16, kind="ExternalInput")
    dmT_h = nc.dram_tensor("dmaskT", [128, NH * 128], F32, kind="ExternalInput")
    qdT_h = nc.dram_tensor("qdecT", [128, NH * 128], F32, kind="ExternalInput")
    kd_h = nc.dram_tensor("kd", [128, NH], F32, kind="ExternalInput")
    kd2_h = nc.dram_tensor("kd2", [128, NH * NCH], F32, kind="ExternalInput")
    rcg_h = nc.dram_tensor("rcg", [128, NH * NCH], F32, kind="ExternalInput")
    H128_h = nc.dram_tensor("H128", [128, 128], BF16, kind="ExternalInput")
    id_h = nc.dram_tensor("identity", [128, 128], BF16, kind="ExternalInput")
    out_h = nc.dram_tensor("out", [D, T], F32, kind="ExternalOutput")

    def scratch(name, shape, dtype):
        kind = "ExternalOutput" if name in taps else "Internal"
        return nc.dram_tensor(name, shape, dtype, kind=kind)

    q_hbm = scratch("q_s", [NH, 128, T], BF16)
    k_hbm = scratch("k_s", [NH, 128, T], BF16)
    v_hbm = scratch("v_s", [NH, 128, T], BF16)
    qt2_hbm = scratch("qt2_s", [NH, 128, T], BF16)
    xp_hbm = scratch("xp_s", [16, 128, T], F32)

    tap_tensors = {}
    for tname in taps:
        if tname == "xln1":
            tap_tensors["xln1"] = nc.dram_tensor("tap_xln1", [128, 16 * T], BF16, kind="ExternalOutput")
        if tname == "attn":
            tap_tensors["attn"] = nc.dram_tensor("tap_attn", [128, 16 * T], BF16, kind="ExternalOutput")
        if tname == "attnraw":
            tap_tensors["attnraw"] = nc.dram_tensor("tap_attnraw", [128, 16 * T], BF16, kind="ExternalOutput")
        if tname == "xprime":
            tap_tensors["xprime"] = nc.dram_tensor("tap_xprime", [128, 16 * T], F32, kind="ExternalOutput")
        if tname == "xln2":
            tap_tensors["xln2"] = nc.dram_tensor("tap_xln2", [128, 16 * T], BF16, kind="ExternalOutput")
        if tname == "hid":
            tap_tensors["hid"] = nc.dram_tensor("tap_hid", [NHC, 128, T], BF16, kind="ExternalOutput")

    stack_persist = ExitStack()
    stack_attn = ExitStack()

    sb = stack_persist.enter_context
    actA = sb(nc.sbuf_tensor("actA", [128, 16 * T], BF16))
    eps_sb = sb(nc.sbuf_tensor("eps_sb", [1, 1], F32))
    ones_col = sb(nc.sbuf_tensor("ones_col", [128, 1], F32))
    ones_row = sb(nc.sbuf_tensor("ones_row", [1, 128], F32))
    H128s = sb(nc.sbuf_tensor("H128s", [128, 128], BF16))

    sa = stack_attn.enter_context
    idents = sa(nc.sbuf_tensor("idents", [128, 128], BF16))
    rmsc = sa(nc.sbuf_tensor("rmsc", [128, 16], F32))
    st_a = sa(nc.sbuf_tensor("st_a", [1, T], F32))
    st_b = sa(nc.sbuf_tensor("st_b", [1, T], F32))
    st_m = sa(nc.sbuf_tensor("st_m", [1, T], F32))
    x_sb = sa(nc.sbuf_tensor("x_sb", [128, 16 * T], F32))
    scrA = sa(nc.sbuf_tensor("scrA", [128, T], F32))
    scrB = sa(nc.sbuf_tensor("scrB", [128, T], F32))
    cFs = sa(nc.sbuf_tensor("cFs", [128, T], F32))
    sFs = sa(nc.sbuf_tensor("sFs", [128, T], F32))
    PTs = sa(nc.sbuf_tensor("PTs", [128, 128], BF16))
    kds = sa(nc.sbuf_tensor("kds", [128, NH], F32))
    rcgs = sa(nc.sbuf_tensor("rcgs", [128, NH * NCH], F32))
    kvf = sa(nc.sbuf_tensor("kvf", [128, 128], F32))
    S7g = sa(nc.sbuf_tensor("S7g", [128, NH * 128], BF16))
    stack_pre = ExitStack()
    sp_ = stack_pre.enter_context
    xlnp = sp_(nc.sbuf_tensor("xlnp", [128, 16 * T], BF16))
    preland = [sp_(nc.sbuf_tensor(f"preland{i}", [128, T], F32)) for i in range(2)]
    wbfp = [sp_(nc.sbuf_tensor(f"wbfp{i}", [128, 2048], BF16)) for i in range(2)]
    kpbuf = sp_(nc.sbuf_tensor("kpbuf", [128, T], BF16))
    kp_rot = sp_(nc.sbuf_tensor("kp_rot", [128, T], BF16))
    vpbuf = sp_(nc.sbuf_tensor("vpbuf", [128, T], BF16))
    cFp = sp_(nc.sbuf_tensor("cFp_s", [128, T], F32))
    sFp = sp_(nc.sbuf_tensor("sFp_s", [128, T], F32))
    kd2s = sp_(nc.sbuf_tensor("kd2s", [128, NH * NCH], F32))
    ktp = [sp_(nc.sbuf_tensor(f"ktp{i}", [128, 128], BF16)) for i in range(2)]
    vtp = [sp_(nc.sbuf_tensor(f"vtp{i}", [128, 128], BF16)) for i in range(2)]

    x_ap3 = x_h.ap().rearrange("(k p) t -> p k t", p=128)
    xs_ap3 = x_sb[:, :].rearrange("p (k t) -> p k t", t=T)
    for g in range(4):
        p.dma("sp", xs_ap3[:, g * 4:(g + 1) * 4, :], x_ap3[:, g * 4:(g + 1) * 4, :], inc="d_x")
    for dst, src in ((cFs, cF_h), (sFs, sF_h), (PTs, PT_h),
                     (kds, kd_h), (rcgs, rcg_h), (H128s, H128_h), (idents, id_h), (rmsc, rmsc_h),
                     (cFp, cFp_h), (sFp, sFp_h), (kd2s, kd2_h)):
        p.dma("sp", dst[:, :], src.ap(), inc="d_c")
    dc_all = p.v("d_c")
    p.emit("dve", lambda e: e.memset(eps_sb[:, :], EPS), inc="dve")
    p.emit("dve", lambda e: e.memset(ones_col[:, :], 1.0), inc="dve")
    ones_ready = p.emit("dve", lambda e: e.memset(ones_row[:, :], 1.0), inc="dve")


    xp2_ap3 = xp2_h.ap().rearrange("(k p) t -> p k t", p=128)
    ctxpa = ExitStack()
    pp_sx = ctxpa.enter_context(nc.psum_tensor("pp_sx", [1, T], F32))
    pp_sx2 = ctxpa.enter_context(nc.psum_tensor("pp_sx2", [1, T], F32))
    sq_d = {}
    sx_d = {}
    sx2_d = {}
    for k in range(16):
        par = k % 2
        p.dma("sp", preland[par][:, :], xp2_ap3[:, k, :],
              waits=[("act", sq_d.get(k - 2, 0)), ("pe", sx_d.get(k - 2, 0))],
              inc=f"d_p{par}")
        dw = [(f"d_p{par}", 16 * (k // 2 + 1))]
        scr = scrA if par == 0 else scrB
        sq_d[k] = p.emit("act", lambda e, par=par, scr=scr: e.activation(
            out=scr[:, :], in_=preland[par][:, :], func=AF.Square),
            waits=dw + [("pe", sx2_d.get(k - 2, 0))], inc="act")
        for half in range(2):
            sx_d[k] = p.emit("pe", lambda e, par=par, k=k, half=half: e.matmul(
                pp_sx[:, half * 512:(half + 1) * 512], ones_col[:, :],
                preland[par][:, half * 512:(half + 1) * 512], start=(k == 0), stop=(k == 15)),
                waits=dw + [("dve", ones_ready)], inc=("pe" if half == 1 else None))
        for half in range(2):
            sx2_d[k] = p.emit("pe", lambda e, par=par, k=k, half=half: e.matmul(
                pp_sx2[:, half * 512:(half + 1) * 512], ones_col[:, :],
                (scrA if par == 0 else scrB)[:, half * 512:(half + 1) * 512],
                start=(k == 0), stop=(k == 15)),
                waits=[("act", sq_d[k])], inc=("pe" if half == 1 else None))
    p.emit("dve", lambda e: e.tensor_scalar_mul(st_m[:, :], pp_sx[:, :], 1.0 / D),
           waits=[("pe", sx2_d[15])])
    p.emit("dve", lambda e: e.tensor_scalar_mul(st_a[:, :], pp_sx2[:, :], 1.0 / D))
    p.emit("dve", lambda e: e.tensor_mul(st_b[:, :], st_m[:, :], st_m[:, :]))
    vp_done = p.emit("dve", lambda e: e.tensor_sub(st_a[:, :], st_a[:, :], st_b[:, :]), inc="dve")
    p.emit("act", lambda e: e.activation(out=st_b[:, :], in_=st_a[:, :], func=AF.Ln, bias=eps_sb[:, :]),
           waits=[("dve", vp_done)])
    rp_done = p.emit("act", lambda e: e.activation(out=st_a[:, :], in_=st_b[:, :], func=AF.Exp,
                                                   scale=-0.5), inc="act")
    ctxpa.close()
    ctxpb = ExitStack()
    pp_bm = ctxpb.enter_context(nc.psum_tensor("pp_bm", [128, T], F32))
    pp_br = ctxpb.enter_context(nc.psum_tensor("pp_br", [128, T], F32))
    bcp = None
    for half in range(2):
        p.emit("pe", lambda e, half=half: e.matmul(
            pp_bm[:, half * 512:(half + 1) * 512], ones_row[:, :],
            st_m[:, half * 512:(half + 1) * 512], start=True, stop=True),
            waits=[("act", rp_done)])
        bcp = p.emit("pe", lambda e, half=half: e.matmul(
            pp_br[:, half * 512:(half + 1) * 512], ones_row[:, :],
            st_a[:, half * 512:(half + 1) * 512], start=True, stop=True),
            inc=("pe" if half == 1 else None))
    nrm_d = {}
    for k in range(16):
        par = k % 2
        p.dma("sp", preland[par][:, :], xp2_ap3[:, k, :],
              waits=[("dve", nrm_d.get(k - 2, 0)), ("act", sq_d[15] if k < 2 else 0),
                     ("pe", sx_d[15] if k < 2 else 0)],
              inc=f"d_p{par}")
        dwv = [(f"d_p{par}", 16 * (8 + k // 2 + 1)), ("pe", bcp)]
        p.emit("dve", lambda e, par=par: e.tensor_sub(scrA[:, :], preland[par][:, :], pp_bm[:, :]),
               waits=dwv)
        nrm_d[k] = p.emit("dve", lambda e, k=k: e.tensor_mul(
            xlnp[:, k * T:(k + 1) * T], scrA[:, :], pp_br[:, :]), inc="dve")
    nrm_done = nrm_d[15]
    ctxpb.close()
    ctxpc = ExitStack()
    ppj = ctxpc.enter_context(nc.psum_tensor("ppj", [128, T], F32))
    pp_rot = ctxpc.enter_context(nc.psum_tensor("pp_rot", [128, T], F32))
    ppT1a = ctxpc.enter_context(nc.psum_tensor("ppT1a", [128, 128], BF16))
    ppT2a = ctxpc.enter_context(nc.psum_tensor("ppT2a", [128, 128], BF16))
    ppS7 = ctxpc.enter_context(nc.psum_tensor("ppS7", [128, 128], F32))
    ppT1b = pp_rot[:, 0:64].bitcast(BF16)
    ppT2b = pp_rot[:, 512:576].bitcast(BF16)
    wl_rd = {}
    last_ev = 0
    prev_cast = 0
    prev_m2p = 0
    prev_ktp = {0: 0, 1: 0}
    prev_vtp = {0: 0, 1: 0}
    s7_cast = 0
    dp_cnt = 32
    for h in range(NH):
        for wsel, (wt_h, dstbuf) in enumerate(((wk_h, kpbuf), (wv_h, vpbuf))):
            wpar = (2 * h + wsel) % 2
            for half in range(2):
                p.dma("sp", preland[half][:, :], wt_h.ap()[h][:, half * 1024:(half + 1) * 1024],
                      waits=[("dve", prev_cast), ("dve", nrm_done)], inc=f"d_p{half}")
            gen = dp_cnt // 2 + 1
            dp_cnt += 2
            for half in range(2):
                prev_cast = p.emit("dve", lambda e, half=half, wpar=wpar: e.tensor_copy(
                    wbfp[wpar][:, half * 1024:(half + 1) * 1024], preland[half][:, :]),
                    waits=[(f"d_p{half}", 16 * gen), ("pe", wl_rd.get(wpar, 0))], inc="dve")
            mmv = None
            for k in range(16):
                for half in range(2):
                    mmv = p.emit("pe", lambda e, k=k, half=half, wpar=wpar: e.matmul(
                        ppj[:, half * 512:(half + 1) * 512],
                        wbfp[wpar][:, k * 128:(k + 1) * 128],
                        xlnp[:, k * T + half * 512: k * T + (half + 1) * 512],
                        start=(k == 0), stop=(k == 15)),
                        waits=([("dve", max(prev_cast, nrm_done)),
                                ("act", last_ev)] if (k == 0 and half == 0) else ()),
                        inc=("pe" if (k == 15 and half == 1) else None))
            wl_rd[wpar] = mmv
            ev = p.emit("act", lambda e, dstbuf=dstbuf: e.activation(
                out=dstbuf[:, :], in_=ppj[:, :], func=AF.Copy),
                waits=[("pe", mmv)], inc="act")
            last_ev = ev
        p.emit("pe", lambda e: e.matmul(pp_rot[:, 0:512], PTs[:, :], kpbuf[:, 0:512],
                                        start=True, stop=True),
               waits=[("act", ev), ("dve", prev_m2p), ("d_c", dc_all)])
        prot = p.emit("pe", lambda e: e.matmul(pp_rot[:, 512:1024], PTs[:, :], kpbuf[:, 512:1024],
                                               start=True, stop=True), inc="pe")
        p.emit("dve", lambda e: e.tensor_mul(scrA[:, :], kpbuf[:, :], cFp[:, :]),
               waits=[("pe", prot)])
        prev_m2p = p.emit("dve", lambda e: e.tensor_mul(scrB[:, :], pp_rot[:, :], sFp[:, :]), inc="dve")
        krotp = p.emit("dve", lambda e: e.tensor_add(kp_rot[:, :], scrA[:, :], scrB[:, :]), inc="dve")
        av = None
        for c in range(NCH):
            ce = c % 2
            cs = slice(c * 128, (c + 1) * 128)
            pT1 = ppT1a if ce == 0 else ppT1b
            pT2 = ppT2a if ce == 0 else ppT2b
            p.emit("pe", lambda e, cs=cs, pT1=pT1: e.transpose(pT1[:, :], kp_rot[:, cs], idents[:, :]),
                   waits=[("dve", max(krotp, prev_m2p)), ("act", prev_ktp[ce])])
            t2v = p.emit("pe", lambda e, cs=cs, pT2=pT2: e.transpose(pT2[:, :], vpbuf[:, cs],
                                                                     idents[:, :]),
                         waits=[("act", prev_vtp[ce])], inc="pe")
            prev_ktp[ce] = p.emit("act", lambda e, h=h, c=c, ce=ce, pT1=pT1: e.activation(
                out=ktp[ce][:, :], in_=pT1[:, :], func=AF.Copy,
                scale=kd2s[:, h * NCH + c: h * NCH + c + 1]),
                waits=[("pe", t2v)], inc="act")
            prev_vtp[ce] = p.emit("act", lambda e, ce=ce, pT2=pT2: e.activation(
                out=vtp[ce][:, :], in_=pT2[:, :], func=AF.Copy),
                waits=[("pe", t2v)], inc="act")
            av = p.emit("pe", lambda e, c=c, ce=ce: e.matmul(ppS7[:, :], ktp[ce][:, :], vtp[ce][:, :],
                                                             start=(c == 0), stop=(c == NCH - 1)),
                        waits=[("act", max(prev_ktp[ce], prev_vtp[ce]))], inc="pe")
        s7_cast = p.emit("act", lambda e, h=h: e.activation(
            out=S7g[:, h * 128:(h + 1) * 128], in_=ppS7[:, :], func=AF.Copy),
            waits=[("pe", av)], inc="act")
    s7_done = s7_cast
    p05_pe_end = p.v("pe")
    p05_dve_end = p.v("dve")
    ctxpc.close()
    stack_pre.close()
    wland = [sa(nc.sbuf_tensor(f"wland{i}", [128, 2048], F32)) for i in range(2)]
    wbf = [sa(nc.sbuf_tensor(f"wbf{i}", [128, 2048], BF16)) for i in range(2)]
    stg = [sa(nc.sbuf_tensor(f"stg{i}", [128, T], BF16)) for i in range(2)]
    bufQ = [sa(nc.sbuf_tensor(f"bufQ{i}", [128, T], BF16)) for i in range(2)]
    bufK = [sa(nc.sbuf_tensor(f"bufK{i}", [128, T], BF16)) for i in range(2)]
    bufV = [sa(nc.sbuf_tensor(f"bufV{i}", [128, T], BF16)) for i in range(2)]
    dmTs = sa(nc.sbuf_tensor("dmTs", [128, NH * 128], F32))
    qdTs = sa(nc.sbuf_tensor("qdTs", [128, NH * 128], F32))
    q_rot = sa(nc.sbuf_tensor("q_rot", [128, T], BF16))
    k_rot = sa(nc.sbuf_tensor("k_rot", [128, T], BF16))
    qt = sa(nc.sbuf_tensor("qt", [128, T], BF16))
    qt2 = sa(nc.sbuf_tensor("qt2", [128, T], BF16))
    ktil = [sa(nc.sbuf_tensor(f"ktil{i}", [128, 128], BF16)) for i in range(2)]
    vtok = [sa(nc.sbuf_tensor(f"vtok{i}", [128, 128], BF16)) for i in range(2)]
    stil = [sa(nc.sbuf_tensor(f"stil{i}", [128, 128], BF16)) for i in range(2)]
    S7stage = sa(nc.sbuf_tensor("S7stage", [128, NH * 128], BF16))
    for dst, srch in ((dmTs, dmT_h), (qdTs, qdT_h)):
        p.dma("sp", dst[:, :], srch.ap(), inc="d_c",
              waits=[("pe", p05_pe_end), ("dve", p05_dve_end), ("act", s7_done)])
    dc_all = p.v("d_c")

    def emit_layernorm(src_sb, dst_sb, first_waits=(), x_dma_gate=False, tag="ln1"):
        ctx = ExitStack()
        p_sx = ctx.enter_context(nc.psum_tensor(f"p_sx_{tag}", [1, T], F32))
        p_sx2 = ctx.enter_context(nc.psum_tensor(f"p_sx2_{tag}", [1, T], F32))
        sq_done = {}
        sx2_done = {}
        for k in range(16):
            dmawait = [("d_x", 16 * (k // 4 + 1))] if x_dma_gate else []
            scr = scrA if k % 2 == 0 else scrB
            sq_done[k] = p.emit(
                "act", lambda e, scr=scr, k=k: e.activation(out=scr[:, :], in_=src_sb[:, k * T:(k + 1) * T],
                                                            func=AF.Square),
                waits=dmawait + [("pe", sx2_done.get(k - 2, 0))] + list(first_waits), inc="act")
            for half in range(2):
                p.emit("pe", lambda e, k=k, half=half: e.matmul(
                    p_sx[:, half * 512:(half + 1) * 512], ones_col[:, :],
                    src_sb[:, k * T + half * 512: k * T + (half + 1) * 512],
                    start=(k == 0), stop=(k == 15)),
                    waits=dmawait + (list(first_waits) + [("dve", ones_ready)] if k == 0 else []))
            for half in range(2):
                sx2_done[k] = p.emit("pe", lambda e, k=k, half=half: e.matmul(
                    p_sx2[:, half * 512:(half + 1) * 512], ones_col[:, :],
                    (scrA if k % 2 == 0 else scrB)[:, half * 512:(half + 1) * 512],
                    start=(k == 0), stop=(k == 15)),
                    waits=[("act", sq_done[k])],
                    inc=("pe" if half == 1 else None))
        stats_pe = sx2_done[15]
        p.emit("dve", lambda e: e.tensor_scalar_mul(st_m[:, :], p_sx[:, :], 1.0 / D),
               waits=[("pe", stats_pe)])
        p.emit("dve", lambda e: e.tensor_scalar_mul(st_a[:, :], p_sx2[:, :], 1.0 / D))
        p.emit("dve", lambda e: e.tensor_mul(st_b[:, :], st_m[:, :], st_m[:, :]))
        v_done = p.emit("dve", lambda e: e.tensor_sub(st_a[:, :], st_a[:, :], st_b[:, :]), inc="dve")
        p.emit("act", lambda e: e.activation(out=st_b[:, :], in_=st_a[:, :], func=AF.Ln, bias=eps_sb[:, :]),
               waits=[("dve", v_done), ("dve", ones_ready)])
        r_done = p.emit("act", lambda e: e.activation(out=st_a[:, :], in_=st_b[:, :], func=AF.Exp,
                                                      scale=-0.5), inc="act")
        ctx.close()
        ctx2 = ExitStack()
        p_bm = ctx2.enter_context(nc.psum_tensor(f"p_bm_{tag}", [128, T], F32))
        p_br = ctx2.enter_context(nc.psum_tensor(f"p_br_{tag}", [128, T], F32))
        bc_done = None
        for half in range(2):
            p.emit("pe", lambda e, half=half: e.matmul(
                p_bm[:, half * 512:(half + 1) * 512], ones_row[:, :],
                st_m[:, half * 512:(half + 1) * 512], start=True, stop=True),
                waits=[("act", r_done), ("dve", ones_ready)])
            bc_done = p.emit("pe", lambda e, half=half: e.matmul(
                p_br[:, half * 512:(half + 1) * 512], ones_row[:, :],
                st_a[:, half * 512:(half + 1) * 512], start=True, stop=True),
                inc=("pe" if half == 1 else None))
        done = None
        for k in range(16):
            p.emit("dve", lambda e, k=k: e.tensor_sub(scrA[:, :], src_sb[:, k * T:(k + 1) * T], p_bm[:, :]),
                   waits=[("pe", bc_done)])
            done = p.emit("dve", lambda e, k=k: e.tensor_mul(dst_sb[:, k * T:(k + 1) * T], scrA[:, :],
                                                             p_br[:, :]),
                          inc=("dve" if k == 15 else None))
        ctx2.close()
        return done


    def _finish_early():
        fw = []
        if p.v("d_tap"):
            fw.append(("d_tap", p.v("d_tap")))
        for nm in list(p.cnt):
            if nm.startswith("d_") and nm != "d_tap":
                fw.append((nm, p.v(nm)))
        p.emit("sp", lambda e: e.nop(), waits=fw)
        stack_ffn_local = None
        p.realize()
        return nc

    ln1_done = emit_layernorm(x_sb, actA, x_dma_gate=True,
                              first_waits=[("pe", p05_pe_end), ("dve", p05_dve_end),
                                           ("act", s7_done)])
    if "xln1" in tap_tensors:
        p.dma("act", tap_tensors["xln1"].ap(), actA[:, :], waits=[("dve", ln1_done)], inc="d_tap")
    if upto == "ln1":
        return _finish_early()

    ctxp2 = ExitStack()
    psumP = [ctxp2.enter_context(nc.psum_tensor(f"psumP{i}", [128, T], F32)) for i in range(2)]
    evac_val = {}
    store_val = {}
    cast_val = {}
    pe_proj = {}
    for widx, (wt, dst) in enumerate(((wq_h, q_hbm), (wk_h, k_hbm), (wv_h, v_hbm))):
        for m in range(16):
            gi = widx * 16 + m
            par = gi % 2
            p.dma("sp", wland[par][:, :], wt.ap()[m],
                  waits=[("dve", cast_val.get(gi - 2, p05_dve_end)),
                         ("pe", p05_pe_end if gi < 2 else 0),
                         ("act", s7_done if gi < 2 else 0)], inc=f"d_w{par}")
            cast_val[gi] = p.emit(
                "dve", lambda e, par=par: e.tensor_copy(wbf[par][:, :], wland[par][:, :]),
                waits=[(f"d_w{par}", 16 * (gi // 2 + 1)),
                       ("dve", p05_dve_end if gi < 2 else 0),
                       ("act", s7_done if gi < 2 else 0),
                       ("pe", pe_proj.get(gi - 2, p05_pe_end))], inc="dve")
            mm_waits = [("dve", max(cast_val[gi], ln1_done)), ("act", evac_val.get(gi - 2, 0))]
            pe_val = None
            for k in range(16):
                for half in range(2):
                    pe_val = p.emit("pe", lambda e, par=par, k=k, half=half: e.matmul(
                        psumP[par][:, half * 512:(half + 1) * 512],
                        wbf[par][:, k * 128:(k + 1) * 128],
                        actA[:, k * T + half * 512: k * T + (half + 1) * 512],
                        start=(k == 0), stop=(k == 15)),
                        waits=(mm_waits if (k == 0 and half == 0) else ()),
                        inc=("pe" if (k == 15 and half == 1) else None))
            pe_proj[gi] = pe_val
            evac_val[gi] = p.emit(
                "act", lambda e, par=par: e.activation(out=stg[par][:, :], in_=psumP[par][:, :],
                                                       func=AF.Copy),
                waits=[("pe", pe_val), (f"d_s{par}", 16 * store_val.get(gi - 2, 0))], inc="act")
            sv = p.dma("act", dst.ap()[m], stg[par][:, :], inc=f"d_s{par}")
            store_val[gi] = sv // 16
    last_cast_par = {0: cast_val[46], 1: cast_val[47]}
    ctxp2.close()
    if upto == "p2":
        return _finish_early()

    ctxp3 = ExitStack()
    ps_r = ctxp3.enter_context(nc.psum_tensor("ps_r", [128, T], F32))
    tp_T1 = ctxp3.enter_context(nc.psum_tensor("tp_T1", [128, 128], BF16))
    tp_T2 = ctxp3.enter_context(nc.psum_tensor("tp_T2", [128, 128], BF16))
    tp_S = [ctxp3.enter_context(nc.psum_tensor(f"tp_S{i}", [128, 128], F32)) for i in range(2)]
    tp_O = ctxp3.enter_context(nc.psum_tensor("tp_O", [128, 128], F32))
    tp_A = ctxp3.enter_context(nc.psum_tensor("tp_A", [128, 128], F32))

    head_end = {}
    last_ktil = 0
    last_vt = 0
    last_stil = {0: 0, 1: 0}
    last_evacO = 0
    last_kvupd = 0
    last_m2 = 0
    attn_evac_vals = np.zeros((NH, NCH), dtype=np.int64)
    qt2_dma = 0
    prev_head_kvcast = s7_done
    for h in range(NH):
        par = h % 2
        he = head_end.get(h - 2, {"pe": 0, "dve": 0, "act": 0})
        need = [(f"d_s{(h) % 2}", 16 * store_val[h]),
                (f"d_s{(16 + h) % 2}", 16 * store_val[16 + h]),
                (f"d_s{(32 + h) % 2}", 16 * store_val[32 + h]),
                ("pe", he["pe"]), ("dve", he["dve"])]
        p.dma("sp", bufQ[par][:, :], q_hbm.ap()[h], waits=need, inc=f"d_q{par}")
        p.dma("sp", bufK[par][:, :], k_hbm.ap()[h], inc=f"d_k{par}")
        p.dma("sp", bufV[par][:, :], v_hbm.ap()[h], inc=f"d_v{par}")
        dq_target = 16 * (h // 2 + 1)
        rot_w = [(f"d_q{par}", dq_target), (f"d_k{par}", dq_target), (f"d_v{par}", dq_target),
                 ("dve", last_m2), ("d_c", dc_all),
                 ("act", evac_val[47] if h == 0 else 0)]
        p.emit("pe", lambda e, par=par: e.matmul(ps_r[:, 0:512], PTs[:, :], bufQ[par][:, 0:512],
                                                 start=True, stop=True), waits=rot_w)
        pq_done = p.emit("pe", lambda e, par=par: e.matmul(ps_r[:, 512:1024], PTs[:, :],
                                                           bufQ[par][:, 512:1024],
                                                           start=True, stop=True), inc="pe")
        p.emit("dve", lambda e, par=par: e.tensor_mul(scrA[:, :], bufQ[par][:, :], cFs[:, :]),
               waits=[("pe", pq_done)])
        last_m2 = p.emit("dve", lambda e: e.tensor_mul(scrB[:, :], ps_r[:, :], sFs[:, :]), inc="dve")
        p.emit("dve", lambda e: e.tensor_add(q_rot[:, :], scrA[:, :], scrB[:, :]))
        p.emit("dve", lambda e, h=h: e.tensor_mul(
            qt[:, :].rearrange("p (c i) -> p c i", c=NCH),
            q_rot[:, :].rearrange("p (c i) -> p c i", c=NCH),
            qdTs[:, h * 128:(h + 1) * 128].unsqueeze(1).broadcast_to([128, NCH, 128])))
        qt_done = p.v("dve")
        qt2_done = None
        for c in range(NCH):
            qt2_done = p.emit("act", lambda e, h=h, c=c: e.activation(
                out=qt2[:, c * 128:(c + 1) * 128], in_=qt[:, c * 128:(c + 1) * 128],
                func=AF.Copy, scale=rcgs[:, h * NCH + c: h * NCH + c + 1]),
                waits=([("d_qt2", 16 * qt2_dma), ("dve", qt_done)] if c == 0 else ()),
                inc="act")
        dv = p.dma("act", qt2_hbm.ap()[h], qt2[:, :], waits=[("act", qt2_done)], inc="d_qt2")
        qt2_dma = dv // 16
        p.emit("pe", lambda e, par=par: e.matmul(ps_r[:, 0:512], PTs[:, :], bufK[par][:, 0:512],
                                                 start=True, stop=True), waits=[("dve", last_m2)])
        pk_done = p.emit("pe", lambda e, par=par: e.matmul(ps_r[:, 512:1024], PTs[:, :],
                                                           bufK[par][:, 512:1024],
                                                           start=True, stop=True), inc="pe")
        p.emit("dve", lambda e, par=par: e.tensor_mul(scrA[:, :], bufK[par][:, :], cFs[:, :]),
               waits=[("pe", pk_done)])
        last_m2 = p.emit("dve", lambda e: e.tensor_mul(scrB[:, :], ps_r[:, :], sFs[:, :]), inc="dve")
        krot_done = p.emit("dve", lambda e: e.tensor_add(k_rot[:, :], scrA[:, :], scrB[:, :]), inc="dve")
        p.emit("dve", lambda e: e.memset(kvf[:, :], 0.0),
               waits=[("act", prev_head_kvcast)])
        kvz = p.emit("dve", lambda e, h=h: e.memset(S7stage[:, h * 128:(h + 1) * 128], 0.0), inc="dve")
        kvcast_prev = kvz
        for c in range(NCH):
            ce = c % 2
            cs = slice(c * 128, (c + 1) * 128)
            p.emit("pe", lambda e, cs=cs: e.transpose(tp_T1[:, :], k_rot[:, cs], idents[:, :]),
                   waits=[("dve", krot_done), ("act", last_ktil)])
            t2v = p.emit("pe", lambda e, cs=cs, par=par: e.transpose(tp_T2[:, :],
                                                                     bufV[par][:, cs],
                                                                     idents[:, :]),
                         waits=[("act", last_vt)], inc="pe")
            s_done = p.emit("pe", lambda e, ce=ce, cs=cs: e.matmul(tp_S[ce][:, :], k_rot[:, cs],
                                                                   q_rot[:, cs], start=True, stop=True),
                            waits=[("dve", last_stil[ce])], inc="pe")
            last_ktil = p.emit(
                "act", lambda e, ce=ce, h=h: e.activation(out=ktil[ce][:, :], in_=tp_T1[:, :],
                                                          func=AF.Copy, scale=kds[:, h:h + 1]),
                waits=[("pe", t2v)], inc="act")
            last_vt = p.emit(
                "act", lambda e, ce=ce: e.activation(out=vtok[ce][:, :], in_=tp_T2[:, :],
                                                     func=AF.Copy),
                waits=[("pe", t2v)], inc="act")
            last_stil[ce] = p.emit(
                "dve", lambda e, ce=ce, h=h: e.tensor_mul(stil[ce][:, :], tp_S[ce][:, :],
                                                          dmTs[:, h * 128:(h + 1) * 128]),
                waits=[("pe", s_done)], inc="dve")
            p.emit("pe", lambda e, ce=ce: e.matmul(tp_O[:, :], vtok[ce][:, :], stil[ce][:, :],
                                                   start=True, stop=False),
                   waits=[("act", max(last_vt, last_evacO)), ("dve", last_stil[ce])])
            o_done = p.emit("pe", lambda e, h=h, cs=cs: e.matmul(tp_O[:, :],
                                                                 S7stage[:, h * 128:(h + 1) * 128],
                                                                 qt[:, cs], start=False, stop=True),
                            waits=[("act", kvcast_prev if c > 0 else 0), ("dve", kvz)], inc="pe")
            a_done = p.emit("pe", lambda e, ce=ce: e.matmul(tp_A[:, :], ktil[ce][:, :], vtok[ce][:, :],
                                                            start=True, stop=True),
                            waits=[("act", last_ktil), ("dve", last_kvupd)], inc="pe")
            last_evacO = p.emit(
                "act", lambda e, h=h, c=c: e.activation(
                    out=actA[:, h * T + c * 128: h * T + (c + 1) * 128], in_=tp_O[:, :], func=AF.Copy),
                waits=[("pe", o_done)], inc="act")
            attn_evac_vals[h, c] = last_evacO
            last_kvupd = p.emit(
                "dve", lambda e, h=h: e.scalar_tensor_tensor(kvf[:, :], kvf[:, :],
                                                             float(r_heads[h] ** 128), tp_A[:, :],
                                                             ALU.mult, ALU.add),
                waits=[("pe", a_done)], inc="dve")
            kvcast_prev = p.emit(
                "act", lambda e, h=h: e.activation(out=S7stage[:, h * 128:(h + 1) * 128],
                                                   in_=kvf[:, :], func=AF.Copy),
                waits=[("dve", last_kvupd)], inc="act")
        prev_head_kvcast = kvcast_prev
        head_end[h] = {"pe": p.v("pe"), "dve": p.v("dve"), "act": p.v("act")}
    all_kv_act = p.v("act")
    p3_pe_end = p.v("pe")
    p3_dve_end = p.v("dve")
    ctxp3.close()
    if "attnraw" in tap_tensors:
        p.dma("act", tap_tensors["attnraw"].ap(), actA[:, :],
              waits=[("act", all_kv_act), ("dve", p3_dve_end)], inc="d_tap")
    if upto == "p3":
        return _finish_early()

    ctxc = ExitStack()
    ps_cr = [ctxc.enter_context(nc.psum_tensor(f"ps_cr{i}", [128, 128], F32)) for i in range(2)]
    last_corr = {0: 0, 1: 0}
    corr_pe = {}
    for h in range(NH):
        par = h % 2
        p.dma("sp", bufQ[par][:, :], qt2_hbm.ap()[h],
              waits=[("d_qt2", 16 * qt2_dma), ("pe", corr_pe.get(h - 2, p3_pe_end)),
                     ("dve", p3_dve_end if h < 2 else 0)],
              inc=f"d_q{par}")
        dq_now = p.v(f"d_q{par}")
        for cl in range(NCH):
            cp = cl % 2
            mmv = p.emit("pe", lambda e, cp=cp, h=h, cl=cl, par=par: e.matmul(
                ps_cr[cp][:, :], S7g[:, h * 128:(h + 1) * 128],
                bufQ[par][:, cl * 128:(cl + 1) * 128], start=True, stop=True),
                waits=[("act", s7_done), (f"d_q{par}", dq_now), ("dve", last_corr[cp]),
                       ("act", all_kv_act if (h == 0 and cl == 0) else 0)], inc="pe")
            last_corr[cp] = p.emit(
                "dve", lambda e, cp=cp, h=h, cl=cl: e.tensor_add(
                    actA[:, h * T + cl * 128: h * T + (cl + 1) * 128],
                    actA[:, h * T + cl * 128: h * T + (cl + 1) * 128], ps_cr[cp][:, :]),
                waits=[("pe", mmv)], inc="dve")
        corr_pe[h] = p.v("pe")
    corr_done = max(last_corr[0], last_corr[1])
    ctxc.close()
    if "attn" in tap_tensors:
        p.dma("act", tap_tensors["attn"].ap(), actA[:, :], waits=[("dve", corr_done)], inc="d_tap")
    if upto == "corr":
        return _finish_early()

    ctx4 = ExitStack()
    p_ss = ctx4.enter_context(nc.psum_tensor("p_ss", [1, T], F32))
    ss_done = None
    sq_prev = 0
    for k in range(16):
        scr = scrA if k % 2 == 0 else scrB
        sq = p.emit("act", lambda e, scr=scr, k=k: e.activation(
            out=scr[:, :], in_=actA[:, k * T:(k + 1) * T], func=AF.Square),
            waits=[("dve", corr_done), ("pe", sq_prev)], inc="act")
        for half in range(2):
            ss_done = p.emit("pe", lambda e, k=k, half=half: e.matmul(
                p_ss[:, half * 512:(half + 1) * 512], ones_col[:, :],
                (scrA if k % 2 == 0 else scrB)[:, half * 512:(half + 1) * 512],
                start=(k == 0), stop=(k == 15)),
                waits=[("act", sq), ("dve", corr_done if k == 0 else 0),
                       ("dve", ones_ready if k == 0 else 0)],
                inc=("pe" if half == 1 else None))
        sq_prev = ss_done
    p.emit("dve", lambda e: e.tensor_scalar_mul(st_a[:, :], p_ss[:, :], 1.0 / D),
           waits=[("pe", ss_done)], inc="dve")
    p.emit("act", lambda e: e.activation(out=st_b[:, :], in_=st_a[:, :], func=AF.Ln, bias=eps_sb[:, :]),
           waits=[("dve", p.v("dve"))])
    rms_r = p.emit("act", lambda e: e.activation(out=st_a[:, :], in_=st_b[:, :], func=AF.Exp,
                                                 scale=-0.5), inc="act")
    ctx4.close()
    ctx4b = ExitStack()
    p_br2 = ctx4b.enter_context(nc.psum_tensor("p_br2", [128, T], F32))
    rbc = None
    for half in range(2):
        rbc = p.emit("pe", lambda e, half=half: e.matmul(
            p_br2[:, half * 512:(half + 1) * 512], ones_row[:, :],
            st_a[:, half * 512:(half + 1) * 512], start=True, stop=True),
            waits=[("act", rms_r)], inc=("pe" if half == 1 else None))
    norm_done_act = None
    for k in range(16):
        mval = p.emit("dve", lambda e, k=k: e.tensor_mul(scrB[:, :] if k % 2 else scrA[:, :],
                                                         actA[:, k * T:(k + 1) * T], p_br2[:, :]),
                      waits=[("pe", rbc), ("act", norm_done_act if k >= 2 else 0)], inc="dve")
        norm_done_act = p.emit("act", lambda e, k=k: e.activation(
            out=actA[:, k * T:(k + 1) * T], in_=(scrB[:, :] if k % 2 else scrA[:, :]),
            func=AF.Copy, scale=rmsc[:, k:k + 1]),
            waits=[("dve", mval)], inc="act")
    norm_done = p.v("dve")
    ctx4b.close()
    ctx4c = ExitStack()
    psumW = [ctx4c.enter_context(nc.psum_tensor(f"psumW{i}", [128, T], F32)) for i in range(2)]
    res_val = {}
    xp_dma = {}
    cast_val2 = {}
    wo_pe_val = {}
    for m in range(16):
        par = m % 2
        p.dma("sp", wland[par][:, :], wo_h.ap()[m],
              waits=[("dve", cast_val2.get(m - 2, last_cast_par[par]))], inc=f"d_w{par}")
        wo_base = p.v(f"d_w{par}")
        cast_val2[m] = p.emit("dve", lambda e, par=par: e.tensor_copy(wbf[par][:, :], wland[par][:, :]),
                              waits=[(f"d_w{par}", wo_base), ("dve", norm_done),
                                     ("pe", wo_pe_val.get(m - 2, pe_proj[47]))], inc="dve")
        wo_pe = None
        for k in range(16):
            for half in range(2):
                wo_pe = p.emit("pe", lambda e, par=par, k=k, half=half: e.matmul(
                    psumW[par][:, half * 512:(half + 1) * 512],
                    wbf[par][:, k * 128:(k + 1) * 128],
                    actA[:, k * T + half * 512: k * T + (half + 1) * 512],
                    start=(k == 0), stop=(k == 15)),
                    waits=([("dve", max(cast_val2[m], res_val.get(m - 2, 0), norm_done)),
                            ("act", norm_done_act)]
                           if (k == 0 and half == 0) else ()),
                    inc=("pe" if (k == 15 and half == 1) else None))
        wo_pe_val[m] = wo_pe
        res_val[m] = p.emit("dve", lambda e, m=m, par=par: e.tensor_add(
            x_sb[:, m * T:(m + 1) * T], x_sb[:, m * T:(m + 1) * T], psumW[par][:, :]),
            waits=[("pe", wo_pe), ("d_xp", 16 * xp_dma.get(m - 2, 0))], inc="dve")
        dv = p.dma("act", xp_hbm.ap()[m], x_sb[:, m * T:(m + 1) * T],
                   waits=[("dve", res_val[m])], inc="d_xp")
        xp_dma[m] = dv // 16
    ctx4c.close()
    if "xprime" in tap_tensors:
        p.dma("act", tap_tensors["xprime"].ap(), x_sb[:, :], waits=[("dve", res_val[15])], inc="d_tap")
    if upto == "p4":
        return _finish_early()

    ln2_done = emit_layernorm(x_sb, actA, first_waits=[("dve", res_val[15])], tag="ln2")
    if "xln2" in tap_tensors:
        p.dma("act", tap_tensors["xln2"].ap(), actA[:, :], waits=[("dve", ln2_done)], inc="d_tap")
    if upto == "ln2":
        return _finish_early()

    stack_attn.close()
    stack_ffn = ExitStack()
    sf = stack_ffn.enter_context
    NSLOT = 66
    hid = sf(nc.sbuf_tensor("hid", [128, NSLOT * T], BF16))
    land1 = [sf(nc.sbuf_tensor(f"land1_{i}", [128, 1024], F32)) for i in range(2)]
    wb1 = [sf(nc.sbuf_tensor(f"wb1_{i}", [128, 2048], BF16)) for i in range(2)]
    stack_p6 = ExitStack()
    land3 = [stack_p6.enter_context(nc.sbuf_tensor(f"land3_{i}", [128, 1024], F32)) for i in range(2)]
    wb3 = [stack_p6.enter_context(nc.sbuf_tensor(f"wb3_{i}", [128, 2048], BF16)) for i in range(2)]
    gsil = [stack_p6.enter_context(nc.sbuf_tensor(f"gsil{i}", [128, T], BF16)) for i in range(2)]
    htmp = [stack_p6.enter_context(nc.sbuf_tensor(f"htmp{i}", [128, T], BF16)) for i in range(2)]

    ctx6 = ExitStack()
    pg = [ctx6.enter_context(nc.psum_tensor(f"pg{i}", [128, T], F32)) for i in range(2)]
    pl = [ctx6.enter_context(nc.psum_tensor(f"pl{i}", [128, T], F32)) for i in range(2)]
    morder = [bitrev6(i) for i in range(64)]
    sched = fwht_schedule(morder)
    slot_of = list(range(64))
    free_slots = [64, 65]
    chunk_src = {}
    evacA_val = {}
    silu_val = {}
    mul_val = {}
    cast6_val = {}
    sa_pe_val = {}
    last_bf = 0
    gate_ffn = [("pe", rbc), ("d_s0", 16 * store_val[46]), ("d_s1", 16 * store_val[47]),
                ("d_q0", p.v("d_q0")), ("d_q1", p.v("d_q1")), ("d_qt2", 16 * qt2_dma)]
    w13_pe_val = {}
    cast1_val = {}
    cast3_val = {}

    def emit_w13_dma(mi):
        if mi >= 64:
            return
        mm_ = morder[mi]
        for half in range(2):
            p.dma("sp", land1[half][:, :], w1_h.ap()[mm_][:, half * 1024:(half + 1) * 1024],
                  waits=[("act", cast1_val.get((mi - 1, half), 0))] + (gate_ffn if mi == 0 else []),
                  inc=f"d_f1{half}")
            p.dma("sp", land3[half][:, :], w3_h.ap()[mm_][:, half * 1024:(half + 1) * 1024],
                  waits=[("act", cast3_val.get((mi - 1, half), 0))], inc=f"d_f3{half}")

    def emit_w13_cast(mi):
        if mi >= 64:
            return
        par = mi % 2
        for half in range(2):
            cast1_val[(mi, half)] = p.emit("act", lambda e, par=par, half=half: e.activation(
                out=wb1[par][:, half * 1024:(half + 1) * 1024], in_=land1[half][:, :], func=AF.Copy),
                waits=[(f"d_f1{half}", 16 * (mi + 1)), ("pe", w13_pe_val.get(mi - 2, 0))], inc="act")
            cast3_val[(mi, half)] = p.emit("act", lambda e, par=par, half=half: e.activation(
                out=wb3[par][:, half * 1024:(half + 1) * 1024], in_=land3[half][:, :], func=AF.Copy),
                waits=[(f"d_f3{half}", 16 * (mi + 1))], inc="act")

    emit_w13_dma(0)
    emit_w13_cast(0)
    emit_w13_dma(1)
    for mi in range(64):
        m = morder[mi]
        par = mi % 2
        w1_pe = None
        for k in range(16):
            for half in range(2):
                w1_pe = p.emit("pe", lambda e, par=par, k=k, half=half: e.matmul(
                    pg[par][:, half * 512:(half + 1) * 512],
                    wb1[par][:, k * 128:(k + 1) * 128],
                    actA[:, k * T + half * 512: k * T + (half + 1) * 512],
                    start=(k == 0), stop=(k == 15)),
                    waits=([("act", max(cast3_val[(mi, 1)], evacA_val.get(mi - 2, 0))),
                            ("dve", ln2_done if mi == 0 else 0)]
                           if (k == 0 and half == 0) else ()),
                    inc=("pe" if (k == 15 and half == 1) else None))
        w13_pe = None
        for k in range(16):
            for half in range(2):
                w13_pe = p.emit("pe", lambda e, par=par, k=k, half=half: e.matmul(
                    pl[par][:, half * 512:(half + 1) * 512],
                    wb3[par][:, k * 128:(k + 1) * 128],
                    actA[:, k * T + half * 512: k * T + (half + 1) * 512],
                    start=(k == 0), stop=(k == 15)),
                    waits=([("dve", mul_val.get(mi - 2, 0))] if (k == 0 and half == 0) else ()),
                    inc=("pe" if (k == 15 and half == 1) else None))
        w13_pe_val[mi] = w13_pe
        emit_w13_cast(mi + 1)
        emit_w13_dma(mi + 2)
        silu_val[mi] = p.emit("act", lambda e, par=par: e.activation(out=gsil[par][:, :],
                                                                     in_=pg[par][:, :], func=AF.Silu),
                              waits=[("pe", w1_pe)], inc="act")
        mul_val[mi] = p.emit("dve", lambda e, par=par: e.tensor_mul(htmp[par][:, :], gsil[par][:, :],
                                                                    pl[par][:, :]),
                             waits=[("act", silu_val[mi]), ("pe", w13_pe),
                                    ("pe", sa_pe_val.get(mi - 2, 0))], inc="dve")
        sa_pe = None
        for half in range(2):
            sa_pe = p.emit("pe", lambda e, par=par, half=half: e.matmul(
                pg[par][:, half * 512:(half + 1) * 512], H128s[:, :],
                htmp[par][:, half * 512:(half + 1) * 512], start=True, stop=True),
                waits=([("dve", mul_val[mi]), ("act", silu_val[mi])] if half == 0 else ()),
                inc=("pe" if half == 1 else None))
        sa_pe_val[mi] = sa_pe
        slot = slot_of[m]
        evacA_val[mi] = p.emit("act", lambda e, par=par, slot=slot: e.activation(
            out=hid[:, slot * T:(slot + 1) * T], in_=pg[par][:, :], func=AF.Copy),
            waits=[("pe", sa_pe), ("d_xp", 16 * xp_dma[15] if mi == 0 else 0)], inc="act")
        chunk_src[m] = ("act", evacA_val[mi])
        for (a, b) in sched[mi]:
            sa_slot, sb_slot = slot_of[a], slot_of[b]
            fa = free_slots.pop(0)
            fb = free_slots.pop(0)
            waits = [("act", val) for (kind, val) in (chunk_src[a], chunk_src[b]) if kind == "act"]
            p.emit("dve", lambda e, fa=fa, sa_slot=sa_slot, sb_slot=sb_slot: e.tensor_add(
                hid[:, fa * T:(fa + 1) * T], hid[:, sa_slot * T:(sa_slot + 1) * T],
                hid[:, sb_slot * T:(sb_slot + 1) * T]), waits=waits)
            last_bf = p.emit("dve", lambda e, fb=fb, sa_slot=sa_slot, sb_slot=sb_slot: e.tensor_sub(
                hid[:, fb * T:(fb + 1) * T], hid[:, sa_slot * T:(sa_slot + 1) * T],
                hid[:, sb_slot * T:(sb_slot + 1) * T]), inc="dve")
            free_slots.append(sa_slot)
            free_slots.append(sb_slot)
            slot_of[a], slot_of[b] = fa, fb
            chunk_src[a] = ("dve", last_bf)
            chunk_src[b] = ("dve", last_bf)
    ctx6.close()
    stack_p6.close()
    xpb = [stack_ffn.enter_context(nc.sbuf_tensor(f"xpb{i}", [128, T], F32)) for i in range(4)]
    if "hid" in tap_tensors:
        for c in range(64):
            p.dma("act", tap_tensors["hid"].ap()[c], hid[:, slot_of[c] * T:(slot_of[c] + 1) * T],
                  waits=[("dve", last_bf)], inc="d_tap")
    if upto == "p6":
        return _finish_early()

    ctx8 = ExitStack()
    p8 = [ctx8.enter_context(nc.psum_tensor(f"p8_{i}", [128, T], F32)) for i in range(4)]
    w2cast = {}
    evac8 = {}
    out_dma_val = {}
    blk_dve = {}
    p8_pe_val = {}
    xb_pref = {}
    for mblk in range(4):
        for m4 in range(4):
            mchunk = mblk * 4 + m4
            p.dma("pool", xpb[m4][:, :], xp_hbm.ap()[mchunk],
                  waits=[("d_xp", 16 * xp_dma[15]),
                         ("pe", sa_pe_val[63] if mblk == 0 else 0),
                         ("dve", mul_val[63] if mblk == 0 else 0),
                         ("d_out", 16 * out_dma_val.get((mblk - 1, m4), 0))],
                  inc=f"d_xb{m4}")
            xb_pref[(mblk, m4)] = p.v(f"d_xb{m4}")
        for cg in range(16):
            par = cg % 2
            gi = mblk * 16 + cg
            for half in range(2):
                p.dma("sp", land1[half][:, :].rearrange("p (c m) -> p c m", c=2),
                      w2_h.ap()[mblk, cg * 4 + 2 * half: cg * 4 + 2 * half + 2].transpose([1, 0, 2]),
                      waits=[("act", w2cast.get((gi - 1, half), cast1_val[(63, half)] if gi == 0 else 0))],
                      inc=f"d_f1{half}")
            for half in range(2):
                w2cast[(gi, half)] = p.emit("act", lambda e, par=par, half=half: e.activation(
                    out=wb1[par][:, half * 1024:(half + 1) * 1024], in_=land1[half][:, :],
                    func=AF.Copy, scale=W2SCALE),
                    waits=[(f"d_f1{half}", 16 * (64 + gi + 1)),
                           ("pe", p8_pe_val.get(gi - 2, w13_pe_val[62 + par]))], inc="act")
            for cc in range(4):
                c = cg * 4 + cc
                hs = slot_of[c]
                for m4 in range(4):
                    for half in range(2):
                        p8_pe_val[gi] = p.emit(
                            "pe", lambda e, par=par, cc=cc, m4=m4, half=half, hs=hs, c=c: e.matmul(
                                p8[m4][:, half * 512:(half + 1) * 512],
                                wb1[par][:, cc * 512 + m4 * 128: cc * 512 + (m4 + 1) * 128],
                                hid[:, hs * T + half * 512: hs * T + (half + 1) * 512],
                                start=(c == 0), stop=(c == 63)),
                            waits=(([("act", w2cast[(gi, 1)]),
                                     ("dve", last_bf if gi % 16 == 0 else 0)]
                                    if (cc == 0 and m4 == 0 and half == 0) else []) +
                                   ([("dve", evac8.get((mblk - 1, m4), 0))]
                                    if (cc == 0 and half == 0 and gi % 16 == 0) else [])),
                            inc=("pe" if (cc == 3 and m4 == 3 and half == 1) else None))
        blk_pe = p.v("pe")
        for m4 in range(4):
            mchunk = mblk * 4 + m4
            par = m4 % 2
            ev = p.emit("dve", lambda e, m4=m4: e.tensor_add(
                xpb[m4][:, :], xpb[m4][:, :], p8[m4][:, :]),
                waits=[("pe", blk_pe), (f"d_xb{m4}", xb_pref[(mblk, m4)])], inc="dve")
            evac8[(mblk, m4)] = ev
            dv = p.dma("act", out_h.ap().rearrange("(k p) t -> p k t", p=128)[:, mchunk, :],
                       xpb[m4][:, :], waits=[("dve", ev)], inc="d_out")
            out_dma_val[(mblk, m4)] = dv // 16
        blk_dve[mblk] = p.v("dve")
    ctx8.close()

    final_waits = [("d_out", 16 * out_dma_val[(3, 3)])]
    if p.v("d_tap"):
        final_waits.append(("d_tap", p.v("d_tap")))
    p.emit("sp", lambda e: e.nop(), waits=final_waits)

    stack_ffn.close()
    stack_persist.close()
    p.realize()
    return nc


def host_prep(inputs):
    x = np.asarray(inputs["L0_attn_x"], dtype=np.float32)
    wq_t = tile_lhsT(np.asarray(inputs["L0_attn_Wq"], dtype=np.float32))
    wk_t = tile_lhsT(np.asarray(inputs["L0_attn_Wk"], dtype=np.float32))
    wv_t = tile_lhsT(np.asarray(inputs["L0_attn_Wv"], dtype=np.float32))
    wo_t = tile_lhsT(np.asarray(inputs["L0_attn_Wo"], dtype=np.float32))
    w1_t = tile_lhsT(np.ascontiguousarray(np.asarray(inputs["L0_ffn_W1"], dtype=np.float32).T))
    w3_t = tile_lhsT(np.ascontiguousarray(np.asarray(inputs["L0_ffn_W3"], dtype=np.float32).T))
    w2T = np.ascontiguousarray(np.asarray(inputs["L0_ffn_W2"], dtype=np.float32).T)
    w2_t = np.ascontiguousarray(w2T.reshape(64, 128, 4, 512).transpose(2, 0, 1, 3))
    rms_cols = np.ascontiguousarray(np.asarray(inputs["L0_attn_rms_w"], dtype=np.float32).reshape(16, 128).T)
    in_maps = []
    zeros_xp = np.zeros((D, T), dtype=np.float32)
    for core in range(8):
        b, half = core // 2, core % 2
        xT = np.ascontiguousarray(x[b, half * T:(half + 1) * T, :].T)
        if half == 1:
            xp2 = np.ascontiguousarray(x[b, 0:T, :].T)
        else:
            xp2 = zeros_xp
        m = {"x": xT, "xp2": xp2, "wq_t": wq_t, "wk_t": wk_t, "wv_t": wv_t, "wo_t": wo_t,
             "w1_t": w1_t, "w3_t": w3_t, "w2_t": w2_t, "rms_cols": rms_cols}
        m.update(host_constants(core))
        in_maps.append(m)
    return in_maps


def kernel(**inputs):
    in_maps = host_prep(inputs)
    nc = build_program()
    res = run_bass_kernel_spmd(nc, in_maps, core_ids=list(range(8)))
    out = np.empty((4, 2048, D), dtype=np.float32)
    for core in range(8):
        b, half = core // 2, core % 2
        out[b, half * T:(half + 1) * T, :] = res.results[core]["out"].T
    return out


# version 13
# speedup vs baseline: 1.0156x; 1.0156x over previous
import math
import numpy as np

import concourse.bass as bass
import concourse.mybir as mybir
from concourse.bass_utils import run_bass_kernel_spmd

F32 = mybir.dt.float32
BF16 = mybir.dt.bfloat16
AF = mybir.ActivationFunctionType
ALU = mybir.AluOpType

D = 2048
NH = 16
T = 1024
NCH = 8
C = 128
NHC = 64
EPS = 1e-5
W2SCALE = 1.0 / math.sqrt(8192.0)

ENGS = ("pe", "act", "dve", "pool", "sp")


def _slopes():
    start = 2.0 ** (-(2.0 ** (-(math.log2(NH) - 3))))
    s = np.array([start * start ** i for i in range(NH)], dtype=np.float64)
    return np.exp(-s)


def _popcount_parity(a):
    v = a.copy()
    out = np.zeros_like(v)
    while v.any():
        out ^= (v & 1)
        v >>= 1
    return out


def host_constants(core_id):
    import ml_dtypes
    r = _slopes()
    t0 = (core_id % 2) * T
    t0p = (1 - core_id % 2) * T
    freq = (1.0 / 1024.0) ** np.linspace(0.0, 1.0, 32)
    t = np.arange(t0, t0 + T, dtype=np.float64)
    theta = t[None, :] * freq[:, None]
    cF = np.ones((128, T), dtype=np.float64)
    sF = np.zeros((128, T), dtype=np.float64)
    cF[0:32] = np.cos(theta); cF[64:96] = np.cos(theta)
    sF[0:32] = np.sin(theta); sF[64:96] = np.sin(theta)
    tp = np.arange(t0p, t0p + T, dtype=np.float64)
    thetap = tp[None, :] * freq[:, None]
    cFp = np.ones((128, T), dtype=np.float64)
    sFp = np.zeros((128, T), dtype=np.float64)
    cFp[0:32] = np.cos(thetap); cFp[64:96] = np.cos(thetap)
    sFp[0:32] = np.sin(thetap); sFp[64:96] = np.sin(thetap)
    P = np.zeros((128, 128), dtype=np.float64)
    for d in range(64):
        P[d, d + 64] = 1.0
        P[d + 64, d] = -1.0
    PT = P.T.copy()
    idx = np.arange(C, dtype=np.float64)
    diff = idx[None, :] - idx[:, None]
    dmaskT = np.zeros((C, NH * C))
    qdecT = np.zeros((C, NH * C))
    kd = np.zeros((C, NH))
    for h in range(NH):
        dmaskT[:, h * C:(h + 1) * C] = np.where(diff >= 0, r[h] ** np.maximum(diff, 0.0), 0.0)
        qdecT[:, h * C:(h + 1) * C] = (r[h] ** (idx + 1.0))[None, :]
        kd[:, h] = r[h] ** (C - 1.0 - idx)
    kd2 = np.zeros((C, NH * NCH))
    for h in range(NH):
        for c in range(NCH):
            kd2[:, h * NCH + c] = (r[h] ** (C - 1.0 - idx)) * (r[h] ** (C * (NCH - 1 - c)))
    gate = 1.0 if (core_id % 2) == 1 else 0.0
    rcg = np.zeros((C, NH * NCH))
    for h in range(NH):
        for cl in range(NCH):
            rcg[:, h * NCH + cl] = (r[h] ** (C * cl)) * gate
    i = np.arange(128)
    H128 = np.where(_popcount_parity(i[:, None] & i[None, :]) == 0, 1.0, -1.0)
    ident = np.eye(128)
    return {
        "cF": cF.astype(np.float32), "sF": sF.astype(np.float32),
        "cFp": cFp.astype(np.float32), "sFp": sFp.astype(np.float32),
        "PT": PT.astype(ml_dtypes.bfloat16), "dmaskT": dmaskT.astype(np.float32),
        "qdecT": qdecT.astype(np.float32), "kd": kd.astype(np.float32),
        "kd2": kd2.astype(np.float32),
        "rcg": rcg.astype(np.float32),
        "H128": H128.astype(ml_dtypes.bfloat16), "identity": ident.astype(ml_dtypes.bfloat16),
    }


def tile_lhsT(W):
    K, M = W.shape
    nk, nm = K // 128, M // 128
    t = W.reshape(nk, 128, nm, 128).transpose(2, 1, 0, 3)
    return np.ascontiguousarray(t.reshape(nm, 128, nk * 128))


def bitrev6(x):
    r = 0
    for b in range(6):
        r |= ((x >> b) & 1) << (5 - b)
    return r


def fwht_schedule(order):
    bits = [32, 16, 8, 4, 2, 1]
    lev = [-1] * 64
    sched = [[] for _ in range(64)]
    for mi, m in enumerate(order):
        lev[m] = 0
        changed = True
        while changed:
            changed = False
            for l in range(6):
                b = bits[l]
                for c in range(64):
                    if (c & b) == 0 and lev[c] == l and lev[c | b] == l:
                        sched[mi].append((c, c | b))
                        lev[c] = l + 1
                        lev[c | b] = l + 1
                        changed = True
    assert all(v == 6 for v in lev)
    return sched


class Plan:

    STRICT = ("dve", "act", "pool")

    def __init__(self, nc):
        self.nc = nc
        self.ops = {e: [] for e in ENGS}
        self.sems = {}
        self.cnt = {}
        self.hw = {e: {} for e in ENGS}
        for e in self.STRICT:
            self.s(e)

    def s(self, name):
        if name not in self.sems:
            self.sems[name] = self.nc.alloc_semaphore(name)
            self.cnt[name] = 0
        return name

    def v(self, name):
        return self.cnt.get(name, 0)

    def emit(self, eng, fn, waits=(), inc=None, amt=1):
        real_waits = []
        for (n, val) in waits:
            if val is None or val <= 0:
                continue
            self.s(n)
            if self.hw[eng].get(n, 0) >= val:
                continue
            self.hw[eng][n] = val
            real_waits.append((self.sems[n], val))
        target = None
        dma_sem = None
        if eng in self.STRICT and (inc is None or inc == eng):
            self.cnt[eng] += 1
            target = self.cnt[eng]
            self.ops[eng].append((tuple(real_waits), fn, self.sems[eng], 1, True))
            return target
        if inc is not None:
            self.s(inc)
            self.cnt[inc] += amt
            target = self.cnt[inc]
            dma_sem = self.sems[inc]
        self.ops[eng].append((tuple(real_waits), fn, dma_sem, amt, False))
        return target

    def dma(self, eng, out, in_, waits=(), inc=None):
        w = list(waits)
        if inc is not None and self.v(inc) > 0:
            w.append((inc, self.v(inc)))
        if eng in self.STRICT:
            w.append((eng, self.v(eng)))
        return self.emit(eng, (lambda e, o=out, i=in_: e.dma_start(out=o, in_=i)),
                         w, inc, 16)

    def realize(self):
        nc = self.nc
        plan = self

        def replay(stream):
            def f(e):
                selfsem = plan.sems.get(stream) if stream in plan.STRICT else None
                self_idx = 0
                for (waits, fn, sem, amt, is_self) in plan.ops[stream]:
                    for (s, v) in waits:
                        e.wait_ge(s, v)
                    if is_self and self_idx > 0:
                        e.wait_ge(selfsem, self_idx)
                    inst = fn(e)
                    if sem is not None:
                        inst.then_inc(sem, amt)
                    if is_self:
                        self_idx += 1
            return f

        with nc.Block() as blk:
            blk.tensor(replay("pe"))
            blk.scalar(replay("act"))
            blk.vector(replay("dve"))
            blk.gpsimd(replay("pool"))
            blk.sync(replay("sp"))


def build_program(taps=(), use_collective=True, upto="all"):
    from contextlib import ExitStack
    nc = bass.Bass("TRN2", target_bir_lowering=False)
    p = Plan(nc)
    r_heads = _slopes()

    x_h = nc.dram_tensor("x", [D, T], F32, kind="ExternalInput")
    wq_h = nc.dram_tensor("wq_t", [16, 128, 2048], F32, kind="ExternalInput")
    wk_h = nc.dram_tensor("wk_t", [16, 128, 2048], F32, kind="ExternalInput")
    wv_h = nc.dram_tensor("wv_t", [16, 128, 2048], F32, kind="ExternalInput")
    wo_h = nc.dram_tensor("wo_t", [16, 128, 2048], F32, kind="ExternalInput")
    w1_h = nc.dram_tensor("w1_t", [64, 128, 2048], F32, kind="ExternalInput")
    w3_h = nc.dram_tensor("w3_t", [64, 128, 2048], F32, kind="ExternalInput")
    w2_h = nc.dram_tensor("w2_t", [4, 64, 128, 512], F32, kind="ExternalInput")
    rmsc_h = nc.dram_tensor("rms_cols", [128, 16], F32, kind="ExternalInput")
    cF_h = nc.dram_tensor("cF", [128, T], F32, kind="ExternalInput")
    sF_h = nc.dram_tensor("sF", [128, T], F32, kind="ExternalInput")
    xp2_h = nc.dram_tensor("xp2", [D, T], F32, kind="ExternalInput")
    cFp_h = nc.dram_tensor("cFp", [128, T], F32, kind="ExternalInput")
    sFp_h = nc.dram_tensor("sFp", [128, T], F32, kind="ExternalInput")
    PT_h = nc.dram_tensor("PT", [128, 128], BF16, kind="ExternalInput")
    dmT_h = nc.dram_tensor("dmaskT", [128, NH * 128], F32, kind="ExternalInput")
    qdT_h = nc.dram_tensor("qdecT", [128, NH * 128], F32, kind="ExternalInput")
    kd_h = nc.dram_tensor("kd", [128, NH], F32, kind="ExternalInput")
    kd2_h = nc.dram_tensor("kd2", [128, NH * NCH], F32, kind="ExternalInput")
    rcg_h = nc.dram_tensor("rcg", [128, NH * NCH], F32, kind="ExternalInput")
    H128_h = nc.dram_tensor("H128", [128, 128], BF16, kind="ExternalInput")
    id_h = nc.dram_tensor("identity", [128, 128], BF16, kind="ExternalInput")
    out_h = nc.dram_tensor("out", [D, T], F32, kind="ExternalOutput")

    def scratch(name, shape, dtype):
        kind = "ExternalOutput" if name in taps else "Internal"
        return nc.dram_tensor(name, shape, dtype, kind=kind)

    q_hbm = scratch("q_s", [NH, 128, T], BF16)
    k_hbm = scratch("k_s", [NH, 128, T], BF16)
    v_hbm = scratch("v_s", [NH, 128, T], BF16)
    qt2_hbm = scratch("qt2_s", [NH, 128, T], BF16)
    xp_hbm = scratch("xp_s", [16, 128, T], F32)

    tap_tensors = {}
    for tname in taps:
        if tname == "xln1":
            tap_tensors["xln1"] = nc.dram_tensor("tap_xln1", [128, 16 * T], BF16, kind="ExternalOutput")
        if tname == "attn":
            tap_tensors["attn"] = nc.dram_tensor("tap_attn", [128, 16 * T], BF16, kind="ExternalOutput")
        if tname == "attnraw":
            tap_tensors["attnraw"] = nc.dram_tensor("tap_attnraw", [128, 16 * T], BF16, kind="ExternalOutput")
        if tname == "xprime":
            tap_tensors["xprime"] = nc.dram_tensor("tap_xprime", [128, 16 * T], F32, kind="ExternalOutput")
        if tname == "xln2":
            tap_tensors["xln2"] = nc.dram_tensor("tap_xln2", [128, 16 * T], BF16, kind="ExternalOutput")
        if tname == "hid":
            tap_tensors["hid"] = nc.dram_tensor("tap_hid", [NHC, 128, T], BF16, kind="ExternalOutput")

    stack_persist = ExitStack()
    stack_attn = ExitStack()

    sb = stack_persist.enter_context
    actA = sb(nc.sbuf_tensor("actA", [128, 16 * T], BF16))
    eps_sb = sb(nc.sbuf_tensor("eps_sb", [1, 1], F32))
    ones_col = sb(nc.sbuf_tensor("ones_col", [128, 1], F32))
    ones_row = sb(nc.sbuf_tensor("ones_row", [1, 128], F32))
    H128s = sb(nc.sbuf_tensor("H128s", [128, 128], BF16))

    sa = stack_attn.enter_context
    idents = sa(nc.sbuf_tensor("idents", [128, 128], BF16))
    rmsc = sa(nc.sbuf_tensor("rmsc", [128, 16], F32))
    st_a = sa(nc.sbuf_tensor("st_a", [1, T], F32))
    st_b = sa(nc.sbuf_tensor("st_b", [1, T], F32))
    st_m = sa(nc.sbuf_tensor("st_m", [1, T], F32))
    x_sb = sa(nc.sbuf_tensor("x_sb", [128, 16 * T], F32))
    scrA = sa(nc.sbuf_tensor("scrA", [128, T], F32))
    scrB = sa(nc.sbuf_tensor("scrB", [128, T], F32))
    cFs = sa(nc.sbuf_tensor("cFs", [128, T], F32))
    sFs = sa(nc.sbuf_tensor("sFs", [128, T], F32))
    PTs = sa(nc.sbuf_tensor("PTs", [128, 128], BF16))
    kds = sa(nc.sbuf_tensor("kds", [128, NH], F32))
    rcgs = sa(nc.sbuf_tensor("rcgs", [128, NH * NCH], F32))
    kvf = sa(nc.sbuf_tensor("kvf", [128, 128], F32))
    S7g = sa(nc.sbuf_tensor("S7g", [128, NH * 128], BF16))
    stack_pre = ExitStack()
    sp_ = stack_pre.enter_context
    xlnp = sp_(nc.sbuf_tensor("xlnp", [128, 16 * T], BF16))
    preland = [sp_(nc.sbuf_tensor(f"preland{i}", [128, T], F32)) for i in range(2)]
    wbfp = [sp_(nc.sbuf_tensor(f"wbfp{i}", [128, 2048], BF16)) for i in range(2)]
    kpbuf = sp_(nc.sbuf_tensor("kpbuf", [128, T], BF16))
    kp_rot = sp_(nc.sbuf_tensor("kp_rot", [128, T], BF16))
    vpbuf = sp_(nc.sbuf_tensor("vpbuf", [128, T], BF16))
    cFp = sp_(nc.sbuf_tensor("cFp_s", [128, T], F32))
    sFp = sp_(nc.sbuf_tensor("sFp_s", [128, T], F32))
    kd2s = sp_(nc.sbuf_tensor("kd2s", [128, NH * NCH], F32))
    ktp = [sp_(nc.sbuf_tensor(f"ktp{i}", [128, 128], BF16)) for i in range(2)]
    vtp = [sp_(nc.sbuf_tensor(f"vtp{i}", [128, 128], BF16)) for i in range(2)]

    x_ap3 = x_h.ap().rearrange("(k p) t -> p k t", p=128)
    xs_ap3 = x_sb[:, :].rearrange("p (k t) -> p k t", t=T)
    for g in range(4):
        p.dma("sp", xs_ap3[:, g * 4:(g + 1) * 4, :], x_ap3[:, g * 4:(g + 1) * 4, :], inc="d_x")
    for dst, src in ((cFs, cF_h), (sFs, sF_h), (PTs, PT_h),
                     (kds, kd_h), (rcgs, rcg_h), (H128s, H128_h), (idents, id_h), (rmsc, rmsc_h),
                     (cFp, cFp_h), (sFp, sFp_h), (kd2s, kd2_h)):
        p.dma("sp", dst[:, :], src.ap(), inc="d_c")
    dc_all = p.v("d_c")
    p.emit("dve", lambda e: e.memset(eps_sb[:, :], EPS), inc="dve")
    p.emit("dve", lambda e: e.memset(ones_col[:, :], 1.0), inc="dve")
    ones_ready = p.emit("dve", lambda e: e.memset(ones_row[:, :], 1.0), inc="dve")


    xp2_ap3 = xp2_h.ap().rearrange("(k p) t -> p k t", p=128)
    ctxpa = ExitStack()
    pp_sx = ctxpa.enter_context(nc.psum_tensor("pp_sx", [1, T], F32))
    pp_sx2 = ctxpa.enter_context(nc.psum_tensor("pp_sx2", [1, T], F32))
    sq_d = {}
    sx_d = {}
    sx2_d = {}
    for k in range(16):
        par = k % 2
        p.dma("sp", preland[par][:, :], xp2_ap3[:, k, :],
              waits=[("act", sq_d.get(k - 2, 0)), ("pe", sx_d.get(k - 2, 0))],
              inc=f"d_p{par}")
        dw = [(f"d_p{par}", 16 * (k // 2 + 1))]
        scr = scrA if par == 0 else scrB
        sq_d[k] = p.emit("act", lambda e, par=par, scr=scr: e.activation(
            out=scr[:, :], in_=preland[par][:, :], func=AF.Square),
            waits=dw + [("pe", sx2_d.get(k - 2, 0))], inc="act")
        for half in range(2):
            sx_d[k] = p.emit("pe", lambda e, par=par, k=k, half=half: e.matmul(
                pp_sx[:, half * 512:(half + 1) * 512], ones_col[:, :],
                preland[par][:, half * 512:(half + 1) * 512], start=(k == 0), stop=(k == 15)),
                waits=dw + [("dve", ones_ready)], inc=("pe" if half == 1 else None))
        for half in range(2):
            sx2_d[k] = p.emit("pe", lambda e, par=par, k=k, half=half: e.matmul(
                pp_sx2[:, half * 512:(half + 1) * 512], ones_col[:, :],
                (scrA if par == 0 else scrB)[:, half * 512:(half + 1) * 512],
                start=(k == 0), stop=(k == 15)),
                waits=[("act", sq_d[k])], inc=("pe" if half == 1 else None))
    p.emit("dve", lambda e: e.tensor_scalar_mul(st_m[:, :], pp_sx[:, :], 1.0 / D),
           waits=[("pe", sx2_d[15])])
    p.emit("dve", lambda e: e.tensor_scalar_mul(st_a[:, :], pp_sx2[:, :], 1.0 / D))
    p.emit("dve", lambda e: e.tensor_mul(st_b[:, :], st_m[:, :], st_m[:, :]))
    vp_done = p.emit("dve", lambda e: e.tensor_sub(st_a[:, :], st_a[:, :], st_b[:, :]), inc="dve")
    p.emit("act", lambda e: e.activation(out=st_b[:, :], in_=st_a[:, :], func=AF.Ln, bias=eps_sb[:, :]),
           waits=[("dve", vp_done)])
    rp_done = p.emit("act", lambda e: e.activation(out=st_a[:, :], in_=st_b[:, :], func=AF.Exp,
                                                   scale=-0.5), inc="act")
    ctxpa.close()
    ctxpb = ExitStack()
    pp_bm = ctxpb.enter_context(nc.psum_tensor("pp_bm", [128, T], F32))
    pp_br = ctxpb.enter_context(nc.psum_tensor("pp_br", [128, T], F32))
    bcp = None
    for half in range(2):
        p.emit("pe", lambda e, half=half: e.matmul(
            pp_bm[:, half * 512:(half + 1) * 512], ones_row[:, :],
            st_m[:, half * 512:(half + 1) * 512], start=True, stop=True),
            waits=[("act", rp_done)])
        bcp = p.emit("pe", lambda e, half=half: e.matmul(
            pp_br[:, half * 512:(half + 1) * 512], ones_row[:, :],
            st_a[:, half * 512:(half + 1) * 512], start=True, stop=True),
            inc=("pe" if half == 1 else None))
    nrm_d = {}
    for k in range(16):
        par = k % 2
        p.dma("sp", preland[par][:, :], xp2_ap3[:, k, :],
              waits=[("dve", nrm_d.get(k - 2, 0)), ("act", sq_d[15] if k < 2 else 0),
                     ("pe", sx_d[15] if k < 2 else 0)],
              inc=f"d_p{par}")
        dwv = [(f"d_p{par}", 16 * (8 + k // 2 + 1)), ("pe", bcp)]
        p.emit("dve", lambda e, par=par: e.tensor_sub(scrA[:, :], preland[par][:, :], pp_bm[:, :]),
               waits=dwv)
        nrm_d[k] = p.emit("dve", lambda e, k=k: e.tensor_mul(
            xlnp[:, k * T:(k + 1) * T], scrA[:, :], pp_br[:, :]), inc="dve")
    nrm_done = nrm_d[15]
    ctxpb.close()
    ctxpc = ExitStack()
    ppj = ctxpc.enter_context(nc.psum_tensor("ppj", [128, T], F32))
    pp_rot = ctxpc.enter_context(nc.psum_tensor("pp_rot", [128, T], F32))
    ppT1a = ctxpc.enter_context(nc.psum_tensor("ppT1a", [128, 128], BF16))
    ppT2a = ctxpc.enter_context(nc.psum_tensor("ppT2a", [128, 128], BF16))
    ppS7 = ctxpc.enter_context(nc.psum_tensor("ppS7", [128, 128], F32))
    ppT1b = pp_rot[:, 0:64].bitcast(BF16)
    ppT2b = pp_rot[:, 512:576].bitcast(BF16)
    wl_rd = {}
    last_ev = 0
    prev_cast = 0
    prev_m2p = 0
    prev_ktp = {0: 0, 1: 0}
    prev_vtp = {0: 0, 1: 0}
    s7_cast = 0
    dp_cnt = 32
    cast_of = {}

    def feed_tile(wt_h, h, wpar, wsel):
        nonlocal prev_cast, dp_cnt
        for half in range(2):
            p.dma("sp", preland[half][:, :], wt_h.ap()[h][:, half * 1024:(half + 1) * 1024],
                  waits=[("dve", prev_cast), ("dve", nrm_done)], inc=f"d_p{half}")
        gen = dp_cnt // 2 + 1
        dp_cnt += 2
        return gen

    def cast_tile(gen, wpar):
        nonlocal prev_cast
        for half in range(2):
            prev_cast = p.emit("dve", lambda e, half=half, wpar=wpar: e.tensor_copy(
                wbfp[wpar][:, half * 1024:(half + 1) * 1024], preland[half][:, :]),
                waits=[(f"d_p{half}", 16 * gen), ("pe", wl_rd.get(wpar, 0))], inc="dve")
        return prev_cast

    def mm_emitter(wpar, cval):
        for j0 in range(0, 32, 4):
            def grp(j0=j0, wpar=wpar, cval=cval):
                mmv = None
                for j in range(j0, j0 + 4):
                    k, half = j // 2, j % 2
                    mmv = p.emit("pe", lambda e, k=k, half=half, wpar=wpar: e.matmul(
                        ppj[:, half * 512:(half + 1) * 512],
                        wbfp[wpar][:, k * 128:(k + 1) * 128],
                        xlnp[:, k * T + half * 512: k * T + (half + 1) * 512],
                        start=(k == 0), stop=(k == 15)),
                        waits=([("dve", max(cval, nrm_done)), ("act", last_ev)]
                               if j == j0 == 0 else ()),
                        inc=("pe" if j == 31 else None))
                return mmv
            yield grp

    g0 = feed_tile(wk_h, 0, 0, 0)
    cast_of[(0, 0)] = cast_tile(g0, 0)
    mm_end = None
    for grp in mm_emitter(0, cast_of[(0, 0)]):
        mm_end = grp()
    wl_rd[0] = mm_end
    for h in range(NH):
        ev_kp = p.emit("act", lambda e: e.activation(out=kpbuf[:, :], in_=ppj[:, :], func=AF.Copy),
                       waits=[("pe", wl_rd[0])], inc="act")
        last_ev = ev_kp
        gv = feed_tile(wv_h, h, 1, 1)
        cast_of[(h, 1)] = cast_tile(gv, 1)
        mm_end = None
        for grp in mm_emitter(1, cast_of[(h, 1)]):
            mm_end = grp()
        wl_rd[1] = mm_end
        ev_vp = p.emit("act", lambda e: e.activation(out=vpbuf[:, :], in_=ppj[:, :], func=AF.Copy),
                       waits=[("pe", wl_rd[1])], inc="act")
        last_ev = ev_vp
        p.emit("pe", lambda e: e.matmul(pp_rot[:, 0:512], PTs[:, :], kpbuf[:, 0:512],
                                        start=True, stop=True),
               waits=[("act", ev_vp), ("dve", prev_m2p), ("d_c", dc_all)])
        prot = p.emit("pe", lambda e: e.matmul(pp_rot[:, 512:1024], PTs[:, :], kpbuf[:, 512:1024],
                                               start=True, stop=True), inc="pe")
        p.emit("dve", lambda e: e.tensor_mul(scrA[:, :], kpbuf[:, :], cFp[:, :]),
               waits=[("pe", prot)])
        prev_m2p = p.emit("dve", lambda e: e.tensor_mul(scrB[:, :], pp_rot[:, :], sFp[:, :]), inc="dve")
        krotp = p.emit("dve", lambda e: e.tensor_add(kp_rot[:, :], scrA[:, :], scrB[:, :]), inc="dve")
        grps = None
        if h + 1 < NH:
            gk = feed_tile(wk_h, h + 1, 0, 0)
            cast_of[(h + 1, 0)] = cast_tile(gk, 0)
            grps = list(mm_emitter(0, cast_of[(h + 1, 0)]))
        av = None
        mm_end = wl_rd.get(0, 0)
        for c in range(NCH):
            ce = c % 2
            cs = slice(c * 128, (c + 1) * 128)
            pT1 = ppT1a if ce == 0 else ppT1b
            pT2 = ppT2a if ce == 0 else ppT2b
            p.emit("pe", lambda e, cs=cs, pT1=pT1: e.transpose(pT1[:, :], kp_rot[:, cs], idents[:, :]),
                   waits=[("dve", max(krotp, prev_m2p)), ("act", prev_ktp[ce])])
            t2v = p.emit("pe", lambda e, cs=cs, pT2=pT2: e.transpose(pT2[:, :], vpbuf[:, cs],
                                                                     idents[:, :]),
                         waits=[("act", prev_vtp[ce])], inc="pe")
            if grps is not None:
                mm_end = grps[c]()
            prev_ktp[ce] = p.emit("act", lambda e, h=h, c=c, ce=ce, pT1=pT1: e.activation(
                out=ktp[ce][:, :], in_=pT1[:, :], func=AF.Copy,
                scale=kd2s[:, h * NCH + c: h * NCH + c + 1]),
                waits=[("pe", t2v)], inc="act")
            prev_vtp[ce] = p.emit("act", lambda e, ce=ce, pT2=pT2: e.activation(
                out=vtp[ce][:, :], in_=pT2[:, :], func=AF.Copy),
                waits=[("pe", t2v)], inc="act")
            av = p.emit("pe", lambda e, c=c, ce=ce: e.matmul(ppS7[:, :], ktp[ce][:, :], vtp[ce][:, :],
                                                             start=(c == 0), stop=(c == NCH - 1)),
                        waits=[("act", max(prev_ktp[ce], prev_vtp[ce]))], inc="pe")
        if grps is not None:
            wl_rd[0] = mm_end
        s7_cast = p.emit("act", lambda e, h=h: e.activation(
            out=S7g[:, h * 128:(h + 1) * 128], in_=ppS7[:, :], func=AF.Copy),
            waits=[("pe", av)], inc="act")
    s7_done = s7_cast
    p05_pe_end = p.v("pe")
    p05_dve_end = p.v("dve")
    ctxpc.close()
    stack_pre.close()
    wland = [sa(nc.sbuf_tensor(f"wland{i}", [128, 2048], F32)) for i in range(2)]
    wbf = [sa(nc.sbuf_tensor(f"wbf{i}", [128, 2048], BF16)) for i in range(2)]
    stg = [sa(nc.sbuf_tensor(f"stg{i}", [128, T], BF16)) for i in range(2)]
    bufQ = [sa(nc.sbuf_tensor(f"bufQ{i}", [128, T], BF16)) for i in range(2)]
    bufK = [sa(nc.sbuf_tensor(f"bufK{i}", [128, T], BF16)) for i in range(2)]
    bufV = [sa(nc.sbuf_tensor(f"bufV{i}", [128, T], BF16)) for i in range(2)]
    dmTs = sa(nc.sbuf_tensor("dmTs", [128, NH * 128], F32))
    qdTs = sa(nc.sbuf_tensor("qdTs", [128, NH * 128], F32))
    q_rot = sa(nc.sbuf_tensor("q_rot", [128, T], BF16))
    k_rot = sa(nc.sbuf_tensor("k_rot", [128, T], BF16))
    qt = sa(nc.sbuf_tensor("qt", [128, T], BF16))
    qt2 = sa(nc.sbuf_tensor("qt2", [128, T], BF16))
    ktil = [sa(nc.sbuf_tensor(f"ktil{i}", [128, 128], BF16)) for i in range(2)]
    vtok = [sa(nc.sbuf_tensor(f"vtok{i}", [128, 128], BF16)) for i in range(2)]
    stil = [sa(nc.sbuf_tensor(f"stil{i}", [128, 128], BF16)) for i in range(2)]
    S7stage = sa(nc.sbuf_tensor("S7stage", [128, NH * 128], BF16))
    for dst, srch in ((dmTs, dmT_h), (qdTs, qdT_h)):
        p.dma("sp", dst[:, :], srch.ap(), inc="d_c",
              waits=[("pe", p05_pe_end), ("dve", p05_dve_end), ("act", s7_done)])
    dc_all = p.v("d_c")

    def emit_layernorm(src_sb, dst_sb, first_waits=(), x_dma_gate=False, tag="ln1"):
        ctx = ExitStack()
        p_sx = ctx.enter_context(nc.psum_tensor(f"p_sx_{tag}", [1, T], F32))
        p_sx2 = ctx.enter_context(nc.psum_tensor(f"p_sx2_{tag}", [1, T], F32))
        sq_done = {}
        sx2_done = {}
        for k in range(16):
            dmawait = [("d_x", 16 * (k // 4 + 1))] if x_dma_gate else []
            scr = scrA if k % 2 == 0 else scrB
            sq_done[k] = p.emit(
                "act", lambda e, scr=scr, k=k: e.activation(out=scr[:, :], in_=src_sb[:, k * T:(k + 1) * T],
                                                            func=AF.Square),
                waits=dmawait + [("pe", sx2_done.get(k - 2, 0))] + list(first_waits), inc="act")
            for half in range(2):
                p.emit("pe", lambda e, k=k, half=half: e.matmul(
                    p_sx[:, half * 512:(half + 1) * 512], ones_col[:, :],
                    src_sb[:, k * T + half * 512: k * T + (half + 1) * 512],
                    start=(k == 0), stop=(k == 15)),
                    waits=dmawait + (list(first_waits) + [("dve", ones_ready)] if k == 0 else []))
            for half in range(2):
                sx2_done[k] = p.emit("pe", lambda e, k=k, half=half: e.matmul(
                    p_sx2[:, half * 512:(half + 1) * 512], ones_col[:, :],
                    (scrA if k % 2 == 0 else scrB)[:, half * 512:(half + 1) * 512],
                    start=(k == 0), stop=(k == 15)),
                    waits=[("act", sq_done[k])],
                    inc=("pe" if half == 1 else None))
        stats_pe = sx2_done[15]
        p.emit("dve", lambda e: e.tensor_scalar_mul(st_m[:, :], p_sx[:, :], 1.0 / D),
               waits=[("pe", stats_pe)])
        p.emit("dve", lambda e: e.tensor_scalar_mul(st_a[:, :], p_sx2[:, :], 1.0 / D))
        p.emit("dve", lambda e: e.tensor_mul(st_b[:, :], st_m[:, :], st_m[:, :]))
        v_done = p.emit("dve", lambda e: e.tensor_sub(st_a[:, :], st_a[:, :], st_b[:, :]), inc="dve")
        p.emit("act", lambda e: e.activation(out=st_b[:, :], in_=st_a[:, :], func=AF.Ln, bias=eps_sb[:, :]),
               waits=[("dve", v_done), ("dve", ones_ready)])
        r_done = p.emit("act", lambda e: e.activation(out=st_a[:, :], in_=st_b[:, :], func=AF.Exp,
                                                      scale=-0.5), inc="act")
        ctx.close()
        ctx2 = ExitStack()
        p_bm = ctx2.enter_context(nc.psum_tensor(f"p_bm_{tag}", [128, T], F32))
        p_br = ctx2.enter_context(nc.psum_tensor(f"p_br_{tag}", [128, T], F32))
        bc_done = None
        for half in range(2):
            p.emit("pe", lambda e, half=half: e.matmul(
                p_bm[:, half * 512:(half + 1) * 512], ones_row[:, :],
                st_m[:, half * 512:(half + 1) * 512], start=True, stop=True),
                waits=[("act", r_done), ("dve", ones_ready)])
            bc_done = p.emit("pe", lambda e, half=half: e.matmul(
                p_br[:, half * 512:(half + 1) * 512], ones_row[:, :],
                st_a[:, half * 512:(half + 1) * 512], start=True, stop=True),
                inc=("pe" if half == 1 else None))
        done = None
        for k in range(16):
            p.emit("dve", lambda e, k=k: e.tensor_sub(scrA[:, :], src_sb[:, k * T:(k + 1) * T], p_bm[:, :]),
                   waits=[("pe", bc_done)])
            done = p.emit("dve", lambda e, k=k: e.tensor_mul(dst_sb[:, k * T:(k + 1) * T], scrA[:, :],
                                                             p_br[:, :]),
                          inc=("dve" if k == 15 else None))
        ctx2.close()
        return done


    def _finish_early():
        fw = []
        if p.v("d_tap"):
            fw.append(("d_tap", p.v("d_tap")))
        for nm in list(p.cnt):
            if nm.startswith("d_") and nm != "d_tap":
                fw.append((nm, p.v(nm)))
        p.emit("sp", lambda e: e.nop(), waits=fw)
        stack_ffn_local = None
        p.realize()
        return nc

    ln1_done = emit_layernorm(x_sb, actA, x_dma_gate=True,
                              first_waits=[("pe", p05_pe_end), ("dve", p05_dve_end),
                                           ("act", s7_done)])
    if "xln1" in tap_tensors:
        p.dma("act", tap_tensors["xln1"].ap(), actA[:, :], waits=[("dve", ln1_done)], inc="d_tap")
    if upto == "ln1":
        return _finish_early()

    ctxp2 = ExitStack()
    psumP = [ctxp2.enter_context(nc.psum_tensor(f"psumP{i}", [128, T], F32)) for i in range(2)]
    evac_val = {}
    store_val = {}
    cast_val = {}
    pe_proj = {}
    for widx, (wt, dst) in enumerate(((wq_h, q_hbm), (wk_h, k_hbm), (wv_h, v_hbm))):
        for m in range(16):
            gi = widx * 16 + m
            par = gi % 2
            p.dma("sp", wland[par][:, :], wt.ap()[m],
                  waits=[("dve", cast_val.get(gi - 2, p05_dve_end)),
                         ("pe", p05_pe_end if gi < 2 else 0),
                         ("act", s7_done if gi < 2 else 0)], inc=f"d_w{par}")
            cast_val[gi] = p.emit(
                "dve", lambda e, par=par: e.tensor_copy(wbf[par][:, :], wland[par][:, :]),
                waits=[(f"d_w{par}", 16 * (gi // 2 + 1)), ("dve", ln1_done),
                       ("pe", pe_proj.get(gi - 2, 0))], inc="dve")
            mm_waits = [("dve", cast_val[gi]), ("act", evac_val.get(gi - 2, 0))]
            pe_val = None
            for k in range(16):
                for half in range(2):
                    pe_val = p.emit("pe", lambda e, par=par, k=k, half=half: e.matmul(
                        psumP[par][:, half * 512:(half + 1) * 512],
                        wbf[par][:, k * 128:(k + 1) * 128],
                        actA[:, k * T + half * 512: k * T + (half + 1) * 512],
                        start=(k == 0), stop=(k == 15)),
                        waits=(mm_waits if (k == 0 and half == 0) else ()),
                        inc=("pe" if (k == 15 and half == 1) else None))
            pe_proj[gi] = pe_val
            evac_val[gi] = p.emit(
                "act", lambda e, par=par: e.activation(out=stg[par][:, :], in_=psumP[par][:, :],
                                                       func=AF.Copy),
                waits=[("pe", pe_val), (f"d_s{par}", 16 * store_val.get(gi - 2, 0))], inc="act")
            sv = p.dma("act", dst.ap()[m], stg[par][:, :], inc=f"d_s{par}")
            store_val[gi] = sv // 16
    last_cast_par = {0: cast_val[46], 1: cast_val[47]}
    ctxp2.close()
    if upto == "p2":
        return _finish_early()

    ctxp3 = ExitStack()
    ps_r = ctxp3.enter_context(nc.psum_tensor("ps_r", [128, T], F32))
    tp_T1 = ctxp3.enter_context(nc.psum_tensor("tp_T1", [128, 128], BF16))
    tp_T2 = ctxp3.enter_context(nc.psum_tensor("tp_T2", [128, 128], BF16))
    tp_S = [ctxp3.enter_context(nc.psum_tensor(f"tp_S{i}", [128, 128], F32)) for i in range(2)]
    tp_O = ctxp3.enter_context(nc.psum_tensor("tp_O", [128, 128], F32))
    tp_A = ctxp3.enter_context(nc.psum_tensor("tp_A", [128, 128], F32))

    head_end = {}
    last_ktil = 0
    last_vt = 0
    last_stil = {0: 0, 1: 0}
    last_evacO = 0
    last_kvupd = 0
    last_m2 = 0
    attn_evac_vals = np.zeros((NH, NCH), dtype=np.int64)
    qt2_dma = 0
    prev_head_kvcast = s7_done
    for h in range(NH):
        par = h % 2
        he = head_end.get(h - 2, {"pe": 0, "dve": 0, "act": 0})
        need = [(f"d_s{(h) % 2}", 16 * store_val[h]),
                (f"d_s{(16 + h) % 2}", 16 * store_val[16 + h]),
                (f"d_s{(32 + h) % 2}", 16 * store_val[32 + h]),
                ("pe", he["pe"]), ("dve", he["dve"])]
        p.dma("sp", bufQ[par][:, :], q_hbm.ap()[h], waits=need, inc=f"d_q{par}")
        p.dma("sp", bufK[par][:, :], k_hbm.ap()[h], inc=f"d_k{par}")
        p.dma("sp", bufV[par][:, :], v_hbm.ap()[h], inc=f"d_v{par}")
        dq_target = 16 * (h // 2 + 1)
        rot_w = [(f"d_q{par}", dq_target), (f"d_k{par}", dq_target), (f"d_v{par}", dq_target),
                 ("dve", last_m2), ("d_c", dc_all),
                 ("act", evac_val[47] if h == 0 else 0)]
        p.emit("pe", lambda e, par=par: e.matmul(ps_r[:, 0:512], PTs[:, :], bufQ[par][:, 0:512],
                                                 start=True, stop=True), waits=rot_w)
        pq_done = p.emit("pe", lambda e, par=par: e.matmul(ps_r[:, 512:1024], PTs[:, :],
                                                           bufQ[par][:, 512:1024],
                                                           start=True, stop=True), inc="pe")
        p.emit("dve", lambda e, par=par: e.tensor_mul(scrA[:, :], bufQ[par][:, :], cFs[:, :]),
               waits=[("pe", pq_done)])
        last_m2 = p.emit("dve", lambda e: e.tensor_mul(scrB[:, :], ps_r[:, :], sFs[:, :]), inc="dve")
        p.emit("dve", lambda e: e.tensor_add(q_rot[:, :], scrA[:, :], scrB[:, :]))
        p.emit("dve", lambda e, h=h: e.tensor_mul(
            qt[:, :].rearrange("p (c i) -> p c i", c=NCH),
            q_rot[:, :].rearrange("p (c i) -> p c i", c=NCH),
            qdTs[:, h * 128:(h + 1) * 128].unsqueeze(1).broadcast_to([128, NCH, 128])))
        qt_done = p.v("dve")
        qt2_done = None
        for c in range(NCH):
            qt2_done = p.emit("act", lambda e, h=h, c=c: e.activation(
                out=qt2[:, c * 128:(c + 1) * 128], in_=qt[:, c * 128:(c + 1) * 128],
                func=AF.Copy, scale=rcgs[:, h * NCH + c: h * NCH + c + 1]),
                waits=([("d_qt2", 16 * qt2_dma), ("dve", qt_done)] if c == 0 else ()),
                inc="act")
        dv = p.dma("act", qt2_hbm.ap()[h], qt2[:, :], waits=[("act", qt2_done)], inc="d_qt2")
        qt2_dma = dv // 16
        p.emit("pe", lambda e, par=par: e.matmul(ps_r[:, 0:512], PTs[:, :], bufK[par][:, 0:512],
                                                 start=True, stop=True), waits=[("dve", last_m2)])
        pk_done = p.emit("pe", lambda e, par=par: e.matmul(ps_r[:, 512:1024], PTs[:, :],
                                                           bufK[par][:, 512:1024],
                                                           start=True, stop=True), inc="pe")
        p.emit("dve", lambda e, par=par: e.tensor_mul(scrA[:, :], bufK[par][:, :], cFs[:, :]),
               waits=[("pe", pk_done)])
        last_m2 = p.emit("dve", lambda e: e.tensor_mul(scrB[:, :], ps_r[:, :], sFs[:, :]), inc="dve")
        krot_done = p.emit("dve", lambda e: e.tensor_add(k_rot[:, :], scrA[:, :], scrB[:, :]), inc="dve")
        p.emit("dve", lambda e: e.memset(kvf[:, :], 0.0),
               waits=[("act", prev_head_kvcast)])
        kvz = p.emit("dve", lambda e, h=h: e.memset(S7stage[:, h * 128:(h + 1) * 128], 0.0), inc="dve")
        kvcast_prev = kvz
        for c in range(NCH):
            ce = c % 2
            cs = slice(c * 128, (c + 1) * 128)
            p.emit("pe", lambda e, cs=cs: e.transpose(tp_T1[:, :], k_rot[:, cs], idents[:, :]),
                   waits=[("dve", krot_done), ("act", last_ktil)])
            t2v = p.emit("pe", lambda e, cs=cs, par=par: e.transpose(tp_T2[:, :],
                                                                     bufV[par][:, cs],
                                                                     idents[:, :]),
                         waits=[("act", last_vt)], inc="pe")
            s_done = p.emit("pe", lambda e, ce=ce, cs=cs: e.matmul(tp_S[ce][:, :], k_rot[:, cs],
                                                                   q_rot[:, cs], start=True, stop=True),
                            waits=[("dve", last_stil[ce])], inc="pe")
            last_ktil = p.emit(
                "act", lambda e, ce=ce, h=h: e.activation(out=ktil[ce][:, :], in_=tp_T1[:, :],
                                                          func=AF.Copy, scale=kds[:, h:h + 1]),
                waits=[("pe", t2v)], inc="act")
            last_vt = p.emit(
                "act", lambda e, ce=ce: e.activation(out=vtok[ce][:, :], in_=tp_T2[:, :],
                                                     func=AF.Copy),
                waits=[("pe", t2v)], inc="act")
            last_stil[ce] = p.emit(
                "dve", lambda e, ce=ce, h=h: e.tensor_mul(stil[ce][:, :], tp_S[ce][:, :],
                                                          dmTs[:, h * 128:(h + 1) * 128]),
                waits=[("pe", s_done)], inc="dve")
            p.emit("pe", lambda e, ce=ce: e.matmul(tp_O[:, :], vtok[ce][:, :], stil[ce][:, :],
                                                   start=True, stop=False),
                   waits=[("act", max(last_vt, last_evacO)), ("dve", last_stil[ce])])
            o_done = p.emit("pe", lambda e, h=h, cs=cs: e.matmul(tp_O[:, :],
                                                                 S7stage[:, h * 128:(h + 1) * 128],
                                                                 qt[:, cs], start=False, stop=True),
                            waits=[("act", kvcast_prev if c > 0 else 0), ("dve", kvz)], inc="pe")
            a_done = p.emit("pe", lambda e, ce=ce: e.matmul(tp_A[:, :], ktil[ce][:, :], vtok[ce][:, :],
                                                            start=True, stop=True),
                            waits=[("act", last_ktil), ("dve", last_kvupd)], inc="pe")
            last_evacO = p.emit(
                "act", lambda e, h=h, c=c: e.activation(
                    out=actA[:, h * T + c * 128: h * T + (c + 1) * 128], in_=tp_O[:, :], func=AF.Copy),
                waits=[("pe", o_done)], inc="act")
            attn_evac_vals[h, c] = last_evacO
            last_kvupd = p.emit(
                "dve", lambda e, h=h: e.scalar_tensor_tensor(kvf[:, :], kvf[:, :],
                                                             float(r_heads[h] ** 128), tp_A[:, :],
                                                             ALU.mult, ALU.add),
                waits=[("pe", a_done)], inc="dve")
            kvcast_prev = p.emit(
                "act", lambda e, h=h: e.activation(out=S7stage[:, h * 128:(h + 1) * 128],
                                                   in_=kvf[:, :], func=AF.Copy),
                waits=[("dve", last_kvupd)], inc="act")
        prev_head_kvcast = kvcast_prev
        head_end[h] = {"pe": p.v("pe"), "dve": p.v("dve"), "act": p.v("act")}
    all_kv_act = p.v("act")
    p3_pe_end = p.v("pe")
    p3_dve_end = p.v("dve")
    ctxp3.close()
    if "attnraw" in tap_tensors:
        p.dma("act", tap_tensors["attnraw"].ap(), actA[:, :],
              waits=[("act", all_kv_act), ("dve", p3_dve_end)], inc="d_tap")
    if upto == "p3":
        return _finish_early()

    ctxc = ExitStack()
    ps_cr = [ctxc.enter_context(nc.psum_tensor(f"ps_cr{i}", [128, 128], F32)) for i in range(2)]
    last_corr = {0: 0, 1: 0}
    corr_pe = {}
    for h in range(NH):
        par = h % 2
        p.dma("sp", bufQ[par][:, :], qt2_hbm.ap()[h],
              waits=[("d_qt2", 16 * qt2_dma), ("pe", corr_pe.get(h - 2, p3_pe_end)),
                     ("dve", p3_dve_end if h < 2 else 0)],
              inc=f"d_q{par}")
        dq_now = p.v(f"d_q{par}")
        for cl in range(NCH):
            cp = cl % 2
            mmv = p.emit("pe", lambda e, cp=cp, h=h, cl=cl, par=par: e.matmul(
                ps_cr[cp][:, :], S7g[:, h * 128:(h + 1) * 128],
                bufQ[par][:, cl * 128:(cl + 1) * 128], start=True, stop=True),
                waits=[("act", s7_done), (f"d_q{par}", dq_now), ("dve", last_corr[cp]),
                       ("act", all_kv_act if (h == 0 and cl == 0) else 0)], inc="pe")
            last_corr[cp] = p.emit(
                "dve", lambda e, cp=cp, h=h, cl=cl: e.tensor_add(
                    actA[:, h * T + cl * 128: h * T + (cl + 1) * 128],
                    actA[:, h * T + cl * 128: h * T + (cl + 1) * 128], ps_cr[cp][:, :]),
                waits=[("pe", mmv)], inc="dve")
        corr_pe[h] = p.v("pe")
    corr_done = max(last_corr[0], last_corr[1])
    ctxc.close()
    if "attn" in tap_tensors:
        p.dma("act", tap_tensors["attn"].ap(), actA[:, :], waits=[("dve", corr_done)], inc="d_tap")
    if upto == "corr":
        return _finish_early()

    ctx4 = ExitStack()
    p_ss = ctx4.enter_context(nc.psum_tensor("p_ss", [1, T], F32))
    ss_done = None
    sq_prev = 0
    for k in range(16):
        scr = scrA if k % 2 == 0 else scrB
        sq = p.emit("act", lambda e, scr=scr, k=k: e.activation(
            out=scr[:, :], in_=actA[:, k * T:(k + 1) * T], func=AF.Square),
            waits=[("dve", corr_done), ("pe", sq_prev)], inc="act")
        for half in range(2):
            ss_done = p.emit("pe", lambda e, k=k, half=half: e.matmul(
                p_ss[:, half * 512:(half + 1) * 512], ones_col[:, :],
                (scrA if k % 2 == 0 else scrB)[:, half * 512:(half + 1) * 512],
                start=(k == 0), stop=(k == 15)),
                waits=[("act", sq), ("dve", corr_done if k == 0 else 0),
                       ("dve", ones_ready if k == 0 else 0)],
                inc=("pe" if half == 1 else None))
        sq_prev = ss_done
    p.emit("dve", lambda e: e.tensor_scalar_mul(st_a[:, :], p_ss[:, :], 1.0 / D),
           waits=[("pe", ss_done)], inc="dve")
    p.emit("act", lambda e: e.activation(out=st_b[:, :], in_=st_a[:, :], func=AF.Ln, bias=eps_sb[:, :]),
           waits=[("dve", p.v("dve"))])
    rms_r = p.emit("act", lambda e: e.activation(out=st_a[:, :], in_=st_b[:, :], func=AF.Exp,
                                                 scale=-0.5), inc="act")
    ctx4.close()
    ctx4b = ExitStack()
    p_br2 = ctx4b.enter_context(nc.psum_tensor("p_br2", [128, T], F32))
    rbc = None
    for half in range(2):
        rbc = p.emit("pe", lambda e, half=half: e.matmul(
            p_br2[:, half * 512:(half + 1) * 512], ones_row[:, :],
            st_a[:, half * 512:(half + 1) * 512], start=True, stop=True),
            waits=[("act", rms_r)], inc=("pe" if half == 1 else None))
    norm_done_act = None
    for k in range(16):
        mval = p.emit("dve", lambda e, k=k: e.tensor_mul(scrB[:, :] if k % 2 else scrA[:, :],
                                                         actA[:, k * T:(k + 1) * T], p_br2[:, :]),
                      waits=[("pe", rbc), ("act", norm_done_act if k >= 2 else 0)], inc="dve")
        norm_done_act = p.emit("act", lambda e, k=k: e.activation(
            out=actA[:, k * T:(k + 1) * T], in_=(scrB[:, :] if k % 2 else scrA[:, :]),
            func=AF.Copy, scale=rmsc[:, k:k + 1]),
            waits=[("dve", mval)], inc="act")
    norm_done = p.v("dve")
    ctx4b.close()
    ctx4c = ExitStack()
    psumW = [ctx4c.enter_context(nc.psum_tensor(f"psumW{i}", [128, T], F32)) for i in range(2)]
    res_val = {}
    xp_dma = {}
    cast_val2 = {}
    wo_pe_val = {}
    for m in range(16):
        par = m % 2
        p.dma("sp", wland[par][:, :], wo_h.ap()[m],
              waits=[("dve", cast_val2.get(m - 2, last_cast_par[par]))], inc=f"d_w{par}")
        wo_base = p.v(f"d_w{par}")
        cast_val2[m] = p.emit("dve", lambda e, par=par: e.tensor_copy(wbf[par][:, :], wland[par][:, :]),
                              waits=[(f"d_w{par}", wo_base), ("dve", norm_done),
                                     ("pe", wo_pe_val.get(m - 2, pe_proj[47]))], inc="dve")
        wo_pe = None
        for k in range(16):
            for half in range(2):
                wo_pe = p.emit("pe", lambda e, par=par, k=k, half=half: e.matmul(
                    psumW[par][:, half * 512:(half + 1) * 512],
                    wbf[par][:, k * 128:(k + 1) * 128],
                    actA[:, k * T + half * 512: k * T + (half + 1) * 512],
                    start=(k == 0), stop=(k == 15)),
                    waits=([("dve", max(cast_val2[m], res_val.get(m - 2, 0), norm_done)),
                            ("act", norm_done_act)]
                           if (k == 0 and half == 0) else ()),
                    inc=("pe" if (k == 15 and half == 1) else None))
        wo_pe_val[m] = wo_pe
        res_val[m] = p.emit("dve", lambda e, m=m, par=par: e.tensor_add(
            x_sb[:, m * T:(m + 1) * T], x_sb[:, m * T:(m + 1) * T], psumW[par][:, :]),
            waits=[("pe", wo_pe), ("d_xp", 16 * xp_dma.get(m - 2, 0))], inc="dve")
        dv = p.dma("act", xp_hbm.ap()[m], x_sb[:, m * T:(m + 1) * T],
                   waits=[("dve", res_val[m])], inc="d_xp")
        xp_dma[m] = dv // 16
    ctx4c.close()
    if "xprime" in tap_tensors:
        p.dma("act", tap_tensors["xprime"].ap(), x_sb[:, :], waits=[("dve", res_val[15])], inc="d_tap")
    if upto == "p4":
        return _finish_early()

    ln2_done = emit_layernorm(x_sb, actA, first_waits=[("dve", res_val[15])], tag="ln2")
    if "xln2" in tap_tensors:
        p.dma("act", tap_tensors["xln2"].ap(), actA[:, :], waits=[("dve", ln2_done)], inc="d_tap")
    if upto == "ln2":
        return _finish_early()

    stack_attn.close()
    stack_ffn = ExitStack()
    sf = stack_ffn.enter_context
    NSLOT = 66
    hid = sf(nc.sbuf_tensor("hid", [128, NSLOT * T], BF16))
    land1 = [sf(nc.sbuf_tensor(f"land1_{i}", [128, 1024], F32)) for i in range(2)]
    wb1 = [sf(nc.sbuf_tensor(f"wb1_{i}", [128, 2048], BF16)) for i in range(2)]
    stack_p6 = ExitStack()
    land3 = [stack_p6.enter_context(nc.sbuf_tensor(f"land3_{i}", [128, 1024], F32)) for i in range(2)]
    wb3 = [stack_p6.enter_context(nc.sbuf_tensor(f"wb3_{i}", [128, 2048], BF16)) for i in range(2)]
    gsil = [stack_p6.enter_context(nc.sbuf_tensor(f"gsil{i}", [128, T], BF16)) for i in range(2)]
    htmp = [stack_p6.enter_context(nc.sbuf_tensor(f"htmp{i}", [128, T], BF16)) for i in range(2)]

    ctx6 = ExitStack()
    pg = [ctx6.enter_context(nc.psum_tensor(f"pg{i}", [128, T], F32)) for i in range(2)]
    pl = [ctx6.enter_context(nc.psum_tensor(f"pl{i}", [128, T], F32)) for i in range(2)]
    morder = [bitrev6(i) for i in range(64)]
    sched = fwht_schedule(morder)
    slot_of = list(range(64))
    free_slots = [64, 65]
    chunk_src = {}
    evacA_val = {}
    silu_val = {}
    mul_val = {}
    cast6_val = {}
    sa_pe_val = {}
    last_bf = 0
    gate_ffn = [("pe", rbc), ("d_s0", 16 * store_val[46]), ("d_s1", 16 * store_val[47]),
                ("d_q0", p.v("d_q0")), ("d_q1", p.v("d_q1")), ("d_qt2", 16 * qt2_dma)]
    w13_pe_val = {}
    cast1_val = {}
    cast3_val = {}

    def emit_w13_dma(mi):
        if mi >= 64:
            return
        mm_ = morder[mi]
        for half in range(2):
            p.dma("sp", land1[half][:, :], w1_h.ap()[mm_][:, half * 1024:(half + 1) * 1024],
                  waits=[("act", cast1_val.get((mi - 1, half), 0))] + (gate_ffn if mi == 0 else []),
                  inc=f"d_f1{half}")
            p.dma("sp", land3[half][:, :], w3_h.ap()[mm_][:, half * 1024:(half + 1) * 1024],
                  waits=[("act", cast3_val.get((mi - 1, half), 0))], inc=f"d_f3{half}")

    def emit_w13_cast(mi):
        if mi >= 64:
            return
        par = mi % 2
        for half in range(2):
            cast1_val[(mi, half)] = p.emit("act", lambda e, par=par, half=half: e.activation(
                out=wb1[par][:, half * 1024:(half + 1) * 1024], in_=land1[half][:, :], func=AF.Copy),
                waits=[(f"d_f1{half}", 16 * (mi + 1)), ("pe", w13_pe_val.get(mi - 2, 0))], inc="act")
            cast3_val[(mi, half)] = p.emit("act", lambda e, par=par, half=half: e.activation(
                out=wb3[par][:, half * 1024:(half + 1) * 1024], in_=land3[half][:, :], func=AF.Copy),
                waits=[(f"d_f3{half}", 16 * (mi + 1))], inc="act")

    emit_w13_dma(0)
    emit_w13_cast(0)
    emit_w13_dma(1)
    for mi in range(64):
        m = morder[mi]
        par = mi % 2
        w1_pe = None
        for k in range(16):
            for half in range(2):
                w1_pe = p.emit("pe", lambda e, par=par, k=k, half=half: e.matmul(
                    pg[par][:, half * 512:(half + 1) * 512],
                    wb1[par][:, k * 128:(k + 1) * 128],
                    actA[:, k * T + half * 512: k * T + (half + 1) * 512],
                    start=(k == 0), stop=(k == 15)),
                    waits=([("act", max(cast3_val[(mi, 1)], evacA_val.get(mi - 2, 0))),
                            ("dve", ln2_done if mi == 0 else 0)]
                           if (k == 0 and half == 0) else ()),
                    inc=("pe" if (k == 15 and half == 1) else None))
        w13_pe = None
        for k in range(16):
            for half in range(2):
                w13_pe = p.emit("pe", lambda e, par=par, k=k, half=half: e.matmul(
                    pl[par][:, half * 512:(half + 1) * 512],
                    wb3[par][:, k * 128:(k + 1) * 128],
                    actA[:, k * T + half * 512: k * T + (half + 1) * 512],
                    start=(k == 0), stop=(k == 15)),
                    waits=([("dve", mul_val.get(mi - 2, 0))] if (k == 0 and half == 0) else ()),
                    inc=("pe" if (k == 15 and half == 1) else None))
        w13_pe_val[mi] = w13_pe
        emit_w13_cast(mi + 1)
        emit_w13_dma(mi + 2)
        silu_val[mi] = p.emit("act", lambda e, par=par: e.activation(out=gsil[par][:, :],
                                                                     in_=pg[par][:, :], func=AF.Silu),
                              waits=[("pe", w1_pe)], inc="act")
        mul_val[mi] = p.emit("dve", lambda e, par=par: e.tensor_mul(htmp[par][:, :], gsil[par][:, :],
                                                                    pl[par][:, :]),
                             waits=[("act", silu_val[mi]), ("pe", w13_pe),
                                    ("pe", sa_pe_val.get(mi - 2, 0))], inc="dve")
        sa_pe = None
        for half in range(2):
            sa_pe = p.emit("pe", lambda e, par=par, half=half: e.matmul(
                pg[par][:, half * 512:(half + 1) * 512], H128s[:, :],
                htmp[par][:, half * 512:(half + 1) * 512], start=True, stop=True),
                waits=([("dve", mul_val[mi]), ("act", silu_val[mi])] if half == 0 else ()),
                inc=("pe" if half == 1 else None))
        sa_pe_val[mi] = sa_pe
        slot = slot_of[m]
        evacA_val[mi] = p.emit("act", lambda e, par=par, slot=slot: e.activation(
            out=hid[:, slot * T:(slot + 1) * T], in_=pg[par][:, :], func=AF.Copy),
            waits=[("pe", sa_pe), ("d_xp", 16 * xp_dma[15] if mi == 0 else 0)], inc="act")
        chunk_src[m] = ("act", evacA_val[mi])
        for (a, b) in sched[mi]:
            sa_slot, sb_slot = slot_of[a], slot_of[b]
            fa = free_slots.pop(0)
            fb = free_slots.pop(0)
            waits = [("act", val) for (kind, val) in (chunk_src[a], chunk_src[b]) if kind == "act"]
            p.emit("dve", lambda e, fa=fa, sa_slot=sa_slot, sb_slot=sb_slot: e.tensor_add(
                hid[:, fa * T:(fa + 1) * T], hid[:, sa_slot * T:(sa_slot + 1) * T],
                hid[:, sb_slot * T:(sb_slot + 1) * T]), waits=waits)
            last_bf = p.emit("dve", lambda e, fb=fb, sa_slot=sa_slot, sb_slot=sb_slot: e.tensor_sub(
                hid[:, fb * T:(fb + 1) * T], hid[:, sa_slot * T:(sa_slot + 1) * T],
                hid[:, sb_slot * T:(sb_slot + 1) * T]), inc="dve")
            free_slots.append(sa_slot)
            free_slots.append(sb_slot)
            slot_of[a], slot_of[b] = fa, fb
            chunk_src[a] = ("dve", last_bf)
            chunk_src[b] = ("dve", last_bf)
    ctx6.close()
    stack_p6.close()
    xpb = [stack_ffn.enter_context(nc.sbuf_tensor(f"xpb{i}", [128, T], F32)) for i in range(4)]
    if "hid" in tap_tensors:
        for c in range(64):
            p.dma("act", tap_tensors["hid"].ap()[c], hid[:, slot_of[c] * T:(slot_of[c] + 1) * T],
                  waits=[("dve", last_bf)], inc="d_tap")
    if upto == "p6":
        return _finish_early()

    ctx8 = ExitStack()
    p8 = [ctx8.enter_context(nc.psum_tensor(f"p8_{i}", [128, T], F32)) for i in range(4)]
    w2cast = {}
    evac8 = {}
    out_dma_val = {}
    blk_dve = {}
    p8_pe_val = {}
    xb_pref = {}
    for mblk in range(4):
        for m4 in range(4):
            mchunk = mblk * 4 + m4
            p.dma("pool", xpb[m4][:, :], xp_hbm.ap()[mchunk],
                  waits=[("d_xp", 16 * xp_dma[15]),
                         ("pe", sa_pe_val[63] if mblk == 0 else 0),
                         ("dve", mul_val[63] if mblk == 0 else 0),
                         ("d_out", 16 * out_dma_val.get((mblk - 1, m4), 0))],
                  inc=f"d_xb{m4}")
            xb_pref[(mblk, m4)] = p.v(f"d_xb{m4}")
        for cg in range(16):
            par = cg % 2
            gi = mblk * 16 + cg
            for half in range(2):
                p.dma("sp", land1[half][:, :].rearrange("p (c m) -> p c m", c=2),
                      w2_h.ap()[mblk, cg * 4 + 2 * half: cg * 4 + 2 * half + 2].transpose([1, 0, 2]),
                      waits=[("act", w2cast.get((gi - 1, half), cast1_val[(63, half)] if gi == 0 else 0))],
                      inc=f"d_f1{half}")
            for half in range(2):
                w2cast[(gi, half)] = p.emit("act", lambda e, par=par, half=half: e.activation(
                    out=wb1[par][:, half * 1024:(half + 1) * 1024], in_=land1[half][:, :],
                    func=AF.Copy, scale=W2SCALE),
                    waits=[(f"d_f1{half}", 16 * (64 + gi + 1)),
                           ("pe", p8_pe_val.get(gi - 2, w13_pe_val[62 + par]))], inc="act")
            for cc in range(4):
                c = cg * 4 + cc
                hs = slot_of[c]
                for m4 in range(4):
                    for half in range(2):
                        p8_pe_val[gi] = p.emit(
                            "pe", lambda e, par=par, cc=cc, m4=m4, half=half, hs=hs, c=c: e.matmul(
                                p8[m4][:, half * 512:(half + 1) * 512],
                                wb1[par][:, cc * 512 + m4 * 128: cc * 512 + (m4 + 1) * 128],
                                hid[:, hs * T + half * 512: hs * T + (half + 1) * 512],
                                start=(c == 0), stop=(c == 63)),
                            waits=(([("act", w2cast[(gi, 1)]),
                                     ("dve", last_bf if gi % 16 == 0 else 0)]
                                    if (cc == 0 and m4 == 0 and half == 0) else []) +
                                   ([("dve", evac8.get((mblk - 1, m4), 0))]
                                    if (cc == 0 and half == 0 and gi % 16 == 0) else [])),
                            inc=("pe" if (cc == 3 and m4 == 3 and half == 1) else None))
        blk_pe = p.v("pe")
        for m4 in range(4):
            mchunk = mblk * 4 + m4
            par = m4 % 2
            ev = p.emit("dve", lambda e, m4=m4: e.tensor_add(
                xpb[m4][:, :], xpb[m4][:, :], p8[m4][:, :]),
                waits=[("pe", blk_pe), (f"d_xb{m4}", xb_pref[(mblk, m4)])], inc="dve")
            evac8[(mblk, m4)] = ev
            dv = p.dma("act", out_h.ap().rearrange("(k p) t -> p k t", p=128)[:, mchunk, :],
                       xpb[m4][:, :], waits=[("dve", ev)], inc="d_out")
            out_dma_val[(mblk, m4)] = dv // 16
        blk_dve[mblk] = p.v("dve")
    ctx8.close()

    final_waits = [("d_out", 16 * out_dma_val[(3, 3)])]
    if p.v("d_tap"):
        final_waits.append(("d_tap", p.v("d_tap")))
    p.emit("sp", lambda e: e.nop(), waits=final_waits)

    stack_ffn.close()
    stack_persist.close()
    p.realize()
    return nc


def host_prep(inputs):
    x = np.asarray(inputs["L0_attn_x"], dtype=np.float32)
    wq_t = tile_lhsT(np.asarray(inputs["L0_attn_Wq"], dtype=np.float32))
    wk_t = tile_lhsT(np.asarray(inputs["L0_attn_Wk"], dtype=np.float32))
    wv_t = tile_lhsT(np.asarray(inputs["L0_attn_Wv"], dtype=np.float32))
    wo_t = tile_lhsT(np.asarray(inputs["L0_attn_Wo"], dtype=np.float32))
    w1_t = tile_lhsT(np.ascontiguousarray(np.asarray(inputs["L0_ffn_W1"], dtype=np.float32).T))
    w3_t = tile_lhsT(np.ascontiguousarray(np.asarray(inputs["L0_ffn_W3"], dtype=np.float32).T))
    w2T = np.ascontiguousarray(np.asarray(inputs["L0_ffn_W2"], dtype=np.float32).T)
    w2_t = np.ascontiguousarray(w2T.reshape(64, 128, 4, 512).transpose(2, 0, 1, 3))
    rms_cols = np.ascontiguousarray(np.asarray(inputs["L0_attn_rms_w"], dtype=np.float32).reshape(16, 128).T)
    in_maps = []
    zeros_xp = np.zeros((D, T), dtype=np.float32)
    for core in range(8):
        b, half = core // 2, core % 2
        xT = np.ascontiguousarray(x[b, half * T:(half + 1) * T, :].T)
        if half == 1:
            xp2 = np.ascontiguousarray(x[b, 0:T, :].T)
        else:
            xp2 = zeros_xp
        m = {"x": xT, "xp2": xp2, "wq_t": wq_t, "wk_t": wk_t, "wv_t": wv_t, "wo_t": wo_t,
             "w1_t": w1_t, "w3_t": w3_t, "w2_t": w2_t, "rms_cols": rms_cols}
        m.update(host_constants(core))
        in_maps.append(m)
    return in_maps


def kernel(**inputs):
    in_maps = host_prep(inputs)
    nc = build_program()
    res = run_bass_kernel_spmd(nc, in_maps, core_ids=list(range(8)))
    out = np.empty((4, 2048, D), dtype=np.float32)
    for core in range(8):
        b, half = core // 2, core % 2
        out[b, half * T:(half + 1) * T, :] = res.results[core]["out"].T
    return out


# version 14 (final)
# speedup vs baseline: 1.0315x; 1.0157x over previous
import math
import numpy as np

import concourse.bass as bass
import concourse.mybir as mybir
from concourse.bass_utils import run_bass_kernel_spmd

F32 = mybir.dt.float32
BF16 = mybir.dt.bfloat16
AF = mybir.ActivationFunctionType
ALU = mybir.AluOpType

D = 2048
NH = 16
T = 1024
NCH = 8
C = 128
NHC = 64
EPS = 1e-5
W2SCALE = 1.0 / math.sqrt(8192.0)

ENGS = ("pe", "act", "dve", "pool", "sp")


def _slopes():
    start = 2.0 ** (-(2.0 ** (-(math.log2(NH) - 3))))
    s = np.array([start * start ** i for i in range(NH)], dtype=np.float64)
    return np.exp(-s)


def _popcount_parity(a):
    v = a.copy()
    out = np.zeros_like(v)
    while v.any():
        out ^= (v & 1)
        v >>= 1
    return out


def host_constants(core_id):
    import ml_dtypes
    r = _slopes()
    t0 = (core_id % 2) * T
    t0p = (1 - core_id % 2) * T
    freq = (1.0 / 1024.0) ** np.linspace(0.0, 1.0, 32)
    t = np.arange(t0, t0 + T, dtype=np.float64)
    theta = t[None, :] * freq[:, None]
    cF = np.ones((128, T), dtype=np.float64)
    sF = np.zeros((128, T), dtype=np.float64)
    cF[0:32] = np.cos(theta); cF[64:96] = np.cos(theta)
    sF[0:32] = np.sin(theta); sF[64:96] = np.sin(theta)
    tp = np.arange(t0p, t0p + T, dtype=np.float64)
    thetap = tp[None, :] * freq[:, None]
    cFp = np.ones((128, T), dtype=np.float64)
    sFp = np.zeros((128, T), dtype=np.float64)
    cFp[0:32] = np.cos(thetap); cFp[64:96] = np.cos(thetap)
    sFp[0:32] = np.sin(thetap); sFp[64:96] = np.sin(thetap)
    P = np.zeros((128, 128), dtype=np.float64)
    for d in range(64):
        P[d, d + 64] = 1.0
        P[d + 64, d] = -1.0
    PT = P.T.copy()
    idx = np.arange(C, dtype=np.float64)
    diff = idx[None, :] - idx[:, None]
    dmaskT = np.zeros((C, NH * C))
    qdecT = np.zeros((C, NH * C))
    kd = np.zeros((C, NH))
    for h in range(NH):
        dmaskT[:, h * C:(h + 1) * C] = np.where(diff >= 0, r[h] ** np.maximum(diff, 0.0), 0.0)
        qdecT[:, h * C:(h + 1) * C] = (r[h] ** (idx + 1.0))[None, :]
        kd[:, h] = r[h] ** (C - 1.0 - idx)
    kd2 = np.zeros((C, NH * NCH))
    for h in range(NH):
        for c in range(NCH):
            kd2[:, h * NCH + c] = (r[h] ** (C - 1.0 - idx)) * (r[h] ** (C * (NCH - 1 - c)))
    gate = 1.0 if (core_id % 2) == 1 else 0.0
    rcg = np.zeros((C, NH * NCH))
    for h in range(NH):
        for cl in range(NCH):
            rcg[:, h * NCH + cl] = (r[h] ** (C * cl)) * gate
    i = np.arange(128)
    H128 = np.where(_popcount_parity(i[:, None] & i[None, :]) == 0, 1.0, -1.0)
    ident = np.eye(128)
    return {
        "cF": cF.astype(np.float32), "sF": sF.astype(np.float32),
        "cFp": cFp.astype(np.float32), "sFp": sFp.astype(np.float32),
        "PT": PT.astype(ml_dtypes.bfloat16), "dmaskT": dmaskT.astype(np.float32),
        "qdecT": qdecT.astype(np.float32), "kd": kd.astype(np.float32),
        "kd2": kd2.astype(np.float32),
        "rcg": rcg.astype(np.float32),
        "H128": H128.astype(ml_dtypes.bfloat16), "identity": ident.astype(ml_dtypes.bfloat16),
    }


def tile_lhsT(W):
    K, M = W.shape
    nk, nm = K // 128, M // 128
    t = W.reshape(nk, 128, nm, 128).transpose(2, 1, 0, 3)
    return np.ascontiguousarray(t.reshape(nm, 128, nk * 128))


def bitrev6(x):
    r = 0
    for b in range(6):
        r |= ((x >> b) & 1) << (5 - b)
    return r


def fwht_schedule(order):
    bits = [32, 16, 8, 4, 2, 1]
    lev = [-1] * 64
    sched = [[] for _ in range(64)]
    for mi, m in enumerate(order):
        lev[m] = 0
        changed = True
        while changed:
            changed = False
            for l in range(6):
                b = bits[l]
                for c in range(64):
                    if (c & b) == 0 and lev[c] == l and lev[c | b] == l:
                        sched[mi].append((c, c | b))
                        lev[c] = l + 1
                        lev[c | b] = l + 1
                        changed = True
    assert all(v == 6 for v in lev)
    return sched


class Plan:

    STRICT = ("dve", "act", "pool")

    def __init__(self, nc):
        self.nc = nc
        self.ops = {e: [] for e in ENGS}
        self.sems = {}
        self.cnt = {}
        self.hw = {e: {} for e in ENGS}
        for e in self.STRICT:
            self.s(e)

    def s(self, name):
        if name not in self.sems:
            self.sems[name] = self.nc.alloc_semaphore(name)
            self.cnt[name] = 0
        return name

    def v(self, name):
        return self.cnt.get(name, 0)

    def emit(self, eng, fn, waits=(), inc=None, amt=1):
        real_waits = []
        for (n, val) in waits:
            if val is None or val <= 0:
                continue
            self.s(n)
            if self.hw[eng].get(n, 0) >= val:
                continue
            self.hw[eng][n] = val
            real_waits.append((self.sems[n], val))
        target = None
        dma_sem = None
        if eng in self.STRICT and (inc is None or inc == eng):
            self.cnt[eng] += 1
            target = self.cnt[eng]
            self.ops[eng].append((tuple(real_waits), fn, self.sems[eng], 1, True))
            return target
        if inc is not None:
            self.s(inc)
            self.cnt[inc] += amt
            target = self.cnt[inc]
            dma_sem = self.sems[inc]
        self.ops[eng].append((tuple(real_waits), fn, dma_sem, amt, False))
        return target

    def dma(self, eng, out, in_, waits=(), inc=None):
        w = list(waits)
        if inc is not None and self.v(inc) > 0:
            w.append((inc, self.v(inc)))
        if eng in self.STRICT:
            w.append((eng, self.v(eng)))
        return self.emit(eng, (lambda e, o=out, i=in_: e.dma_start(out=o, in_=i)),
                         w, inc, 16)

    def realize(self):
        nc = self.nc
        plan = self

        def replay(stream):
            def f(e):
                selfsem = plan.sems.get(stream) if stream in plan.STRICT else None
                self_idx = 0
                for (waits, fn, sem, amt, is_self) in plan.ops[stream]:
                    for (s, v) in waits:
                        e.wait_ge(s, v)
                    if is_self and self_idx > 0:
                        e.wait_ge(selfsem, self_idx)
                    inst = fn(e)
                    if sem is not None:
                        inst.then_inc(sem, amt)
                    if is_self:
                        self_idx += 1
            return f

        with nc.Block() as blk:
            blk.tensor(replay("pe"))
            blk.scalar(replay("act"))
            blk.vector(replay("dve"))
            blk.gpsimd(replay("pool"))
            blk.sync(replay("sp"))


def build_program(taps=(), use_collective=True, upto="all"):
    from contextlib import ExitStack
    nc = bass.Bass("TRN2", target_bir_lowering=False)
    p = Plan(nc)
    r_heads = _slopes()

    x_h = nc.dram_tensor("x", [D, T], F32, kind="ExternalInput")
    wq_h = nc.dram_tensor("wq_t", [16, 128, 2048], F32, kind="ExternalInput")
    wk_h = nc.dram_tensor("wk_t", [16, 128, 2048], F32, kind="ExternalInput")
    wv_h = nc.dram_tensor("wv_t", [16, 128, 2048], F32, kind="ExternalInput")
    wo_h = nc.dram_tensor("wo_t", [16, 128, 2048], F32, kind="ExternalInput")
    w1_h = nc.dram_tensor("w1_t", [64, 128, 2048], F32, kind="ExternalInput")
    w3_h = nc.dram_tensor("w3_t", [64, 128, 2048], F32, kind="ExternalInput")
    w2_h = nc.dram_tensor("w2_t", [4, 64, 128, 512], F32, kind="ExternalInput")
    rmsc_h = nc.dram_tensor("rms_cols", [128, 16], F32, kind="ExternalInput")
    cF_h = nc.dram_tensor("cF", [128, T], F32, kind="ExternalInput")
    sF_h = nc.dram_tensor("sF", [128, T], F32, kind="ExternalInput")
    xp2_h = nc.dram_tensor("xp2", [D, T], F32, kind="ExternalInput")
    cFp_h = nc.dram_tensor("cFp", [128, T], F32, kind="ExternalInput")
    sFp_h = nc.dram_tensor("sFp", [128, T], F32, kind="ExternalInput")
    PT_h = nc.dram_tensor("PT", [128, 128], BF16, kind="ExternalInput")
    dmT_h = nc.dram_tensor("dmaskT", [128, NH * 128], F32, kind="ExternalInput")
    qdT_h = nc.dram_tensor("qdecT", [128, NH * 128], F32, kind="ExternalInput")
    kd_h = nc.dram_tensor("kd", [128, NH], F32, kind="ExternalInput")
    kd2_h = nc.dram_tensor("kd2", [128, NH * NCH], F32, kind="ExternalInput")
    rcg_h = nc.dram_tensor("rcg", [128, NH * NCH], F32, kind="ExternalInput")
    H128_h = nc.dram_tensor("H128", [128, 128], BF16, kind="ExternalInput")
    id_h = nc.dram_tensor("identity", [128, 128], BF16, kind="ExternalInput")
    out_h = nc.dram_tensor("out", [D, T], F32, kind="ExternalOutput")

    def scratch(name, shape, dtype):
        kind = "ExternalOutput" if name in taps else "Internal"
        return nc.dram_tensor(name, shape, dtype, kind=kind)

    q_hbm = scratch("q_s", [NH, 128, T], BF16)
    k_hbm = scratch("k_s", [NH, 128, T], BF16)
    v_hbm = scratch("v_s", [NH, 128, T], BF16)
    qt2_hbm = scratch("qt2_s", [NH, 128, T], BF16)
    xp_hbm = scratch("xp_s", [16, 128, T], F32)

    tap_tensors = {}
    for tname in taps:
        if tname == "xln1":
            tap_tensors["xln1"] = nc.dram_tensor("tap_xln1", [128, 16 * T], BF16, kind="ExternalOutput")
        if tname == "attn":
            tap_tensors["attn"] = nc.dram_tensor("tap_attn", [128, 16 * T], BF16, kind="ExternalOutput")
        if tname == "attnraw":
            tap_tensors["attnraw"] = nc.dram_tensor("tap_attnraw", [128, 16 * T], BF16, kind="ExternalOutput")
        if tname == "xprime":
            tap_tensors["xprime"] = nc.dram_tensor("tap_xprime", [128, 16 * T], F32, kind="ExternalOutput")
        if tname == "xln2":
            tap_tensors["xln2"] = nc.dram_tensor("tap_xln2", [128, 16 * T], BF16, kind="ExternalOutput")
        if tname == "hid":
            tap_tensors["hid"] = nc.dram_tensor("tap_hid", [NHC, 128, T], BF16, kind="ExternalOutput")

    stack_persist = ExitStack()
    stack_attn = ExitStack()

    sb = stack_persist.enter_context
    actA = sb(nc.sbuf_tensor("actA", [128, 16 * T], BF16))
    eps_sb = sb(nc.sbuf_tensor("eps_sb", [1, 1], F32))
    ones_col = sb(nc.sbuf_tensor("ones_col", [128, 1], F32))
    ones_row = sb(nc.sbuf_tensor("ones_row", [1, 128], F32))
    H128s = sb(nc.sbuf_tensor("H128s", [128, 128], BF16))

    sa = stack_attn.enter_context
    idents = sa(nc.sbuf_tensor("idents", [128, 128], BF16))
    rmsc = sa(nc.sbuf_tensor("rmsc", [128, 16], F32))
    st_a = sa(nc.sbuf_tensor("st_a", [1, T], F32))
    st_b = sa(nc.sbuf_tensor("st_b", [1, T], F32))
    st_m = sa(nc.sbuf_tensor("st_m", [1, T], F32))
    x_sb = sa(nc.sbuf_tensor("x_sb", [128, 16 * T], F32))
    scrA = sa(nc.sbuf_tensor("scrA", [128, T], F32))
    scrB = sa(nc.sbuf_tensor("scrB", [128, T], F32))
    cFs = sa(nc.sbuf_tensor("cFs", [128, T], F32))
    sFs = sa(nc.sbuf_tensor("sFs", [128, T], F32))
    PTs = sa(nc.sbuf_tensor("PTs", [128, 128], BF16))
    kds = sa(nc.sbuf_tensor("kds", [128, NH], F32))
    rcgs = sa(nc.sbuf_tensor("rcgs", [128, NH * NCH], F32))
    kvf = sa(nc.sbuf_tensor("kvf", [128, 128], F32))
    S7g = sa(nc.sbuf_tensor("S7g", [128, NH * 128], BF16))
    stack_pre = ExitStack()
    sp_ = stack_pre.enter_context
    xlnp = sp_(nc.sbuf_tensor("xlnp", [128, 16 * T], BF16))
    preland = [sp_(nc.sbuf_tensor(f"preland{i}", [128, T], F32)) for i in range(2)]
    wbfp = [sp_(nc.sbuf_tensor(f"wbfp{i}", [128, 2048], BF16)) for i in range(2)]
    kpbuf = sp_(nc.sbuf_tensor("kpbuf", [128, T], BF16))
    kp_rot = sp_(nc.sbuf_tensor("kp_rot", [128, T], BF16))
    vpbuf = sp_(nc.sbuf_tensor("vpbuf", [128, T], BF16))
    cFp = sp_(nc.sbuf_tensor("cFp_s", [128, T], F32))
    sFp = sp_(nc.sbuf_tensor("sFp_s", [128, T], F32))
    kd2s = sp_(nc.sbuf_tensor("kd2s", [128, NH * NCH], F32))
    ktp = [sp_(nc.sbuf_tensor(f"ktp{i}", [128, 128], BF16)) for i in range(2)]
    vtp = [sp_(nc.sbuf_tensor(f"vtp{i}", [128, 128], BF16)) for i in range(2)]

    x_ap3 = x_h.ap().rearrange("(k p) t -> p k t", p=128)
    xs_ap3 = x_sb[:, :].rearrange("p (k t) -> p k t", t=T)
    for g in range(4):
        p.dma("sp", xs_ap3[:, g * 4:(g + 1) * 4, :], x_ap3[:, g * 4:(g + 1) * 4, :], inc="d_x")
    for dst, src in ((cFs, cF_h), (sFs, sF_h), (PTs, PT_h),
                     (kds, kd_h), (rcgs, rcg_h), (H128s, H128_h), (idents, id_h), (rmsc, rmsc_h),
                     (cFp, cFp_h), (sFp, sFp_h), (kd2s, kd2_h)):
        p.dma("sp", dst[:, :], src.ap(), inc="d_c")
    dc_all = p.v("d_c")
    p.emit("dve", lambda e: e.memset(eps_sb[:, :], EPS), inc="dve")
    p.emit("dve", lambda e: e.memset(ones_col[:, :], 1.0), inc="dve")
    ones_ready = p.emit("dve", lambda e: e.memset(ones_row[:, :], 1.0), inc="dve")


    xp2_ap3 = xp2_h.ap().rearrange("(k p) t -> p k t", p=128)
    ctxpa = ExitStack()
    pp_sx = ctxpa.enter_context(nc.psum_tensor("pp_sx", [1, T], F32))
    pp_sx2 = ctxpa.enter_context(nc.psum_tensor("pp_sx2", [1, T], F32))
    sq_d = {}
    sx_d = {}
    sx2_d = {}
    for k in range(16):
        par = k % 2
        p.dma("sp", preland[par][:, :], xp2_ap3[:, k, :],
              waits=[("act", sq_d.get(k - 2, 0)), ("pe", sx_d.get(k - 2, 0))],
              inc=f"d_p{par}")
        dw = [(f"d_p{par}", 16 * (k // 2 + 1))]
        scr = scrA if par == 0 else scrB
        sq_d[k] = p.emit("act", lambda e, par=par, scr=scr: e.activation(
            out=scr[:, :], in_=preland[par][:, :], func=AF.Square),
            waits=dw + [("pe", sx2_d.get(k - 2, 0))], inc="act")
        for half in range(2):
            sx_d[k] = p.emit("pe", lambda e, par=par, k=k, half=half: e.matmul(
                pp_sx[:, half * 512:(half + 1) * 512], ones_col[:, :],
                preland[par][:, half * 512:(half + 1) * 512], start=(k == 0), stop=(k == 15)),
                waits=dw + [("dve", ones_ready)], inc=("pe" if half == 1 else None))
        for half in range(2):
            sx2_d[k] = p.emit("pe", lambda e, par=par, k=k, half=half: e.matmul(
                pp_sx2[:, half * 512:(half + 1) * 512], ones_col[:, :],
                (scrA if par == 0 else scrB)[:, half * 512:(half + 1) * 512],
                start=(k == 0), stop=(k == 15)),
                waits=[("act", sq_d[k])], inc=("pe" if half == 1 else None))
    p.emit("dve", lambda e: e.tensor_scalar_mul(st_m[:, :], pp_sx[:, :], 1.0 / D),
           waits=[("pe", sx2_d[15])])
    p.emit("dve", lambda e: e.tensor_scalar_mul(st_a[:, :], pp_sx2[:, :], 1.0 / D))
    p.emit("dve", lambda e: e.tensor_mul(st_b[:, :], st_m[:, :], st_m[:, :]))
    vp_done = p.emit("dve", lambda e: e.tensor_sub(st_a[:, :], st_a[:, :], st_b[:, :]), inc="dve")
    p.emit("act", lambda e: e.activation(out=st_b[:, :], in_=st_a[:, :], func=AF.Ln, bias=eps_sb[:, :]),
           waits=[("dve", vp_done)])
    rp_done = p.emit("act", lambda e: e.activation(out=st_a[:, :], in_=st_b[:, :], func=AF.Exp,
                                                   scale=-0.5), inc="act")
    ctxpa.close()
    ctxpb = ExitStack()
    pp_bm = ctxpb.enter_context(nc.psum_tensor("pp_bm", [128, T], F32))
    pp_br = ctxpb.enter_context(nc.psum_tensor("pp_br", [128, T], F32))
    bcp = None
    for half in range(2):
        p.emit("pe", lambda e, half=half: e.matmul(
            pp_bm[:, half * 512:(half + 1) * 512], ones_row[:, :],
            st_m[:, half * 512:(half + 1) * 512], start=True, stop=True),
            waits=[("act", rp_done)])
        bcp = p.emit("pe", lambda e, half=half: e.matmul(
            pp_br[:, half * 512:(half + 1) * 512], ones_row[:, :],
            st_a[:, half * 512:(half + 1) * 512], start=True, stop=True),
            inc=("pe" if half == 1 else None))
    nrm_d = {}
    for k in range(16):
        par = k % 2
        p.dma("sp", preland[par][:, :], xp2_ap3[:, k, :],
              waits=[("dve", nrm_d.get(k - 2, 0)), ("act", sq_d[15] if k < 2 else 0),
                     ("pe", sx_d[15] if k < 2 else 0)],
              inc=f"d_p{par}")
        dwv = [(f"d_p{par}", 16 * (8 + k // 2 + 1)), ("pe", bcp)]
        p.emit("dve", lambda e, par=par: e.tensor_sub(scrA[:, :], preland[par][:, :], pp_bm[:, :]),
               waits=dwv)
        nrm_d[k] = p.emit("dve", lambda e, k=k: e.tensor_mul(
            xlnp[:, k * T:(k + 1) * T], scrA[:, :], pp_br[:, :]), inc="dve")
    nrm_done = nrm_d[15]
    ctxpb.close()
    ctxpc = ExitStack()
    ppj = ctxpc.enter_context(nc.psum_tensor("ppj", [128, T], F32))
    pp_rot = ctxpc.enter_context(nc.psum_tensor("pp_rot", [128, T], F32))
    ppT1a = ctxpc.enter_context(nc.psum_tensor("ppT1a", [128, 128], BF16))
    ppT2a = ctxpc.enter_context(nc.psum_tensor("ppT2a", [128, 128], BF16))
    ppS7 = ctxpc.enter_context(nc.psum_tensor("ppS7", [128, 128], F32))
    ppT1b = pp_rot[:, 0:64].bitcast(BF16)
    ppT2b = pp_rot[:, 512:576].bitcast(BF16)
    wl_rd = {}
    last_ev = 0
    prev_cast = 0
    prev_m2p = 0
    prev_ktp = {0: 0, 1: 0}
    prev_vtp = {0: 0, 1: 0}
    s7_cast = 0
    dp_cnt = 32
    cast_of = {}

    def feed_tile(wt_h, h, wpar, wsel):
        nonlocal prev_cast, dp_cnt
        for half in range(2):
            p.dma("sp", preland[half][:, :], wt_h.ap()[h][:, half * 1024:(half + 1) * 1024],
                  waits=[("dve", prev_cast), ("dve", nrm_done)], inc=f"d_p{half}")
        gen = dp_cnt // 2 + 1
        dp_cnt += 2
        return gen

    def cast_tile(gen, wpar):
        nonlocal prev_cast
        for half in range(2):
            prev_cast = p.emit("dve", lambda e, half=half, wpar=wpar: e.tensor_copy(
                wbfp[wpar][:, half * 1024:(half + 1) * 1024], preland[half][:, :]),
                waits=[(f"d_p{half}", 16 * gen), ("pe", wl_rd.get(wpar, 0))], inc="dve")
        return prev_cast

    def mm_emitter(wpar, cval):
        for j0 in range(0, 32, 4):
            def grp(j0=j0, wpar=wpar, cval=cval):
                mmv = None
                for j in range(j0, j0 + 4):
                    k, half = j // 2, j % 2
                    mmv = p.emit("pe", lambda e, k=k, half=half, wpar=wpar: e.matmul(
                        ppj[:, half * 512:(half + 1) * 512],
                        wbfp[wpar][:, k * 128:(k + 1) * 128],
                        xlnp[:, k * T + half * 512: k * T + (half + 1) * 512],
                        start=(k == 0), stop=(k == 15)),
                        waits=([("dve", max(cval, nrm_done)), ("act", last_ev)]
                               if j == j0 == 0 else ()),
                        inc=("pe" if j == 31 else None))
                return mmv
            yield grp

    g0 = feed_tile(wk_h, 0, 0, 0)
    cast_of[(0, 0)] = cast_tile(g0, 0)
    mm_end = None
    for grp in mm_emitter(0, cast_of[(0, 0)]):
        mm_end = grp()
    wl_rd[0] = mm_end
    for h in range(NH):
        ev_kp = p.emit("act", lambda e: e.activation(out=kpbuf[:, :], in_=ppj[:, :], func=AF.Copy),
                       waits=[("pe", wl_rd[0])], inc="act")
        last_ev = ev_kp
        gv = feed_tile(wv_h, h, 1, 1)
        cast_of[(h, 1)] = cast_tile(gv, 1)
        mm_end = None
        for grp in mm_emitter(1, cast_of[(h, 1)]):
            mm_end = grp()
        wl_rd[1] = mm_end
        ev_vp = p.emit("act", lambda e: e.activation(out=vpbuf[:, :], in_=ppj[:, :], func=AF.Copy),
                       waits=[("pe", wl_rd[1])], inc="act")
        last_ev = ev_vp
        p.emit("pe", lambda e: e.matmul(pp_rot[:, 0:512], PTs[:, :], kpbuf[:, 0:512],
                                        start=True, stop=True),
               waits=[("act", ev_vp), ("dve", prev_m2p), ("d_c", dc_all)])
        prot = p.emit("pe", lambda e: e.matmul(pp_rot[:, 512:1024], PTs[:, :], kpbuf[:, 512:1024],
                                               start=True, stop=True), inc="pe")
        p.emit("dve", lambda e: e.tensor_mul(scrA[:, :], kpbuf[:, :], cFp[:, :]),
               waits=[("pe", prot)])
        prev_m2p = p.emit("dve", lambda e: e.tensor_mul(scrB[:, :], pp_rot[:, :], sFp[:, :]), inc="dve")
        krotp = p.emit("dve", lambda e: e.tensor_add(kp_rot[:, :], scrA[:, :], scrB[:, :]), inc="dve")
        grps = None
        if h + 1 < NH:
            gk = feed_tile(wk_h, h + 1, 0, 0)
            cast_of[(h + 1, 0)] = cast_tile(gk, 0)
            grps = list(mm_emitter(0, cast_of[(h + 1, 0)]))
        av = None
        mm_end = wl_rd.get(0, 0)
        for c in range(NCH):
            ce = c % 2
            cs = slice(c * 128, (c + 1) * 128)
            pT1 = ppT1a if ce == 0 else ppT1b
            pT2 = ppT2a if ce == 0 else ppT2b
            p.emit("pe", lambda e, cs=cs, pT1=pT1: e.transpose(pT1[:, :], kp_rot[:, cs], idents[:, :]),
                   waits=[("dve", max(krotp, prev_m2p)), ("act", prev_ktp[ce])])
            t2v = p.emit("pe", lambda e, cs=cs, pT2=pT2: e.transpose(pT2[:, :], vpbuf[:, cs],
                                                                     idents[:, :]),
                         waits=[("act", prev_vtp[ce])], inc="pe")
            if grps is not None:
                mm_end = grps[c]()
            prev_ktp[ce] = p.emit("act", lambda e, h=h, c=c, ce=ce, pT1=pT1: e.activation(
                out=ktp[ce][:, :], in_=pT1[:, :], func=AF.Copy,
                scale=kd2s[:, h * NCH + c: h * NCH + c + 1]),
                waits=[("pe", t2v)], inc="act")
            prev_vtp[ce] = p.emit("act", lambda e, ce=ce, pT2=pT2: e.activation(
                out=vtp[ce][:, :], in_=pT2[:, :], func=AF.Copy),
                waits=[("pe", t2v)], inc="act")
            av = p.emit("pe", lambda e, c=c, ce=ce: e.matmul(ppS7[:, :], ktp[ce][:, :], vtp[ce][:, :],
                                                             start=(c == 0), stop=(c == NCH - 1)),
                        waits=[("act", max(prev_ktp[ce], prev_vtp[ce]))], inc="pe")
        if grps is not None:
            wl_rd[0] = mm_end
        s7_cast = p.emit("act", lambda e, h=h: e.activation(
            out=S7g[:, h * 128:(h + 1) * 128], in_=ppS7[:, :], func=AF.Copy),
            waits=[("pe", av)], inc="act")
    s7_done = s7_cast
    p05_pe_end = p.v("pe")
    p05_dve_end = p.v("dve")
    ctxpc.close()
    stack_pre.close()
    wland = [sa(nc.sbuf_tensor(f"wland{i}", [128, 2048], F32)) for i in range(2)]
    wbf = [sa(nc.sbuf_tensor(f"wbf{i}", [128, 2048], BF16)) for i in range(2)]
    stg = [sa(nc.sbuf_tensor(f"stg{i}", [128, T], BF16)) for i in range(2)]
    bufQ = [sa(nc.sbuf_tensor(f"bufQ{i}", [128, T], BF16)) for i in range(2)]
    bufK = [sa(nc.sbuf_tensor(f"bufK{i}", [128, T], BF16)) for i in range(2)]
    bufV = [sa(nc.sbuf_tensor(f"bufV{i}", [128, T], BF16)) for i in range(2)]
    dmTs = sa(nc.sbuf_tensor("dmTs", [128, NH * 128], F32))
    qdTs = sa(nc.sbuf_tensor("qdTs", [128, NH * 128], F32))
    q_rot = sa(nc.sbuf_tensor("q_rot", [128, T], BF16))
    k_rot = sa(nc.sbuf_tensor("k_rot", [128, T], BF16))
    qt = sa(nc.sbuf_tensor("qt", [128, T], BF16))
    qt2 = sa(nc.sbuf_tensor("qt2", [128, T], BF16))
    ktil = [sa(nc.sbuf_tensor(f"ktil{i}", [128, 128], BF16)) for i in range(2)]
    vtok = [sa(nc.sbuf_tensor(f"vtok{i}", [128, 128], BF16)) for i in range(2)]
    stil = [sa(nc.sbuf_tensor(f"stil{i}", [128, 128], BF16)) for i in range(2)]
    S7stage = sa(nc.sbuf_tensor("S7stage", [128, NH * 128], BF16))
    for dst, srch in ((dmTs, dmT_h), (qdTs, qdT_h)):
        p.dma("sp", dst[:, :], srch.ap(), inc="d_c",
              waits=[("pe", p05_pe_end), ("dve", p05_dve_end), ("act", s7_done)])
    dc_all = p.v("d_c")

    def emit_layernorm(src_sb, dst_sb, first_waits=(), x_dma_gate=False, tag="ln1"):
        ctx = ExitStack()
        p_sx = ctx.enter_context(nc.psum_tensor(f"p_sx_{tag}", [1, T], F32))
        p_sx2 = ctx.enter_context(nc.psum_tensor(f"p_sx2_{tag}", [1, T], F32))
        sq_done = {}
        sx2_done = {}
        for k in range(16):
            dmawait = [("d_x", 16 * (k // 4 + 1))] if x_dma_gate else []
            scr = scrA if k % 2 == 0 else scrB
            sq_done[k] = p.emit(
                "act", lambda e, scr=scr, k=k: e.activation(out=scr[:, :], in_=src_sb[:, k * T:(k + 1) * T],
                                                            func=AF.Square),
                waits=dmawait + [("pe", sx2_done.get(k - 2, 0))] + list(first_waits), inc="act")
            for half in range(2):
                p.emit("pe", lambda e, k=k, half=half: e.matmul(
                    p_sx[:, half * 512:(half + 1) * 512], ones_col[:, :],
                    src_sb[:, k * T + half * 512: k * T + (half + 1) * 512],
                    start=(k == 0), stop=(k == 15)),
                    waits=dmawait + (list(first_waits) + [("dve", ones_ready)] if k == 0 else []))
            for half in range(2):
                sx2_done[k] = p.emit("pe", lambda e, k=k, half=half: e.matmul(
                    p_sx2[:, half * 512:(half + 1) * 512], ones_col[:, :],
                    (scrA if k % 2 == 0 else scrB)[:, half * 512:(half + 1) * 512],
                    start=(k == 0), stop=(k == 15)),
                    waits=[("act", sq_done[k])],
                    inc=("pe" if half == 1 else None))
        stats_pe = sx2_done[15]
        p.emit("dve", lambda e: e.tensor_scalar_mul(st_m[:, :], p_sx[:, :], 1.0 / D),
               waits=[("pe", stats_pe)])
        p.emit("dve", lambda e: e.tensor_scalar_mul(st_a[:, :], p_sx2[:, :], 1.0 / D))
        p.emit("dve", lambda e: e.tensor_mul(st_b[:, :], st_m[:, :], st_m[:, :]))
        v_done = p.emit("dve", lambda e: e.tensor_sub(st_a[:, :], st_a[:, :], st_b[:, :]), inc="dve")
        p.emit("act", lambda e: e.activation(out=st_b[:, :], in_=st_a[:, :], func=AF.Ln, bias=eps_sb[:, :]),
               waits=[("dve", v_done), ("dve", ones_ready)])
        r_done = p.emit("act", lambda e: e.activation(out=st_a[:, :], in_=st_b[:, :], func=AF.Exp,
                                                      scale=-0.5), inc="act")
        ctx.close()
        ctx2 = ExitStack()
        p_bm = ctx2.enter_context(nc.psum_tensor(f"p_bm_{tag}", [128, T], F32))
        p_br = ctx2.enter_context(nc.psum_tensor(f"p_br_{tag}", [128, T], F32))
        bc_done = None
        for half in range(2):
            p.emit("pe", lambda e, half=half: e.matmul(
                p_bm[:, half * 512:(half + 1) * 512], ones_row[:, :],
                st_m[:, half * 512:(half + 1) * 512], start=True, stop=True),
                waits=[("act", r_done), ("dve", ones_ready)])
            bc_done = p.emit("pe", lambda e, half=half: e.matmul(
                p_br[:, half * 512:(half + 1) * 512], ones_row[:, :],
                st_a[:, half * 512:(half + 1) * 512], start=True, stop=True),
                inc=("pe" if half == 1 else None))
        done = None
        for k in range(16):
            p.emit("dve", lambda e, k=k: e.tensor_sub(scrA[:, :], src_sb[:, k * T:(k + 1) * T], p_bm[:, :]),
                   waits=[("pe", bc_done)])
            done = p.emit("dve", lambda e, k=k: e.tensor_mul(dst_sb[:, k * T:(k + 1) * T], scrA[:, :],
                                                             p_br[:, :]),
                          inc=("dve" if k == 15 else None))
        ctx2.close()
        return done


    def _finish_early():
        fw = []
        if p.v("d_tap"):
            fw.append(("d_tap", p.v("d_tap")))
        for nm in list(p.cnt):
            if nm.startswith("d_") and nm != "d_tap":
                fw.append((nm, p.v(nm)))
        p.emit("sp", lambda e: e.nop(), waits=fw)
        stack_ffn_local = None
        p.realize()
        return nc

    ln1_done = emit_layernorm(x_sb, actA, x_dma_gate=True,
                              first_waits=[("pe", p05_pe_end), ("dve", p05_dve_end),
                                           ("act", s7_done)])
    if "xln1" in tap_tensors:
        p.dma("act", tap_tensors["xln1"].ap(), actA[:, :], waits=[("dve", ln1_done)], inc="d_tap")
    if upto == "ln1":
        return _finish_early()

    ctxp2 = ExitStack()
    psumP = [ctxp2.enter_context(nc.psum_tensor(f"psumP{i}", [128, T], F32)) for i in range(2)]
    evac_val = {}
    store_val = {}
    cast_val = {}
    pe_proj = {}
    for widx, (wt, dst) in enumerate(((wq_h, q_hbm), (wk_h, k_hbm), (wv_h, v_hbm))):
        for m in range(16):
            gi = widx * 16 + m
            par = gi % 2
            p.dma("sp", wland[par][:, :], wt.ap()[m],
                  waits=[("dve", cast_val.get(gi - 2, p05_dve_end)),
                         ("pe", p05_pe_end if gi < 2 else 0),
                         ("act", s7_done if gi < 2 else 0)], inc=f"d_w{par}")
            cast_val[gi] = p.emit(
                "dve", lambda e, par=par: e.tensor_copy(wbf[par][:, :], wland[par][:, :]),
                waits=[(f"d_w{par}", 16 * (gi // 2 + 1)), ("dve", ln1_done),
                       ("pe", pe_proj.get(gi - 2, 0))], inc="dve")
            mm_waits = [("dve", cast_val[gi]), ("act", evac_val.get(gi - 2, 0))]
            pe_val = None
            for k in range(16):
                for half in range(2):
                    pe_val = p.emit("pe", lambda e, par=par, k=k, half=half: e.matmul(
                        psumP[par][:, half * 512:(half + 1) * 512],
                        wbf[par][:, k * 128:(k + 1) * 128],
                        actA[:, k * T + half * 512: k * T + (half + 1) * 512],
                        start=(k == 0), stop=(k == 15)),
                        waits=(mm_waits if (k == 0 and half == 0) else ()),
                        inc=("pe" if (k == 15 and half == 1) else None))
            pe_proj[gi] = pe_val
            evac_val[gi] = p.emit(
                "act", lambda e, par=par: e.activation(out=stg[par][:, :], in_=psumP[par][:, :],
                                                       func=AF.Copy),
                waits=[("pe", pe_val), (f"d_s{par}", 16 * store_val.get(gi - 2, 0))], inc="act")
            sv = p.dma("act", dst.ap()[m], stg[par][:, :], inc=f"d_s{par}")
            store_val[gi] = sv // 16
    last_cast_par = {0: cast_val[46], 1: cast_val[47]}
    ctxp2.close()
    if upto == "p2":
        return _finish_early()

    ctxp3 = ExitStack()
    ps_r = ctxp3.enter_context(nc.psum_tensor("ps_r", [128, T], F32))
    tp_T1 = ctxp3.enter_context(nc.psum_tensor("tp_T1", [128, 128], BF16))
    tp_T2 = ctxp3.enter_context(nc.psum_tensor("tp_T2", [128, 128], BF16))
    tp_S = [ctxp3.enter_context(nc.psum_tensor(f"tp_S{i}", [128, 128], F32)) for i in range(2)]
    tp_O = ctxp3.enter_context(nc.psum_tensor("tp_O", [128, 128], F32))
    tp_A = ctxp3.enter_context(nc.psum_tensor("tp_A", [128, 128], F32))

    head_end = {}
    last_ktil = 0
    last_vt = 0
    last_stil = {0: 0, 1: 0}
    last_evacO = 0
    last_kvupd = 0
    last_m2 = 0
    attn_evac_vals = np.zeros((NH, NCH), dtype=np.int64)
    qt2_dma = 0
    prev_head_kvcast = s7_done
    for h in range(NH):
        par = h % 2
        he = head_end.get(h - 2, {"pe": 0, "dve": 0, "act": 0})
        need = [(f"d_s{(h) % 2}", 16 * store_val[h]),
                (f"d_s{(16 + h) % 2}", 16 * store_val[16 + h]),
                (f"d_s{(32 + h) % 2}", 16 * store_val[32 + h]),
                ("pe", he["pe"]), ("dve", he["dve"])]
        p.dma("sp", bufQ[par][:, :], q_hbm.ap()[h], waits=need, inc=f"d_q{par}")
        p.dma("sp", bufK[par][:, :], k_hbm.ap()[h], inc=f"d_k{par}")
        p.dma("sp", bufV[par][:, :], v_hbm.ap()[h], inc=f"d_v{par}")
        dq_target = 16 * (h // 2 + 1)
        rot_w = [(f"d_q{par}", dq_target), (f"d_k{par}", dq_target), (f"d_v{par}", dq_target),
                 ("dve", last_m2), ("d_c", dc_all),
                 ("act", evac_val[47] if h == 0 else 0)]
        p.emit("pe", lambda e, par=par: e.matmul(ps_r[:, 0:512], PTs[:, :], bufQ[par][:, 0:512],
                                                 start=True, stop=True), waits=rot_w)
        pq_done = p.emit("pe", lambda e, par=par: e.matmul(ps_r[:, 512:1024], PTs[:, :],
                                                           bufQ[par][:, 512:1024],
                                                           start=True, stop=True), inc="pe")
        p.emit("dve", lambda e, par=par: e.tensor_mul(scrA[:, :], bufQ[par][:, :], cFs[:, :]),
               waits=[("pe", pq_done)])
        last_m2 = p.emit("dve", lambda e: e.tensor_mul(scrB[:, :], ps_r[:, :], sFs[:, :]), inc="dve")
        p.emit("dve", lambda e: e.tensor_add(q_rot[:, :], scrA[:, :], scrB[:, :]))
        p.emit("dve", lambda e, h=h: e.tensor_mul(
            qt[:, :].rearrange("p (c i) -> p c i", c=NCH),
            q_rot[:, :].rearrange("p (c i) -> p c i", c=NCH),
            qdTs[:, h * 128:(h + 1) * 128].unsqueeze(1).broadcast_to([128, NCH, 128])))
        qt_done = p.v("dve")
        qt2_done = None
        for c in range(NCH):
            qt2_done = p.emit("act", lambda e, h=h, c=c: e.activation(
                out=qt2[:, c * 128:(c + 1) * 128], in_=qt[:, c * 128:(c + 1) * 128],
                func=AF.Copy, scale=rcgs[:, h * NCH + c: h * NCH + c + 1]),
                waits=([("d_qt2", 16 * qt2_dma), ("dve", qt_done)] if c == 0 else ()),
                inc="act")
        dv = p.dma("act", qt2_hbm.ap()[h], qt2[:, :], waits=[("act", qt2_done)], inc="d_qt2")
        qt2_dma = dv // 16
        p.emit("pe", lambda e, par=par: e.matmul(ps_r[:, 0:512], PTs[:, :], bufK[par][:, 0:512],
                                                 start=True, stop=True), waits=[("dve", last_m2)])
        pk_done = p.emit("pe", lambda e, par=par: e.matmul(ps_r[:, 512:1024], PTs[:, :],
                                                           bufK[par][:, 512:1024],
                                                           start=True, stop=True), inc="pe")
        p.emit("dve", lambda e, par=par: e.tensor_mul(scrA[:, :], bufK[par][:, :], cFs[:, :]),
               waits=[("pe", pk_done)])
        last_m2 = p.emit("dve", lambda e: e.tensor_mul(scrB[:, :], ps_r[:, :], sFs[:, :]), inc="dve")
        krot_done = p.emit("dve", lambda e: e.tensor_add(k_rot[:, :], scrA[:, :], scrB[:, :]), inc="dve")
        p.emit("dve", lambda e: e.memset(kvf[:, :], 0.0),
               waits=[("act", prev_head_kvcast)])
        kvz = p.emit("dve", lambda e, h=h: e.memset(S7stage[:, h * 128:(h + 1) * 128], 0.0), inc="dve")
        kvcast_prev = kvz
        for c in range(NCH):
            ce = c % 2
            cs = slice(c * 128, (c + 1) * 128)
            p.emit("pe", lambda e, cs=cs: e.transpose(tp_T1[:, :], k_rot[:, cs], idents[:, :]),
                   waits=[("dve", krot_done), ("act", last_ktil)])
            t2v = p.emit("pe", lambda e, cs=cs, par=par: e.transpose(tp_T2[:, :],
                                                                     bufV[par][:, cs],
                                                                     idents[:, :]),
                         waits=[("act", last_vt)], inc="pe")
            s_done = p.emit("pe", lambda e, ce=ce, cs=cs: e.matmul(tp_S[ce][:, :], k_rot[:, cs],
                                                                   q_rot[:, cs], start=True, stop=True),
                            waits=[("dve", last_stil[ce])], inc="pe")
            last_ktil = p.emit(
                "act", lambda e, ce=ce, h=h: e.activation(out=ktil[ce][:, :], in_=tp_T1[:, :],
                                                          func=AF.Copy, scale=kds[:, h:h + 1]),
                waits=[("pe", t2v)], inc="act")
            last_vt = p.emit(
                "act", lambda e, ce=ce: e.activation(out=vtok[ce][:, :], in_=tp_T2[:, :],
                                                     func=AF.Copy),
                waits=[("pe", t2v)], inc="act")
            last_stil[ce] = p.emit(
                "dve", lambda e, ce=ce, h=h: e.tensor_mul(stil[ce][:, :], tp_S[ce][:, :],
                                                          dmTs[:, h * 128:(h + 1) * 128]),
                waits=[("pe", s_done)], inc="dve")
            p.emit("pe", lambda e, ce=ce: e.matmul(tp_O[:, :], vtok[ce][:, :], stil[ce][:, :],
                                                   start=True, stop=False),
                   waits=[("act", max(last_vt, last_evacO)), ("dve", last_stil[ce])])
            o_done = p.emit("pe", lambda e, h=h, cs=cs: e.matmul(tp_O[:, :],
                                                                 S7stage[:, h * 128:(h + 1) * 128],
                                                                 qt[:, cs], start=False, stop=True),
                            waits=[("act", kvcast_prev if c > 0 else 0), ("dve", kvz)], inc="pe")
            a_done = p.emit("pe", lambda e, ce=ce: e.matmul(tp_A[:, :], ktil[ce][:, :], vtok[ce][:, :],
                                                            start=True, stop=True),
                            waits=[("act", last_ktil), ("dve", last_kvupd)], inc="pe")
            last_evacO = p.emit(
                "act", lambda e, h=h, c=c: e.activation(
                    out=actA[:, h * T + c * 128: h * T + (c + 1) * 128], in_=tp_O[:, :], func=AF.Copy),
                waits=[("pe", o_done)], inc="act")
            attn_evac_vals[h, c] = last_evacO
            last_kvupd = p.emit(
                "dve", lambda e, h=h: e.scalar_tensor_tensor(kvf[:, :], kvf[:, :],
                                                             float(r_heads[h] ** 128), tp_A[:, :],
                                                             ALU.mult, ALU.add),
                waits=[("pe", a_done)], inc="dve")
            kvcast_prev = p.emit(
                "act", lambda e, h=h: e.activation(out=S7stage[:, h * 128:(h + 1) * 128],
                                                   in_=kvf[:, :], func=AF.Copy),
                waits=[("dve", last_kvupd)], inc="act")
        prev_head_kvcast = kvcast_prev
        head_end[h] = {"pe": p.v("pe"), "dve": p.v("dve"), "act": p.v("act")}
    all_kv_act = p.v("act")
    p3_pe_end = p.v("pe")
    p3_dve_end = p.v("dve")
    ctxp3.close()
    if "attnraw" in tap_tensors:
        p.dma("act", tap_tensors["attnraw"].ap(), actA[:, :],
              waits=[("act", all_kv_act), ("dve", p3_dve_end)], inc="d_tap")
    if upto == "p3":
        return _finish_early()

    ctxc = ExitStack()
    ps_cr = [ctxc.enter_context(nc.psum_tensor(f"ps_cr{i}", [128, 128], F32)) for i in range(2)]
    last_corr = {0: 0, 1: 0}
    corr_pe = {}
    for h in range(NH):
        par = h % 2
        p.dma("sp", bufQ[par][:, :], qt2_hbm.ap()[h],
              waits=[("d_qt2", 16 * qt2_dma), ("pe", corr_pe.get(h - 2, p3_pe_end)),
                     ("dve", p3_dve_end if h < 2 else 0)],
              inc=f"d_q{par}")
        dq_now = p.v(f"d_q{par}")
        for cl in range(NCH):
            cp = cl % 2
            mmv = p.emit("pe", lambda e, cp=cp, h=h, cl=cl, par=par: e.matmul(
                ps_cr[cp][:, :], S7g[:, h * 128:(h + 1) * 128],
                bufQ[par][:, cl * 128:(cl + 1) * 128], start=True, stop=True),
                waits=[("act", s7_done), (f"d_q{par}", dq_now), ("dve", last_corr[cp]),
                       ("act", all_kv_act if (h == 0 and cl == 0) else 0)], inc="pe")
            last_corr[cp] = p.emit(
                "dve", lambda e, cp=cp, h=h, cl=cl: e.tensor_add(
                    actA[:, h * T + cl * 128: h * T + (cl + 1) * 128],
                    actA[:, h * T + cl * 128: h * T + (cl + 1) * 128], ps_cr[cp][:, :]),
                waits=[("pe", mmv)], inc="dve")
        corr_pe[h] = p.v("pe")
    corr_done = max(last_corr[0], last_corr[1])
    ctxc.close()
    if "attn" in tap_tensors:
        p.dma("act", tap_tensors["attn"].ap(), actA[:, :], waits=[("dve", corr_done)], inc="d_tap")
    if upto == "corr":
        return _finish_early()

    ctx4 = ExitStack()
    p_ss = ctx4.enter_context(nc.psum_tensor("p_ss", [1, T], F32))
    ss_done = None
    sq_prev = 0
    for k in range(16):
        scr = scrA if k % 2 == 0 else scrB
        sq = p.emit("act", lambda e, scr=scr, k=k: e.activation(
            out=scr[:, :], in_=actA[:, k * T:(k + 1) * T], func=AF.Square),
            waits=[("dve", corr_done), ("pe", sq_prev)], inc="act")
        for half in range(2):
            ss_done = p.emit("pe", lambda e, k=k, half=half: e.matmul(
                p_ss[:, half * 512:(half + 1) * 512], ones_col[:, :],
                (scrA if k % 2 == 0 else scrB)[:, half * 512:(half + 1) * 512],
                start=(k == 0), stop=(k == 15)),
                waits=[("act", sq), ("dve", corr_done if k == 0 else 0),
                       ("dve", ones_ready if k == 0 else 0)],
                inc=("pe" if half == 1 else None))
        sq_prev = ss_done
    p.emit("dve", lambda e: e.tensor_scalar_mul(st_a[:, :], p_ss[:, :], 1.0 / D),
           waits=[("pe", ss_done)], inc="dve")
    p.emit("act", lambda e: e.activation(out=st_b[:, :], in_=st_a[:, :], func=AF.Ln, bias=eps_sb[:, :]),
           waits=[("dve", p.v("dve"))])
    rms_r = p.emit("act", lambda e: e.activation(out=st_a[:, :], in_=st_b[:, :], func=AF.Exp,
                                                 scale=-0.5), inc="act")
    ctx4.close()
    ctx4b = ExitStack()
    p_br2 = ctx4b.enter_context(nc.psum_tensor("p_br2", [128, T], F32))
    rbc = None
    for half in range(2):
        rbc = p.emit("pe", lambda e, half=half: e.matmul(
            p_br2[:, half * 512:(half + 1) * 512], ones_row[:, :],
            st_a[:, half * 512:(half + 1) * 512], start=True, stop=True),
            waits=[("act", rms_r)], inc=("pe" if half == 1 else None))
    norm_done_act = None
    for k in range(16):
        mval = p.emit("dve", lambda e, k=k: e.tensor_mul(scrB[:, :] if k % 2 else scrA[:, :],
                                                         actA[:, k * T:(k + 1) * T], p_br2[:, :]),
                      waits=[("pe", rbc), ("act", norm_done_act if k >= 2 else 0)], inc="dve")
        norm_done_act = p.emit("act", lambda e, k=k: e.activation(
            out=actA[:, k * T:(k + 1) * T], in_=(scrB[:, :] if k % 2 else scrA[:, :]),
            func=AF.Copy, scale=rmsc[:, k:k + 1]),
            waits=[("dve", mval)], inc="act")
    norm_done = p.v("dve")
    ctx4b.close()
    ctx4c = ExitStack()
    psumW = [ctx4c.enter_context(nc.psum_tensor(f"psumW{i}", [128, T], F32)) for i in range(2)]
    res_val = {}
    xp_dma = {}
    cast_val2 = {}
    wo_pe_val = {}
    wo_gen = {}

    def wo_feed(m):
        if m >= 16:
            return
        par = m % 2
        p.dma("sp", wland[par][:, :], wo_h.ap()[m],
              waits=[("dve", cast_val2.get(m - 2, last_cast_par[par]))], inc=f"d_w{par}")
        wo_gen[m] = p.v(f"d_w{par}")

    def wo_cast(m):
        if m >= 16:
            return
        par = m % 2
        cast_val2[m] = p.emit("dve", lambda e, par=par: e.tensor_copy(wbf[par][:, :],
                                                                     wland[par][:, :]),
                              waits=[(f"d_w{par}", wo_gen[m]), ("dve", norm_done),
                                     ("pe", wo_pe_val.get(m - 2, pe_proj[47]))], inc="dve")

    wo_feed(0)
    wo_cast(0)
    wo_feed(1)
    for m in range(16):
        par = m % 2
        wo_pe = None
        for k in range(16):
            for half in range(2):
                wo_pe = p.emit("pe", lambda e, par=par, k=k, half=half: e.matmul(
                    psumW[par][:, half * 512:(half + 1) * 512],
                    wbf[par][:, k * 128:(k + 1) * 128],
                    actA[:, k * T + half * 512: k * T + (half + 1) * 512],
                    start=(k == 0), stop=(k == 15)),
                    waits=([("dve", max(cast_val2[m], res_val.get(m - 2, 0), norm_done)),
                            ("act", norm_done_act)]
                           if (k == 0 and half == 0) else ()),
                    inc=("pe" if (k == 15 and half == 1) else None))
        wo_pe_val[m] = wo_pe
        wo_cast(m + 1)
        wo_feed(m + 2)
        res_val[m] = p.emit("dve", lambda e, m=m, par=par: e.tensor_add(
            x_sb[:, m * T:(m + 1) * T], x_sb[:, m * T:(m + 1) * T], psumW[par][:, :]),
            waits=[("pe", wo_pe), ("d_xp", 16 * xp_dma.get(m - 2, 0))], inc="dve")
        dv = p.dma("act", xp_hbm.ap()[m], x_sb[:, m * T:(m + 1) * T],
                   waits=[("dve", res_val[m])], inc="d_xp")
        xp_dma[m] = dv // 16
    ctx4c.close()
    if "xprime" in tap_tensors:
        p.dma("act", tap_tensors["xprime"].ap(), x_sb[:, :], waits=[("dve", res_val[15])], inc="d_tap")
    if upto == "p4":
        return _finish_early()

    ln2_done = emit_layernorm(x_sb, actA, first_waits=[("dve", res_val[15])], tag="ln2")
    if "xln2" in tap_tensors:
        p.dma("act", tap_tensors["xln2"].ap(), actA[:, :], waits=[("dve", ln2_done)], inc="d_tap")
    if upto == "ln2":
        return _finish_early()

    stack_attn.close()
    stack_ffn = ExitStack()
    sf = stack_ffn.enter_context
    NSLOT = 66
    hid = sf(nc.sbuf_tensor("hid", [128, NSLOT * T], BF16))
    land1 = [sf(nc.sbuf_tensor(f"land1_{i}", [128, 1024], F32)) for i in range(2)]
    wb1 = [sf(nc.sbuf_tensor(f"wb1_{i}", [128, 2048], BF16)) for i in range(2)]
    stack_p6 = ExitStack()
    land3 = [stack_p6.enter_context(nc.sbuf_tensor(f"land3_{i}", [128, 1024], F32)) for i in range(2)]
    wb3 = [stack_p6.enter_context(nc.sbuf_tensor(f"wb3_{i}", [128, 2048], BF16)) for i in range(2)]
    gsil = [stack_p6.enter_context(nc.sbuf_tensor(f"gsil{i}", [128, T], BF16)) for i in range(2)]
    htmp = [stack_p6.enter_context(nc.sbuf_tensor(f"htmp{i}", [128, T], BF16)) for i in range(2)]

    ctx6 = ExitStack()
    pg = [ctx6.enter_context(nc.psum_tensor(f"pg{i}", [128, T], F32)) for i in range(2)]
    pl = [ctx6.enter_context(nc.psum_tensor(f"pl{i}", [128, T], F32)) for i in range(2)]
    morder = [bitrev6(i) for i in range(64)]
    sched = fwht_schedule(morder)
    slot_of = list(range(64))
    free_slots = [64, 65]
    chunk_src = {}
    evacA_val = {}
    silu_val = {}
    mul_val = {}
    cast6_val = {}
    sa_pe_val = {}
    last_bf = 0
    gate_ffn = [("pe", rbc), ("d_s0", 16 * store_val[46]), ("d_s1", 16 * store_val[47]),
                ("d_q0", p.v("d_q0")), ("d_q1", p.v("d_q1")), ("d_qt2", 16 * qt2_dma)]
    w13_pe_val = {}
    cast1_val = {}
    cast3_val = {}

    def emit_w13_dma(mi):
        if mi >= 64:
            return
        mm_ = morder[mi]
        for half in range(2):
            p.dma("sp", land1[half][:, :], w1_h.ap()[mm_][:, half * 1024:(half + 1) * 1024],
                  waits=[("act", cast1_val.get((mi - 1, half), 0))] + (gate_ffn if mi == 0 else []),
                  inc=f"d_f1{half}")
            p.dma("sp", land3[half][:, :], w3_h.ap()[mm_][:, half * 1024:(half + 1) * 1024],
                  waits=[("act", cast3_val.get((mi - 1, half), 0))], inc=f"d_f3{half}")

    def emit_w13_cast(mi):
        if mi >= 64:
            return
        par = mi % 2
        for half in range(2):
            cast1_val[(mi, half)] = p.emit("act", lambda e, par=par, half=half: e.activation(
                out=wb1[par][:, half * 1024:(half + 1) * 1024], in_=land1[half][:, :], func=AF.Copy),
                waits=[(f"d_f1{half}", 16 * (mi + 1)), ("pe", w13_pe_val.get(mi - 2, 0))], inc="act")
            cast3_val[(mi, half)] = p.emit("act", lambda e, par=par, half=half: e.activation(
                out=wb3[par][:, half * 1024:(half + 1) * 1024], in_=land3[half][:, :], func=AF.Copy),
                waits=[(f"d_f3{half}", 16 * (mi + 1))], inc="act")

    emit_w13_dma(0)
    emit_w13_cast(0)
    emit_w13_dma(1)
    for mi in range(64):
        m = morder[mi]
        par = mi % 2
        w1_pe = None
        for k in range(16):
            for half in range(2):
                w1_pe = p.emit("pe", lambda e, par=par, k=k, half=half: e.matmul(
                    pg[par][:, half * 512:(half + 1) * 512],
                    wb1[par][:, k * 128:(k + 1) * 128],
                    actA[:, k * T + half * 512: k * T + (half + 1) * 512],
                    start=(k == 0), stop=(k == 15)),
                    waits=([("act", max(cast3_val[(mi, 1)], evacA_val.get(mi - 2, 0))),
                            ("dve", ln2_done if mi == 0 else 0)]
                           if (k == 0 and half == 0) else ()),
                    inc=("pe" if (k == 15 and half == 1) else None))
        w13_pe = None
        for k in range(16):
            for half in range(2):
                w13_pe = p.emit("pe", lambda e, par=par, k=k, half=half: e.matmul(
                    pl[par][:, half * 512:(half + 1) * 512],
                    wb3[par][:, k * 128:(k + 1) * 128],
                    actA[:, k * T + half * 512: k * T + (half + 1) * 512],
                    start=(k == 0), stop=(k == 15)),
                    waits=([("dve", mul_val.get(mi - 2, 0))] if (k == 0 and half == 0) else ()),
                    inc=("pe" if (k == 15 and half == 1) else None))
        w13_pe_val[mi] = w13_pe
        emit_w13_cast(mi + 1)
        emit_w13_dma(mi + 2)
        silu_val[mi] = p.emit("act", lambda e, par=par: e.activation(out=gsil[par][:, :],
                                                                     in_=pg[par][:, :], func=AF.Silu),
                              waits=[("pe", w1_pe)], inc="act")
        mul_val[mi] = p.emit("dve", lambda e, par=par: e.tensor_mul(htmp[par][:, :], gsil[par][:, :],
                                                                    pl[par][:, :]),
                             waits=[("act", silu_val[mi]), ("pe", w13_pe),
                                    ("pe", sa_pe_val.get(mi - 2, 0))], inc="dve")
        sa_pe = None
        for half in range(2):
            sa_pe = p.emit("pe", lambda e, par=par, half=half: e.matmul(
                pg[par][:, half * 512:(half + 1) * 512], H128s[:, :],
                htmp[par][:, half * 512:(half + 1) * 512], start=True, stop=True),
                waits=([("dve", mul_val[mi]), ("act", silu_val[mi])] if half == 0 else ()),
                inc=("pe" if half == 1 else None))
        sa_pe_val[mi] = sa_pe
        slot = slot_of[m]
        evacA_val[mi] = p.emit("act", lambda e, par=par, slot=slot: e.activation(
            out=hid[:, slot * T:(slot + 1) * T], in_=pg[par][:, :], func=AF.Copy),
            waits=[("pe", sa_pe), ("d_xp", 16 * xp_dma[15] if mi == 0 else 0)], inc="act")
        chunk_src[m] = ("act", evacA_val[mi])
        for (a, b) in sched[mi]:
            sa_slot, sb_slot = slot_of[a], slot_of[b]
            fa = free_slots.pop(0)
            fb = free_slots.pop(0)
            waits = [("act", val) for (kind, val) in (chunk_src[a], chunk_src[b]) if kind == "act"]
            p.emit("dve", lambda e, fa=fa, sa_slot=sa_slot, sb_slot=sb_slot: e.tensor_add(
                hid[:, fa * T:(fa + 1) * T], hid[:, sa_slot * T:(sa_slot + 1) * T],
                hid[:, sb_slot * T:(sb_slot + 1) * T]), waits=waits)
            last_bf = p.emit("dve", lambda e, fb=fb, sa_slot=sa_slot, sb_slot=sb_slot: e.tensor_sub(
                hid[:, fb * T:(fb + 1) * T], hid[:, sa_slot * T:(sa_slot + 1) * T],
                hid[:, sb_slot * T:(sb_slot + 1) * T]), inc="dve")
            free_slots.append(sa_slot)
            free_slots.append(sb_slot)
            slot_of[a], slot_of[b] = fa, fb
            chunk_src[a] = ("dve", last_bf)
            chunk_src[b] = ("dve", last_bf)
    ctx6.close()
    stack_p6.close()
    xpb = [stack_ffn.enter_context(nc.sbuf_tensor(f"xpb{i}", [128, T], F32)) for i in range(4)]
    if "hid" in tap_tensors:
        for c in range(64):
            p.dma("act", tap_tensors["hid"].ap()[c], hid[:, slot_of[c] * T:(slot_of[c] + 1) * T],
                  waits=[("dve", last_bf)], inc="d_tap")
    if upto == "p6":
        return _finish_early()

    ctx8 = ExitStack()
    p8 = [ctx8.enter_context(nc.psum_tensor(f"p8_{i}", [128, T], F32)) for i in range(4)]
    w2cast = {}
    evac8 = {}
    out_dma_val = {}
    blk_dve = {}
    p8_pe_val = {}
    xb_pref = {}
    for mblk in range(4):
        for m4 in range(4):
            mchunk = mblk * 4 + m4
            p.dma("pool", xpb[m4][:, :], xp_hbm.ap()[mchunk],
                  waits=[("d_xp", 16 * xp_dma[15]),
                         ("pe", sa_pe_val[63] if mblk == 0 else 0),
                         ("dve", mul_val[63] if mblk == 0 else 0),
                         ("d_out", 16 * out_dma_val.get((mblk - 1, m4), 0))],
                  inc=f"d_xb{m4}")
            xb_pref[(mblk, m4)] = p.v(f"d_xb{m4}")
        for cg in range(16):
            par = cg % 2
            gi = mblk * 16 + cg
            for half in range(2):
                p.dma("sp", land1[half][:, :].rearrange("p (c m) -> p c m", c=2),
                      w2_h.ap()[mblk, cg * 4 + 2 * half: cg * 4 + 2 * half + 2].transpose([1, 0, 2]),
                      waits=[("act", w2cast.get((gi - 1, half), cast1_val[(63, half)] if gi == 0 else 0))],
                      inc=f"d_f1{half}")
            for half in range(2):
                w2cast[(gi, half)] = p.emit("act", lambda e, par=par, half=half: e.activation(
                    out=wb1[par][:, half * 1024:(half + 1) * 1024], in_=land1[half][:, :],
                    func=AF.Copy, scale=W2SCALE),
                    waits=[(f"d_f1{half}", 16 * (64 + gi + 1)),
                           ("pe", p8_pe_val.get(gi - 2, w13_pe_val[62 + par]))], inc="act")
            for cc in range(4):
                c = cg * 4 + cc
                hs = slot_of[c]
                for m4 in range(4):
                    for half in range(2):
                        p8_pe_val[gi] = p.emit(
                            "pe", lambda e, par=par, cc=cc, m4=m4, half=half, hs=hs, c=c: e.matmul(
                                p8[m4][:, half * 512:(half + 1) * 512],
                                wb1[par][:, cc * 512 + m4 * 128: cc * 512 + (m4 + 1) * 128],
                                hid[:, hs * T + half * 512: hs * T + (half + 1) * 512],
                                start=(c == 0), stop=(c == 63)),
                            waits=(([("act", w2cast[(gi, 1)]),
                                     ("dve", last_bf if gi % 16 == 0 else 0)]
                                    if (cc == 0 and m4 == 0 and half == 0) else []) +
                                   ([("dve", evac8.get((mblk - 1, m4), 0))]
                                    if (cc == 0 and half == 0 and gi % 16 == 0) else [])),
                            inc=("pe" if (cc == 3 and m4 == 3 and half == 1) else None))
        blk_pe = p.v("pe")
        for m4 in range(4):
            mchunk = mblk * 4 + m4
            par = m4 % 2
            ev = p.emit("dve", lambda e, m4=m4: e.tensor_add(
                xpb[m4][:, :], xpb[m4][:, :], p8[m4][:, :]),
                waits=[("pe", blk_pe), (f"d_xb{m4}", xb_pref[(mblk, m4)])], inc="dve")
            evac8[(mblk, m4)] = ev
            dv = p.dma("act", out_h.ap().rearrange("(k p) t -> p k t", p=128)[:, mchunk, :],
                       xpb[m4][:, :], waits=[("dve", ev)], inc="d_out")
            out_dma_val[(mblk, m4)] = dv // 16
        blk_dve[mblk] = p.v("dve")
    ctx8.close()

    final_waits = [("d_out", 16 * out_dma_val[(3, 3)])]
    if p.v("d_tap"):
        final_waits.append(("d_tap", p.v("d_tap")))
    p.emit("sp", lambda e: e.nop(), waits=final_waits)

    stack_ffn.close()
    stack_persist.close()
    p.realize()
    return nc


def host_prep(inputs):
    x = np.asarray(inputs["L0_attn_x"], dtype=np.float32)
    wq_t = tile_lhsT(np.asarray(inputs["L0_attn_Wq"], dtype=np.float32))
    wk_t = tile_lhsT(np.asarray(inputs["L0_attn_Wk"], dtype=np.float32))
    wv_t = tile_lhsT(np.asarray(inputs["L0_attn_Wv"], dtype=np.float32))
    wo_t = tile_lhsT(np.asarray(inputs["L0_attn_Wo"], dtype=np.float32))
    w1_t = tile_lhsT(np.ascontiguousarray(np.asarray(inputs["L0_ffn_W1"], dtype=np.float32).T))
    w3_t = tile_lhsT(np.ascontiguousarray(np.asarray(inputs["L0_ffn_W3"], dtype=np.float32).T))
    w2T = np.ascontiguousarray(np.asarray(inputs["L0_ffn_W2"], dtype=np.float32).T)
    w2_t = np.ascontiguousarray(w2T.reshape(64, 128, 4, 512).transpose(2, 0, 1, 3))
    rms_cols = np.ascontiguousarray(np.asarray(inputs["L0_attn_rms_w"], dtype=np.float32).reshape(16, 128).T)
    in_maps = []
    zeros_xp = np.zeros((D, T), dtype=np.float32)
    for core in range(8):
        b, half = core // 2, core % 2
        xT = np.ascontiguousarray(x[b, half * T:(half + 1) * T, :].T)
        if half == 1:
            xp2 = np.ascontiguousarray(x[b, 0:T, :].T)
        else:
            xp2 = zeros_xp
        m = {"x": xT, "xp2": xp2, "wq_t": wq_t, "wk_t": wk_t, "wv_t": wv_t, "wo_t": wo_t,
             "w1_t": w1_t, "w3_t": w3_t, "w2_t": w2_t, "rms_cols": rms_cols}
        m.update(host_constants(core))
        in_maps.append(m)
    return in_maps


def kernel(**inputs):
    in_maps = host_prep(inputs)
    nc = build_program()
    res = run_bass_kernel_spmd(nc, in_maps, core_ids=list(range(8)))
    out = np.empty((4, 2048, D), dtype=np.float32)
    for core in range(8):
        b, half = core // 2, core % 2
        out[b, half * T:(half + 1) * T, :] = res.results[core]["out"].T
    return out


# version 15
# speedup vs baseline: 1.0357x; 1.0041x over previous
import math
import numpy as np

import concourse.bass as bass
import concourse.mybir as mybir
from concourse.bass_utils import run_bass_kernel_spmd

F32 = mybir.dt.float32
BF16 = mybir.dt.bfloat16
AF = mybir.ActivationFunctionType
ALU = mybir.AluOpType

D = 2048
NH = 16
T = 1024
NCH = 8
C = 128
NHC = 64
EPS = 1e-5
W2SCALE = 1.0 / math.sqrt(8192.0)

ENGS = ("pe", "act", "dve", "pool", "sp")


def _slopes():
    start = 2.0 ** (-(2.0 ** (-(math.log2(NH) - 3))))
    s = np.array([start * start ** i for i in range(NH)], dtype=np.float64)
    return np.exp(-s)


def _popcount_parity(a):
    v = a.copy()
    out = np.zeros_like(v)
    while v.any():
        out ^= (v & 1)
        v >>= 1
    return out


def host_constants(core_id):
    import ml_dtypes
    r = _slopes()
    t0 = (core_id % 2) * T
    t0p = (1 - core_id % 2) * T
    freq = (1.0 / 1024.0) ** np.linspace(0.0, 1.0, 32)
    t = np.arange(t0, t0 + T, dtype=np.float64)
    theta = t[None, :] * freq[:, None]
    cF = np.ones((128, T), dtype=np.float64)
    sF = np.zeros((128, T), dtype=np.float64)
    cF[0:32] = np.cos(theta); cF[64:96] = np.cos(theta)
    sF[0:32] = np.sin(theta); sF[64:96] = np.sin(theta)
    tp = np.arange(t0p, t0p + T, dtype=np.float64)
    thetap = tp[None, :] * freq[:, None]
    cFp = np.ones((128, T), dtype=np.float64)
    sFp = np.zeros((128, T), dtype=np.float64)
    cFp[0:32] = np.cos(thetap); cFp[64:96] = np.cos(thetap)
    sFp[0:32] = np.sin(thetap); sFp[64:96] = np.sin(thetap)
    P = np.zeros((128, 128), dtype=np.float64)
    for d in range(64):
        P[d, d + 64] = 1.0
        P[d + 64, d] = -1.0
    PT = P.T.copy()
    idx = np.arange(C, dtype=np.float64)
    diff = idx[None, :] - idx[:, None]
    dmaskT = np.zeros((C, NH * C))
    qdecT = np.zeros((C, NH * C))
    kd = np.zeros((C, NH))
    for h in range(NH):
        dmaskT[:, h * C:(h + 1) * C] = np.where(diff >= 0, r[h] ** np.maximum(diff, 0.0), 0.0)
        qdecT[:, h * C:(h + 1) * C] = (r[h] ** (idx + 1.0))[None, :]
        kd[:, h] = r[h] ** (C - 1.0 - idx)
    kd2 = np.zeros((C, NH * NCH))
    for h in range(NH):
        for c in range(NCH):
            kd2[:, h * NCH + c] = (r[h] ** (C - 1.0 - idx)) * (r[h] ** (C * (NCH - 1 - c)))
    gate = 1.0 if (core_id % 2) == 1 else 0.0
    rcg = np.zeros((C, NH * NCH))
    for h in range(NH):
        for cl in range(NCH):
            rcg[:, h * NCH + cl] = (r[h] ** (C * cl)) * gate
    i = np.arange(128)
    H128 = np.where(_popcount_parity(i[:, None] & i[None, :]) == 0, 1.0, -1.0)
    ident = np.eye(128)
    return {
        "cF": cF.astype(np.float32), "sF": sF.astype(np.float32),
        "cFp": cFp.astype(np.float32), "sFp": sFp.astype(np.float32),
        "PT": PT.astype(ml_dtypes.bfloat16), "dmaskT": dmaskT.astype(np.float32),
        "qdecT": qdecT.astype(np.float32), "kd": kd.astype(np.float32),
        "kd2": kd2.astype(np.float32),
        "rcg": rcg.astype(np.float32),
        "H128": H128.astype(ml_dtypes.bfloat16), "identity": ident.astype(ml_dtypes.bfloat16),
    }


def tile_lhsT(W):
    K, M = W.shape
    nk, nm = K // 128, M // 128
    t = W.reshape(nk, 128, nm, 128).transpose(2, 1, 0, 3)
    return np.ascontiguousarray(t.reshape(nm, 128, nk * 128))


def bitrev6(x):
    r = 0
    for b in range(6):
        r |= ((x >> b) & 1) << (5 - b)
    return r


def fwht_schedule(order):
    bits = [32, 16, 8, 4, 2, 1]
    lev = [-1] * 64
    sched = [[] for _ in range(64)]
    for mi, m in enumerate(order):
        lev[m] = 0
        changed = True
        while changed:
            changed = False
            for l in range(6):
                b = bits[l]
                for c in range(64):
                    if (c & b) == 0 and lev[c] == l and lev[c | b] == l:
                        sched[mi].append((c, c | b))
                        lev[c] = l + 1
                        lev[c | b] = l + 1
                        changed = True
    assert all(v == 6 for v in lev)
    return sched


class Plan:

    STRICT = ("dve", "act", "pool")

    def __init__(self, nc):
        self.nc = nc
        self.ops = {e: [] for e in ENGS}
        self.sems = {}
        self.cnt = {}
        self.hw = {e: {} for e in ENGS}
        for e in self.STRICT:
            self.s(e)

    def s(self, name):
        if name not in self.sems:
            self.sems[name] = self.nc.alloc_semaphore(name)
            self.cnt[name] = 0
        return name

    def v(self, name):
        return self.cnt.get(name, 0)

    def emit(self, eng, fn, waits=(), inc=None, amt=1):
        real_waits = []
        for (n, val) in waits:
            if val is None or val <= 0:
                continue
            self.s(n)
            if self.hw[eng].get(n, 0) >= val:
                continue
            self.hw[eng][n] = val
            real_waits.append((self.sems[n], val))
        target = None
        dma_sem = None
        if eng in self.STRICT and (inc is None or inc == eng):
            self.cnt[eng] += 1
            target = self.cnt[eng]
            self.ops[eng].append((tuple(real_waits), fn, self.sems[eng], 1, True))
            return target
        if inc is not None:
            self.s(inc)
            self.cnt[inc] += amt
            target = self.cnt[inc]
            dma_sem = self.sems[inc]
        self.ops[eng].append((tuple(real_waits), fn, dma_sem, amt, False))
        return target

    def dma(self, eng, out, in_, waits=(), inc=None):
        w = list(waits)
        if inc is not None and self.v(inc) > 0:
            w.append((inc, self.v(inc)))
        if eng in self.STRICT:
            w.append((eng, self.v(eng)))
        return self.emit(eng, (lambda e, o=out, i=in_: e.dma_start(out=o, in_=i)),
                         w, inc, 16)

    def realize(self):
        nc = self.nc
        plan = self

        def replay(stream):
            def f(e):
                selfsem = plan.sems.get(stream) if stream in plan.STRICT else None
                self_idx = 0
                for (waits, fn, sem, amt, is_self) in plan.ops[stream]:
                    for (s, v) in waits:
                        e.wait_ge(s, v)
                    if is_self and self_idx > 0:
                        e.wait_ge(selfsem, self_idx)
                    inst = fn(e)
                    if sem is not None:
                        inst.then_inc(sem, amt)
                    if is_self:
                        self_idx += 1
            return f

        with nc.Block() as blk:
            blk.tensor(replay("pe"))
            blk.scalar(replay("act"))
            blk.vector(replay("dve"))
            blk.gpsimd(replay("pool"))
            blk.sync(replay("sp"))


def build_program(taps=(), use_collective=True, upto="all"):
    from contextlib import ExitStack
    nc = bass.Bass("TRN2", target_bir_lowering=False)
    p = Plan(nc)
    r_heads = _slopes()

    x_h = nc.dram_tensor("x", [D, T], F32, kind="ExternalInput")
    wq_h = nc.dram_tensor("wq_t", [16, 128, 2048], F32, kind="ExternalInput")
    wk_h = nc.dram_tensor("wk_t", [16, 128, 2048], F32, kind="ExternalInput")
    wv_h = nc.dram_tensor("wv_t", [16, 128, 2048], F32, kind="ExternalInput")
    wo_h = nc.dram_tensor("wo_t", [16, 128, 2048], F32, kind="ExternalInput")
    w1_h = nc.dram_tensor("w1_t", [64, 128, 2048], F32, kind="ExternalInput")
    w3_h = nc.dram_tensor("w3_t", [64, 128, 2048], F32, kind="ExternalInput")
    w2_h = nc.dram_tensor("w2_t", [4, 64, 128, 512], F32, kind="ExternalInput")
    rmsc_h = nc.dram_tensor("rms_cols", [128, 16], F32, kind="ExternalInput")
    cF_h = nc.dram_tensor("cF", [128, T], F32, kind="ExternalInput")
    sF_h = nc.dram_tensor("sF", [128, T], F32, kind="ExternalInput")
    xp2_h = nc.dram_tensor("xp2", [D, T], F32, kind="ExternalInput")
    cFp_h = nc.dram_tensor("cFp", [128, T], F32, kind="ExternalInput")
    sFp_h = nc.dram_tensor("sFp", [128, T], F32, kind="ExternalInput")
    PT_h = nc.dram_tensor("PT", [128, 128], BF16, kind="ExternalInput")
    dmT_h = nc.dram_tensor("dmaskT", [128, NH * 128], F32, kind="ExternalInput")
    qdT_h = nc.dram_tensor("qdecT", [128, NH * 128], F32, kind="ExternalInput")
    kd_h = nc.dram_tensor("kd", [128, NH], F32, kind="ExternalInput")
    kd2_h = nc.dram_tensor("kd2", [128, NH * NCH], F32, kind="ExternalInput")
    rcg_h = nc.dram_tensor("rcg", [128, NH * NCH], F32, kind="ExternalInput")
    H128_h = nc.dram_tensor("H128", [128, 128], BF16, kind="ExternalInput")
    id_h = nc.dram_tensor("identity", [128, 128], BF16, kind="ExternalInput")
    out_h = nc.dram_tensor("out", [D, T], F32, kind="ExternalOutput")

    def scratch(name, shape, dtype):
        kind = "ExternalOutput" if name in taps else "Internal"
        return nc.dram_tensor(name, shape, dtype, kind=kind)

    q_hbm = scratch("q_s", [NH, 128, T], BF16)
    k_hbm = scratch("k_s", [NH, 128, T], BF16)
    v_hbm = scratch("v_s", [NH, 128, T], BF16)
    qt2_hbm = scratch("qt2_s", [NH, 128, T], BF16)
    xp_hbm = scratch("xp_s", [16, 128, T], F32)

    tap_tensors = {}
    for tname in taps:
        if tname == "xln1":
            tap_tensors["xln1"] = nc.dram_tensor("tap_xln1", [128, 16 * T], BF16, kind="ExternalOutput")
        if tname == "attn":
            tap_tensors["attn"] = nc.dram_tensor("tap_attn", [128, 16 * T], BF16, kind="ExternalOutput")
        if tname == "attnraw":
            tap_tensors["attnraw"] = nc.dram_tensor("tap_attnraw", [128, 16 * T], BF16, kind="ExternalOutput")
        if tname == "xprime":
            tap_tensors["xprime"] = nc.dram_tensor("tap_xprime", [128, 16 * T], F32, kind="ExternalOutput")
        if tname == "xln2":
            tap_tensors["xln2"] = nc.dram_tensor("tap_xln2", [128, 16 * T], BF16, kind="ExternalOutput")
        if tname == "hid":
            tap_tensors["hid"] = nc.dram_tensor("tap_hid", [NHC, 128, T], BF16, kind="ExternalOutput")

    stack_persist = ExitStack()
    stack_attn = ExitStack()

    sb = stack_persist.enter_context
    actA = sb(nc.sbuf_tensor("actA", [128, 16 * T], BF16))
    eps_sb = sb(nc.sbuf_tensor("eps_sb", [1, 1], F32))
    ones_col = sb(nc.sbuf_tensor("ones_col", [128, 1], F32))
    ones_row = sb(nc.sbuf_tensor("ones_row", [1, 128], F32))
    H128s = sb(nc.sbuf_tensor("H128s", [128, 128], BF16))

    sa = stack_attn.enter_context
    idents = sa(nc.sbuf_tensor("idents", [128, 128], BF16))
    rmsc = sa(nc.sbuf_tensor("rmsc", [128, 16], F32))
    st_a = sa(nc.sbuf_tensor("st_a", [1, T], F32))
    st_b = sa(nc.sbuf_tensor("st_b", [1, T], F32))
    st_m = sa(nc.sbuf_tensor("st_m", [1, T], F32))
    x_sb = sa(nc.sbuf_tensor("x_sb", [128, 16 * T], F32))
    scrA = sa(nc.sbuf_tensor("scrA", [128, T], F32))
    scrB = sa(nc.sbuf_tensor("scrB", [128, T], F32))
    cFs = sa(nc.sbuf_tensor("cFs", [128, T], F32))
    sFs = sa(nc.sbuf_tensor("sFs", [128, T], F32))
    PTs = sa(nc.sbuf_tensor("PTs", [128, 128], BF16))
    kds = sa(nc.sbuf_tensor("kds", [128, NH], F32))
    rcgs = sa(nc.sbuf_tensor("rcgs", [128, NH * NCH], F32))
    kvf = sa(nc.sbuf_tensor("kvf", [128, 128], F32))
    S7g = sa(nc.sbuf_tensor("S7g", [128, NH * 128], BF16))
    stack_pre = ExitStack()
    sp_ = stack_pre.enter_context
    xlnp = sp_(nc.sbuf_tensor("xlnp", [128, 16 * T], BF16))
    preland = [sp_(nc.sbuf_tensor(f"preland{i}", [128, T], F32)) for i in range(2)]
    wbfp = [sp_(nc.sbuf_tensor(f"wbfp{i}", [128, 2048], BF16)) for i in range(2)]
    kpbuf = sp_(nc.sbuf_tensor("kpbuf", [128, T], BF16))
    kp_rot = sp_(nc.sbuf_tensor("kp_rot", [128, T], BF16))
    vpbuf = sp_(nc.sbuf_tensor("vpbuf", [128, T], BF16))
    cFp = sp_(nc.sbuf_tensor("cFp_s", [128, T], F32))
    sFp = sp_(nc.sbuf_tensor("sFp_s", [128, T], F32))
    kd2s = sp_(nc.sbuf_tensor("kd2s", [128, NH * NCH], F32))
    ktp = [sp_(nc.sbuf_tensor(f"ktp{i}", [128, 128], BF16)) for i in range(2)]
    vtp = [sp_(nc.sbuf_tensor(f"vtp{i}", [128, 128], BF16)) for i in range(2)]

    x_ap3 = x_h.ap().rearrange("(k p) t -> p k t", p=128)
    xs_ap3 = x_sb[:, :].rearrange("p (k t) -> p k t", t=T)
    for g in range(4):
        p.dma("sp", xs_ap3[:, g * 4:(g + 1) * 4, :], x_ap3[:, g * 4:(g + 1) * 4, :], inc="d_x")
    for dst, src in ((cFs, cF_h), (sFs, sF_h), (PTs, PT_h),
                     (kds, kd_h), (rcgs, rcg_h), (H128s, H128_h), (idents, id_h), (rmsc, rmsc_h),
                     (cFp, cFp_h), (sFp, sFp_h), (kd2s, kd2_h)):
        p.dma("sp", dst[:, :], src.ap(), inc="d_c")
    dc_all = p.v("d_c")
    p.emit("dve", lambda e: e.memset(eps_sb[:, :], EPS), inc="dve")
    p.emit("dve", lambda e: e.memset(ones_col[:, :], 1.0), inc="dve")
    ones_ready = p.emit("dve", lambda e: e.memset(ones_row[:, :], 1.0), inc="dve")


    xp2_ap3 = xp2_h.ap().rearrange("(k p) t -> p k t", p=128)
    ctxpa = ExitStack()
    pp_sx = ctxpa.enter_context(nc.psum_tensor("pp_sx", [1, T], F32))
    pp_sx2 = ctxpa.enter_context(nc.psum_tensor("pp_sx2", [1, T], F32))
    sq_d = {}
    sx_d = {}
    sx2_d = {}
    for k in range(16):
        par = k % 2
        p.dma("sp", preland[par][:, :], xp2_ap3[:, k, :],
              waits=[("act", sq_d.get(k - 2, 0)), ("pe", sx_d.get(k - 2, 0))],
              inc=f"d_p{par}")
        dw = [(f"d_p{par}", 16 * (k // 2 + 1))]
        scr = scrA if par == 0 else scrB
        sq_d[k] = p.emit("act", lambda e, par=par, scr=scr: e.activation(
            out=scr[:, :], in_=preland[par][:, :], func=AF.Square),
            waits=dw + [("pe", sx2_d.get(k - 2, 0))], inc="act")
        for half in range(2):
            sx_d[k] = p.emit("pe", lambda e, par=par, k=k, half=half: e.matmul(
                pp_sx[:, half * 512:(half + 1) * 512], ones_col[:, :],
                preland[par][:, half * 512:(half + 1) * 512], start=(k == 0), stop=(k == 15)),
                waits=dw + [("dve", ones_ready)], inc=("pe" if half == 1 else None))
        for half in range(2):
            sx2_d[k] = p.emit("pe", lambda e, par=par, k=k, half=half: e.matmul(
                pp_sx2[:, half * 512:(half + 1) * 512], ones_col[:, :],
                (scrA if par == 0 else scrB)[:, half * 512:(half + 1) * 512],
                start=(k == 0), stop=(k == 15)),
                waits=[("act", sq_d[k])], inc=("pe" if half == 1 else None))
    p.emit("dve", lambda e: e.tensor_scalar_mul(st_m[:, :], pp_sx[:, :], 1.0 / D),
           waits=[("pe", sx2_d[15])])
    p.emit("dve", lambda e: e.tensor_scalar_mul(st_a[:, :], pp_sx2[:, :], 1.0 / D))
    p.emit("dve", lambda e: e.tensor_mul(st_b[:, :], st_m[:, :], st_m[:, :]))
    vp_done = p.emit("dve", lambda e: e.tensor_sub(st_a[:, :], st_a[:, :], st_b[:, :]), inc="dve")
    p.emit("act", lambda e: e.activation(out=st_b[:, :], in_=st_a[:, :], func=AF.Ln, bias=eps_sb[:, :]),
           waits=[("dve", vp_done)])
    rp_done = p.emit("act", lambda e: e.activation(out=st_a[:, :], in_=st_b[:, :], func=AF.Exp,
                                                   scale=-0.5), inc="act")
    ctxpa.close()
    ctxpb = ExitStack()
    pp_bm = ctxpb.enter_context(nc.psum_tensor("pp_bm", [128, T], F32))
    pp_br = ctxpb.enter_context(nc.psum_tensor("pp_br", [128, T], F32))
    bcp = None
    for half in range(2):
        p.emit("pe", lambda e, half=half: e.matmul(
            pp_bm[:, half * 512:(half + 1) * 512], ones_row[:, :],
            st_m[:, half * 512:(half + 1) * 512], start=True, stop=True),
            waits=[("act", rp_done)])
        bcp = p.emit("pe", lambda e, half=half: e.matmul(
            pp_br[:, half * 512:(half + 1) * 512], ones_row[:, :],
            st_a[:, half * 512:(half + 1) * 512], start=True, stop=True),
            inc=("pe" if half == 1 else None))
    nrm_d = {}
    for k in range(16):
        par = k % 2
        p.dma("sp", preland[par][:, :], xp2_ap3[:, k, :],
              waits=[("dve", nrm_d.get(k - 2, 0)), ("act", sq_d[15] if k < 2 else 0),
                     ("pe", sx_d[15] if k < 2 else 0)],
              inc=f"d_p{par}")
        dwv = [(f"d_p{par}", 16 * (8 + k // 2 + 1)), ("pe", bcp)]
        p.emit("dve", lambda e, par=par: e.tensor_sub(scrA[:, :], preland[par][:, :], pp_bm[:, :]),
               waits=dwv)
        nrm_d[k] = p.emit("dve", lambda e, k=k: e.tensor_mul(
            xlnp[:, k * T:(k + 1) * T], scrA[:, :], pp_br[:, :]), inc="dve")
    nrm_done = nrm_d[15]
    ctxpb.close()
    ctxpc = ExitStack()
    ppj = ctxpc.enter_context(nc.psum_tensor("ppj", [128, T], F32))
    pp_rot = ctxpc.enter_context(nc.psum_tensor("pp_rot", [128, T], F32))
    ppT1a = ctxpc.enter_context(nc.psum_tensor("ppT1a", [128, 128], BF16))
    ppT2a = ctxpc.enter_context(nc.psum_tensor("ppT2a", [128, 128], BF16))
    ppS7 = ctxpc.enter_context(nc.psum_tensor("ppS7", [128, 128], F32))
    ppT1b = pp_rot[:, 0:64].bitcast(BF16)
    ppT2b = pp_rot[:, 512:576].bitcast(BF16)
    wl_rd = {}
    last_ev = 0
    prev_cast = 0
    prev_m2p = 0
    prev_ktp = {0: 0, 1: 0}
    prev_vtp = {0: 0, 1: 0}
    s7_cast = 0
    dp_cnt = 32
    cast_of = {}

    def feed_tile(wt_h, h, wpar, wsel):
        nonlocal prev_cast, dp_cnt
        for half in range(2):
            p.dma("sp", preland[half][:, :], wt_h.ap()[h][:, half * 1024:(half + 1) * 1024],
                  waits=[("dve", prev_cast), ("dve", nrm_done)], inc=f"d_p{half}")
        gen = dp_cnt // 2 + 1
        dp_cnt += 2
        return gen

    def cast_tile(gen, wpar):
        nonlocal prev_cast
        for half in range(2):
            prev_cast = p.emit("dve", lambda e, half=half, wpar=wpar: e.tensor_copy(
                wbfp[wpar][:, half * 1024:(half + 1) * 1024], preland[half][:, :]),
                waits=[(f"d_p{half}", 16 * gen), ("pe", wl_rd.get(wpar, 0))], inc="dve")
        return prev_cast

    def mm_emitter(wpar, cval):
        for j0 in range(0, 32, 4):
            def grp(j0=j0, wpar=wpar, cval=cval):
                mmv = None
                for j in range(j0, j0 + 4):
                    k, half = j // 2, j % 2
                    mmv = p.emit("pe", lambda e, k=k, half=half, wpar=wpar: e.matmul(
                        ppj[:, half * 512:(half + 1) * 512],
                        wbfp[wpar][:, k * 128:(k + 1) * 128],
                        xlnp[:, k * T + half * 512: k * T + (half + 1) * 512],
                        start=(k == 0), stop=(k == 15)),
                        waits=([("dve", max(cval, nrm_done)), ("act", last_ev)]
                               if j == j0 == 0 else ()),
                        inc=("pe" if j == 31 else None))
                return mmv
            yield grp

    g0 = feed_tile(wk_h, 0, 0, 0)
    cast_of[(0, 0)] = cast_tile(g0, 0)
    mm_end = None
    for grp in mm_emitter(0, cast_of[(0, 0)]):
        mm_end = grp()
    wl_rd[0] = mm_end
    for h in range(NH):
        ev_kp = p.emit("act", lambda e: e.activation(out=kpbuf[:, :], in_=ppj[:, :], func=AF.Copy),
                       waits=[("pe", wl_rd[0])], inc="act")
        last_ev = ev_kp
        gv = feed_tile(wv_h, h, 1, 1)
        cast_of[(h, 1)] = cast_tile(gv, 1)
        mm_end = None
        for grp in mm_emitter(1, cast_of[(h, 1)]):
            mm_end = grp()
        wl_rd[1] = mm_end
        ev_vp = p.emit("act", lambda e: e.activation(out=vpbuf[:, :], in_=ppj[:, :], func=AF.Copy),
                       waits=[("pe", wl_rd[1])], inc="act")
        last_ev = ev_vp
        p.emit("pe", lambda e: e.matmul(pp_rot[:, 0:512], PTs[:, :], kpbuf[:, 0:512],
                                        start=True, stop=True),
               waits=[("act", ev_vp), ("dve", prev_m2p), ("d_c", dc_all)])
        prot = p.emit("pe", lambda e: e.matmul(pp_rot[:, 512:1024], PTs[:, :], kpbuf[:, 512:1024],
                                               start=True, stop=True), inc="pe")
        p.emit("dve", lambda e: e.tensor_mul(scrA[:, :], kpbuf[:, :], cFp[:, :]),
               waits=[("pe", prot)])
        prev_m2p = p.emit("dve", lambda e: e.tensor_mul(scrB[:, :], pp_rot[:, :], sFp[:, :]), inc="dve")
        krotp = p.emit("dve", lambda e: e.tensor_add(kp_rot[:, :], scrA[:, :], scrB[:, :]), inc="dve")
        grps = None
        if h + 1 < NH:
            gk = feed_tile(wk_h, h + 1, 0, 0)
            cast_of[(h + 1, 0)] = cast_tile(gk, 0)
            grps = list(mm_emitter(0, cast_of[(h + 1, 0)]))
        av = None
        mm_end = wl_rd.get(0, 0)
        for c in range(NCH):
            ce = c % 2
            cs = slice(c * 128, (c + 1) * 128)
            pT1 = ppT1a if ce == 0 else ppT1b
            pT2 = ppT2a if ce == 0 else ppT2b
            p.emit("pe", lambda e, cs=cs, pT1=pT1: e.transpose(pT1[:, :], kp_rot[:, cs], idents[:, :]),
                   waits=[("dve", max(krotp, prev_m2p)), ("act", prev_ktp[ce])])
            t2v = p.emit("pe", lambda e, cs=cs, pT2=pT2: e.transpose(pT2[:, :], vpbuf[:, cs],
                                                                     idents[:, :]),
                         waits=[("act", prev_vtp[ce])], inc="pe")
            if grps is not None:
                mm_end = grps[c]()
            prev_ktp[ce] = p.emit("act", lambda e, h=h, c=c, ce=ce, pT1=pT1: e.activation(
                out=ktp[ce][:, :], in_=pT1[:, :], func=AF.Copy,
                scale=kd2s[:, h * NCH + c: h * NCH + c + 1]),
                waits=[("pe", t2v)], inc="act")
            prev_vtp[ce] = p.emit("act", lambda e, ce=ce, pT2=pT2: e.activation(
                out=vtp[ce][:, :], in_=pT2[:, :], func=AF.Copy),
                waits=[("pe", t2v)], inc="act")
            av = p.emit("pe", lambda e, c=c, ce=ce: e.matmul(ppS7[:, :], ktp[ce][:, :], vtp[ce][:, :],
                                                             start=(c == 0), stop=(c == NCH - 1)),
                        waits=[("act", max(prev_ktp[ce], prev_vtp[ce]))], inc="pe")
        if grps is not None:
            wl_rd[0] = mm_end
        s7_cast = p.emit("act", lambda e, h=h: e.activation(
            out=S7g[:, h * 128:(h + 1) * 128], in_=ppS7[:, :], func=AF.Copy),
            waits=[("pe", av)], inc="act")
    s7_done = s7_cast
    p05_pe_end = p.v("pe")
    p05_dve_end = p.v("dve")
    ctxpc.close()
    stack_pre.close()
    wland = [sa(nc.sbuf_tensor(f"wland{i}", [128, 2048], F32)) for i in range(2)]
    wbf = [sa(nc.sbuf_tensor(f"wbf{i}", [128, 2048], BF16)) for i in range(2)]
    stg = [sa(nc.sbuf_tensor(f"stg{i}", [128, T], BF16)) for i in range(2)]
    bufQ = [sa(nc.sbuf_tensor(f"bufQ{i}", [128, T], BF16)) for i in range(2)]
    bufK = [sa(nc.sbuf_tensor(f"bufK{i}", [128, T], BF16)) for i in range(2)]
    bufV = [sa(nc.sbuf_tensor(f"bufV{i}", [128, T], BF16)) for i in range(2)]
    dmTs = sa(nc.sbuf_tensor("dmTs", [128, NH * 128], F32))
    qdTs = sa(nc.sbuf_tensor("qdTs", [128, NH * 128], F32))
    q_rot = sa(nc.sbuf_tensor("q_rot", [128, T], BF16))
    k_rot = sa(nc.sbuf_tensor("k_rot", [128, T], BF16))
    qt = sa(nc.sbuf_tensor("qt", [128, T], BF16))
    qt2 = sa(nc.sbuf_tensor("qt2", [128, T], BF16))
    ktil = [sa(nc.sbuf_tensor(f"ktil{i}", [128, 128], BF16)) for i in range(2)]
    vtok = [sa(nc.sbuf_tensor(f"vtok{i}", [128, 128], BF16)) for i in range(2)]
    stil = [sa(nc.sbuf_tensor(f"stil{i}", [128, 128], BF16)) for i in range(2)]
    S7stage = sa(nc.sbuf_tensor("S7stage", [128, NH * 128], BF16))
    for dst, srch in ((dmTs, dmT_h), (qdTs, qdT_h)):
        p.dma("sp", dst[:, :], srch.ap(), inc="d_c",
              waits=[("pe", p05_pe_end), ("dve", p05_dve_end), ("act", s7_done)])
    dc_all = p.v("d_c")

    def emit_layernorm(src_sb, dst_sb, first_waits=(), x_dma_gate=False, tag="ln1"):
        ctx = ExitStack()
        p_sx = ctx.enter_context(nc.psum_tensor(f"p_sx_{tag}", [1, T], F32))
        p_sx2 = ctx.enter_context(nc.psum_tensor(f"p_sx2_{tag}", [1, T], F32))
        sq_done = {}
        sx2_done = {}
        for k in range(16):
            dmawait = [("d_x", 16 * (k // 4 + 1))] if x_dma_gate else []
            scr = scrA if k % 2 == 0 else scrB
            sq_done[k] = p.emit(
                "act", lambda e, scr=scr, k=k: e.activation(out=scr[:, :], in_=src_sb[:, k * T:(k + 1) * T],
                                                            func=AF.Square),
                waits=dmawait + [("pe", sx2_done.get(k - 2, 0))] + list(first_waits), inc="act")
            for half in range(2):
                p.emit("pe", lambda e, k=k, half=half: e.matmul(
                    p_sx[:, half * 512:(half + 1) * 512], ones_col[:, :],
                    src_sb[:, k * T + half * 512: k * T + (half + 1) * 512],
                    start=(k == 0), stop=(k == 15)),
                    waits=dmawait + (list(first_waits) + [("dve", ones_ready)] if k == 0 else []))
            for half in range(2):
                sx2_done[k] = p.emit("pe", lambda e, k=k, half=half: e.matmul(
                    p_sx2[:, half * 512:(half + 1) * 512], ones_col[:, :],
                    (scrA if k % 2 == 0 else scrB)[:, half * 512:(half + 1) * 512],
                    start=(k == 0), stop=(k == 15)),
                    waits=[("act", sq_done[k])],
                    inc=("pe" if half == 1 else None))
        stats_pe = sx2_done[15]
        p.emit("dve", lambda e: e.tensor_scalar_mul(st_m[:, :], p_sx[:, :], 1.0 / D),
               waits=[("pe", stats_pe)])
        p.emit("dve", lambda e: e.tensor_scalar_mul(st_a[:, :], p_sx2[:, :], 1.0 / D))
        p.emit("dve", lambda e: e.tensor_mul(st_b[:, :], st_m[:, :], st_m[:, :]))
        v_done = p.emit("dve", lambda e: e.tensor_sub(st_a[:, :], st_a[:, :], st_b[:, :]), inc="dve")
        p.emit("act", lambda e: e.activation(out=st_b[:, :], in_=st_a[:, :], func=AF.Ln, bias=eps_sb[:, :]),
               waits=[("dve", v_done), ("dve", ones_ready)])
        r_done = p.emit("act", lambda e: e.activation(out=st_a[:, :], in_=st_b[:, :], func=AF.Exp,
                                                      scale=-0.5), inc="act")
        ctx.close()
        ctx2 = ExitStack()
        p_bm = ctx2.enter_context(nc.psum_tensor(f"p_bm_{tag}", [128, T], F32))
        p_br = ctx2.enter_context(nc.psum_tensor(f"p_br_{tag}", [128, T], F32))
        bc_done = None
        for half in range(2):
            p.emit("pe", lambda e, half=half: e.matmul(
                p_bm[:, half * 512:(half + 1) * 512], ones_row[:, :],
                st_m[:, half * 512:(half + 1) * 512], start=True, stop=True),
                waits=[("act", r_done), ("dve", ones_ready)])
            bc_done = p.emit("pe", lambda e, half=half: e.matmul(
                p_br[:, half * 512:(half + 1) * 512], ones_row[:, :],
                st_a[:, half * 512:(half + 1) * 512], start=True, stop=True),
                inc=("pe" if half == 1 else None))
        done = None
        for k in range(16):
            p.emit("dve", lambda e, k=k: e.tensor_sub(scrA[:, :], src_sb[:, k * T:(k + 1) * T], p_bm[:, :]),
                   waits=[("pe", bc_done)])
            done = p.emit("dve", lambda e, k=k: e.tensor_mul(dst_sb[:, k * T:(k + 1) * T], scrA[:, :],
                                                             p_br[:, :]),
                          inc=("dve" if k == 15 else None))
        ctx2.close()
        return done


    def _finish_early():
        fw = []
        if p.v("d_tap"):
            fw.append(("d_tap", p.v("d_tap")))
        for nm in list(p.cnt):
            if nm.startswith("d_") and nm != "d_tap":
                fw.append((nm, p.v(nm)))
        p.emit("sp", lambda e: e.nop(), waits=fw)
        stack_ffn_local = None
        p.realize()
        return nc

    ln1_done = emit_layernorm(x_sb, actA, x_dma_gate=True,
                              first_waits=[("pe", p05_pe_end), ("dve", p05_dve_end),
                                           ("act", s7_done)])
    if "xln1" in tap_tensors:
        p.dma("act", tap_tensors["xln1"].ap(), actA[:, :], waits=[("dve", ln1_done)], inc="d_tap")
    if upto == "ln1":
        return _finish_early()

    ctxp2 = ExitStack()
    psumP = [ctxp2.enter_context(nc.psum_tensor(f"psumP{i}", [128, T], F32)) for i in range(2)]
    evac_val = {}
    store_val = {}
    cast_val = {}
    pe_proj = {}
    for widx, (wt, dst) in enumerate(((wq_h, q_hbm), (wk_h, k_hbm), (wv_h, v_hbm))):
        for m in range(16):
            gi = widx * 16 + m
            par = gi % 2
            p.dma("sp", wland[par][:, :], wt.ap()[m],
                  waits=[("dve", cast_val.get(gi - 2, p05_dve_end)),
                         ("pe", p05_pe_end if gi < 2 else 0),
                         ("act", s7_done if gi < 2 else 0)], inc=f"d_w{par}")
            cast_val[gi] = p.emit(
                "dve", lambda e, par=par: e.tensor_copy(wbf[par][:, :], wland[par][:, :]),
                waits=[(f"d_w{par}", 16 * (gi // 2 + 1)), ("dve", ln1_done),
                       ("pe", pe_proj.get(gi - 2, 0))], inc="dve")
            mm_waits = [("dve", cast_val[gi]), ("act", evac_val.get(gi - 2, 0))]
            pe_val = None
            for k in range(16):
                for half in range(2):
                    pe_val = p.emit("pe", lambda e, par=par, k=k, half=half: e.matmul(
                        psumP[par][:, half * 512:(half + 1) * 512],
                        wbf[par][:, k * 128:(k + 1) * 128],
                        actA[:, k * T + half * 512: k * T + (half + 1) * 512],
                        start=(k == 0), stop=(k == 15)),
                        waits=(mm_waits if (k == 0 and half == 0) else ()),
                        inc=("pe" if (k == 15 and half == 1) else None))
            pe_proj[gi] = pe_val
            evac_val[gi] = p.emit(
                "act", lambda e, par=par: e.activation(out=stg[par][:, :], in_=psumP[par][:, :],
                                                       func=AF.Copy),
                waits=[("pe", pe_val), (f"d_s{par}", 16 * store_val.get(gi - 2, 0))], inc="act")
            sv = p.dma("act", dst.ap()[m], stg[par][:, :], inc=f"d_s{par}")
            store_val[gi] = sv // 16
    last_cast_par = {0: cast_val[46], 1: cast_val[47]}
    ctxp2.close()
    if upto == "p2":
        return _finish_early()

    ctxp3 = ExitStack()
    ps_r = ctxp3.enter_context(nc.psum_tensor("ps_r", [128, T], F32))
    tp_T1 = ctxp3.enter_context(nc.psum_tensor("tp_T1", [128, 128], BF16))
    tp_T2 = ctxp3.enter_context(nc.psum_tensor("tp_T2", [128, 128], BF16))
    tp_S = [ctxp3.enter_context(nc.psum_tensor(f"tp_S{i}", [128, 128], F32)) for i in range(2)]
    tp_O = ctxp3.enter_context(nc.psum_tensor("tp_O", [128, 128], F32))
    tp_A = ctxp3.enter_context(nc.psum_tensor("tp_A", [128, 128], F32))

    head_end = {}
    last_ktil = 0
    last_vt = 0
    last_stil = {0: 0, 1: 0}
    last_evacO = 0
    last_kvupd = 0
    last_m2 = 0
    attn_evac_vals = np.zeros((NH, NCH), dtype=np.int64)
    qt2_dma = 0
    qt2_prev = 0
    prev_head_kvcast = s7_done
    for h in range(NH):
        par = h % 2
        he = head_end.get(h - 2, {"pe": 0, "dve": 0, "act": 0})
        need = [(f"d_s{(h) % 2}", 16 * store_val[h]),
                (f"d_s{(16 + h) % 2}", 16 * store_val[16 + h]),
                (f"d_s{(32 + h) % 2}", 16 * store_val[32 + h]),
                ("pe", he["pe"]), ("dve", he["dve"])]
        p.dma("sp", bufQ[par][:, :], q_hbm.ap()[h], waits=need, inc=f"d_q{par}")
        p.dma("sp", bufK[par][:, :], k_hbm.ap()[h], inc=f"d_k{par}")
        p.dma("sp", bufV[par][:, :], v_hbm.ap()[h], inc=f"d_v{par}")
        dq_target = 16 * (h // 2 + 1)
        rot_w = [(f"d_q{par}", dq_target), (f"d_k{par}", dq_target), (f"d_v{par}", dq_target),
                 ("dve", last_m2), ("d_c", dc_all),
                 ("act", evac_val[47] if h == 0 else 0)]
        p.emit("pe", lambda e, par=par: e.matmul(ps_r[:, 0:512], PTs[:, :], bufQ[par][:, 0:512],
                                                 start=True, stop=True), waits=rot_w)
        pq_done = p.emit("pe", lambda e, par=par: e.matmul(ps_r[:, 512:1024], PTs[:, :],
                                                           bufQ[par][:, 512:1024],
                                                           start=True, stop=True), inc="pe")
        p.emit("dve", lambda e, par=par: e.tensor_mul(scrA[:, :], bufQ[par][:, :], cFs[:, :]),
               waits=[("pe", pq_done)])
        last_m2 = p.emit("dve", lambda e: e.tensor_mul(scrB[:, :], ps_r[:, :], sFs[:, :]), inc="dve")
        p.emit("dve", lambda e: e.tensor_add(q_rot[:, :], scrA[:, :], scrB[:, :]))
        p.emit("dve", lambda e, h=h: e.tensor_mul(
            qt[:, :].rearrange("p (c i) -> p c i", c=NCH),
            q_rot[:, :].rearrange("p (c i) -> p c i", c=NCH),
            qdTs[:, h * 128:(h + 1) * 128].unsqueeze(1).broadcast_to([128, NCH, 128])),
            waits=[("act", qt2_prev)])
        qt_done = p.v("dve")
        p.emit("pe", lambda e, par=par: e.matmul(ps_r[:, 0:512], PTs[:, :], bufK[par][:, 0:512],
                                                 start=True, stop=True), waits=[("dve", last_m2)])
        pk_done = p.emit("pe", lambda e, par=par: e.matmul(ps_r[:, 512:1024], PTs[:, :],
                                                           bufK[par][:, 512:1024],
                                                           start=True, stop=True), inc="pe")
        p.emit("dve", lambda e, par=par: e.tensor_mul(scrA[:, :], bufK[par][:, :], cFs[:, :]),
               waits=[("pe", pk_done)])
        last_m2 = p.emit("dve", lambda e: e.tensor_mul(scrB[:, :], ps_r[:, :], sFs[:, :]), inc="dve")
        krot_done = p.emit("dve", lambda e: e.tensor_add(k_rot[:, :], scrA[:, :], scrB[:, :]), inc="dve")
        p.emit("dve", lambda e: e.memset(kvf[:, :], 0.0),
               waits=[("act", prev_head_kvcast)])
        kvz = p.emit("dve", lambda e, h=h: e.memset(S7stage[:, h * 128:(h + 1) * 128], 0.0), inc="dve")
        kvcast_prev = kvz
        for c in range(NCH):
            ce = c % 2
            cs = slice(c * 128, (c + 1) * 128)
            p.emit("pe", lambda e, cs=cs: e.transpose(tp_T1[:, :], k_rot[:, cs], idents[:, :]),
                   waits=[("dve", krot_done), ("act", last_ktil)])
            t2v = p.emit("pe", lambda e, cs=cs, par=par: e.transpose(tp_T2[:, :],
                                                                     bufV[par][:, cs],
                                                                     idents[:, :]),
                         waits=[("act", last_vt)], inc="pe")
            s_done = p.emit("pe", lambda e, ce=ce, cs=cs: e.matmul(tp_S[ce][:, :], k_rot[:, cs],
                                                                   q_rot[:, cs], start=True, stop=True),
                            waits=[("dve", last_stil[ce])], inc="pe")
            last_ktil = p.emit(
                "act", lambda e, ce=ce, h=h: e.activation(out=ktil[ce][:, :], in_=tp_T1[:, :],
                                                          func=AF.Copy, scale=kds[:, h:h + 1]),
                waits=[("pe", t2v)], inc="act")
            last_vt = p.emit(
                "act", lambda e, ce=ce: e.activation(out=vtok[ce][:, :], in_=tp_T2[:, :],
                                                     func=AF.Copy),
                waits=[("pe", t2v)], inc="act")
            last_stil[ce] = p.emit(
                "dve", lambda e, ce=ce, h=h: e.tensor_mul(stil[ce][:, :], tp_S[ce][:, :],
                                                          dmTs[:, h * 128:(h + 1) * 128]),
                waits=[("pe", s_done)], inc="dve")
            p.emit("pe", lambda e, ce=ce: e.matmul(tp_O[:, :], vtok[ce][:, :], stil[ce][:, :],
                                                   start=True, stop=False),
                   waits=[("act", max(last_vt, last_evacO)), ("dve", last_stil[ce])])
            o_done = p.emit("pe", lambda e, h=h, cs=cs: e.matmul(tp_O[:, :],
                                                                 S7stage[:, h * 128:(h + 1) * 128],
                                                                 qt[:, cs], start=False, stop=True),
                            waits=[("act", kvcast_prev if c > 0 else 0), ("dve", kvz)], inc="pe")
            a_done = p.emit("pe", lambda e, ce=ce: e.matmul(tp_A[:, :], ktil[ce][:, :], vtok[ce][:, :],
                                                            start=True, stop=True),
                            waits=[("act", last_ktil), ("dve", last_kvupd)], inc="pe")
            last_evacO = p.emit(
                "act", lambda e, h=h, c=c: e.activation(
                    out=actA[:, h * T + c * 128: h * T + (c + 1) * 128], in_=tp_O[:, :], func=AF.Copy),
                waits=[("pe", o_done)], inc="act")
            attn_evac_vals[h, c] = last_evacO
            last_kvupd = p.emit(
                "dve", lambda e, h=h: e.scalar_tensor_tensor(kvf[:, :], kvf[:, :],
                                                             float(r_heads[h] ** 128), tp_A[:, :],
                                                             ALU.mult, ALU.add),
                waits=[("pe", a_done)], inc="dve")
            kvcast_prev = p.emit(
                "act", lambda e, h=h: e.activation(out=S7stage[:, h * 128:(h + 1) * 128],
                                                   in_=kvf[:, :], func=AF.Copy),
                waits=[("dve", last_kvupd)], inc="act")
        prev_head_kvcast = kvcast_prev
        qt2_done = None
        for c in range(NCH):
            qt2_done = p.emit("act", lambda e, h=h, c=c: e.activation(
                out=qt2[:, c * 128:(c + 1) * 128], in_=qt[:, c * 128:(c + 1) * 128],
                func=AF.Copy, scale=rcgs[:, h * NCH + c: h * NCH + c + 1]),
                waits=([("d_qt2", 16 * qt2_dma), ("dve", qt_done)] if c == 0 else ()),
                inc="act")
        dv = p.dma("act", qt2_hbm.ap()[h], qt2[:, :], waits=[("act", qt2_done)], inc="d_qt2")
        qt2_dma = dv // 16
        qt2_prev = qt2_done
        head_end[h] = {"pe": p.v("pe"), "dve": p.v("dve"), "act": p.v("act")}
    all_kv_act = p.v("act")
    p3_pe_end = p.v("pe")
    p3_dve_end = p.v("dve")
    ctxp3.close()
    if "attnraw" in tap_tensors:
        p.dma("act", tap_tensors["attnraw"].ap(), actA[:, :],
              waits=[("act", all_kv_act), ("dve", p3_dve_end)], inc="d_tap")
    if upto == "p3":
        return _finish_early()

    ctxc = ExitStack()
    ps_cr = [ctxc.enter_context(nc.psum_tensor(f"ps_cr{i}", [128, 128], F32)) for i in range(2)]
    last_corr = {0: 0, 1: 0}
    corr_pe = {}
    for h in range(NH):
        par = h % 2
        p.dma("sp", bufQ[par][:, :], qt2_hbm.ap()[h],
              waits=[("d_qt2", 16 * qt2_dma), ("pe", corr_pe.get(h - 2, p3_pe_end)),
                     ("dve", p3_dve_end if h < 2 else 0)],
              inc=f"d_q{par}")
        dq_now = p.v(f"d_q{par}")
        for cl in range(NCH):
            cp = cl % 2
            mmv = p.emit("pe", lambda e, cp=cp, h=h, cl=cl, par=par: e.matmul(
                ps_cr[cp][:, :], S7g[:, h * 128:(h + 1) * 128],
                bufQ[par][:, cl * 128:(cl + 1) * 128], start=True, stop=True),
                waits=[("act", s7_done), (f"d_q{par}", dq_now), ("dve", last_corr[cp]),
                       ("act", all_kv_act if (h == 0 and cl == 0) else 0)], inc="pe")
            last_corr[cp] = p.emit(
                "dve", lambda e, cp=cp, h=h, cl=cl: e.tensor_add(
                    actA[:, h * T + cl * 128: h * T + (cl + 1) * 128],
                    actA[:, h * T + cl * 128: h * T + (cl + 1) * 128], ps_cr[cp][:, :]),
                waits=[("pe", mmv)], inc="dve")
        corr_pe[h] = p.v("pe")
    corr_done = max(last_corr[0], last_corr[1])
    ctxc.close()
    if "attn" in tap_tensors:
        p.dma("act", tap_tensors["attn"].ap(), actA[:, :], waits=[("dve", corr_done)], inc="d_tap")
    if upto == "corr":
        return _finish_early()

    ctx4 = ExitStack()
    p_ss = ctx4.enter_context(nc.psum_tensor("p_ss", [1, T], F32))
    ss_done = None
    sq_prev = 0
    for k in range(16):
        scr = scrA if k % 2 == 0 else scrB
        sq = p.emit("act", lambda e, scr=scr, k=k: e.activation(
            out=scr[:, :], in_=actA[:, k * T:(k + 1) * T], func=AF.Square),
            waits=[("dve", corr_done), ("pe", sq_prev)], inc="act")
        for half in range(2):
            ss_done = p.emit("pe", lambda e, k=k, half=half: e.matmul(
                p_ss[:, half * 512:(half + 1) * 512], ones_col[:, :],
                (scrA if k % 2 == 0 else scrB)[:, half * 512:(half + 1) * 512],
                start=(k == 0), stop=(k == 15)),
                waits=[("act", sq), ("dve", corr_done if k == 0 else 0),
                       ("dve", ones_ready if k == 0 else 0)],
                inc=("pe" if half == 1 else None))
        sq_prev = ss_done
    p.emit("dve", lambda e: e.tensor_scalar_mul(st_a[:, :], p_ss[:, :], 1.0 / D),
           waits=[("pe", ss_done)], inc="dve")
    p.emit("act", lambda e: e.activation(out=st_b[:, :], in_=st_a[:, :], func=AF.Ln, bias=eps_sb[:, :]),
           waits=[("dve", p.v("dve"))])
    rms_r = p.emit("act", lambda e: e.activation(out=st_a[:, :], in_=st_b[:, :], func=AF.Exp,
                                                 scale=-0.5), inc="act")
    ctx4.close()
    ctx4b = ExitStack()
    p_br2 = ctx4b.enter_context(nc.psum_tensor("p_br2", [128, T], F32))
    rbc = None
    for half in range(2):
        rbc = p.emit("pe", lambda e, half=half: e.matmul(
            p_br2[:, half * 512:(half + 1) * 512], ones_row[:, :],
            st_a[:, half * 512:(half + 1) * 512], start=True, stop=True),
            waits=[("act", rms_r)], inc=("pe" if half == 1 else None))
    norm_done_act = None
    for k in range(16):
        mval = p.emit("dve", lambda e, k=k: e.tensor_mul(scrB[:, :] if k % 2 else scrA[:, :],
                                                         actA[:, k * T:(k + 1) * T], p_br2[:, :]),
                      waits=[("pe", rbc), ("act", norm_done_act if k >= 2 else 0)], inc="dve")
        norm_done_act = p.emit("act", lambda e, k=k: e.activation(
            out=actA[:, k * T:(k + 1) * T], in_=(scrB[:, :] if k % 2 else scrA[:, :]),
            func=AF.Copy, scale=rmsc[:, k:k + 1]),
            waits=[("dve", mval)], inc="act")
    norm_done = p.v("dve")
    ctx4b.close()
    ctx4c = ExitStack()
    psumW = [ctx4c.enter_context(nc.psum_tensor(f"psumW{i}", [128, T], F32)) for i in range(2)]
    res_val = {}
    xp_dma = {}
    cast_val2 = {}
    wo_pe_val = {}
    wo_gen = {}

    def wo_feed(m):
        if m >= 16:
            return
        par = m % 2
        p.dma("sp", wland[par][:, :], wo_h.ap()[m],
              waits=[("dve", cast_val2.get(m - 2, last_cast_par[par]))], inc=f"d_w{par}")
        wo_gen[m] = p.v(f"d_w{par}")

    def wo_cast(m):
        if m >= 16:
            return
        par = m % 2
        cast_val2[m] = p.emit("dve", lambda e, par=par: e.tensor_copy(wbf[par][:, :],
                                                                     wland[par][:, :]),
                              waits=[(f"d_w{par}", wo_gen[m]), ("dve", norm_done),
                                     ("pe", wo_pe_val.get(m - 2, pe_proj[47]))], inc="dve")

    wo_feed(0)
    wo_cast(0)
    wo_feed(1)
    for m in range(16):
        par = m % 2
        wo_pe = None
        for k in range(16):
            for half in range(2):
                wo_pe = p.emit("pe", lambda e, par=par, k=k, half=half: e.matmul(
                    psumW[par][:, half * 512:(half + 1) * 512],
                    wbf[par][:, k * 128:(k + 1) * 128],
                    actA[:, k * T + half * 512: k * T + (half + 1) * 512],
                    start=(k == 0), stop=(k == 15)),
                    waits=([("dve", max(cast_val2[m], res_val.get(m - 2, 0), norm_done)),
                            ("act", norm_done_act)]
                           if (k == 0 and half == 0) else ()),
                    inc=("pe" if (k == 15 and half == 1) else None))
        wo_pe_val[m] = wo_pe
        wo_cast(m + 1)
        wo_feed(m + 2)
        res_val[m] = p.emit("dve", lambda e, m=m, par=par: e.tensor_add(
            x_sb[:, m * T:(m + 1) * T], x_sb[:, m * T:(m + 1) * T], psumW[par][:, :]),
            waits=[("pe", wo_pe), ("d_xp", 16 * xp_dma.get(m - 2, 0))], inc="dve")
        dv = p.dma("act", xp_hbm.ap()[m], x_sb[:, m * T:(m + 1) * T],
                   waits=[("dve", res_val[m])], inc="d_xp")
        xp_dma[m] = dv // 16
    ctx4c.close()
    if "xprime" in tap_tensors:
        p.dma("act", tap_tensors["xprime"].ap(), x_sb[:, :], waits=[("dve", res_val[15])], inc="d_tap")
    if upto == "p4":
        return _finish_early()

    ln2_done = emit_layernorm(x_sb, actA, first_waits=[("dve", res_val[15])], tag="ln2")
    if "xln2" in tap_tensors:
        p.dma("act", tap_tensors["xln2"].ap(), actA[:, :], waits=[("dve", ln2_done)], inc="d_tap")
    if upto == "ln2":
        return _finish_early()

    stack_attn.close()
    stack_ffn = ExitStack()
    sf = stack_ffn.enter_context
    NSLOT = 66
    hid = sf(nc.sbuf_tensor("hid", [128, NSLOT * T], BF16))
    land1 = [sf(nc.sbuf_tensor(f"land1_{i}", [128, 1024], F32)) for i in range(2)]
    wb1 = [sf(nc.sbuf_tensor(f"wb1_{i}", [128, 2048], BF16)) for i in range(2)]
    stack_p6 = ExitStack()
    land3 = [stack_p6.enter_context(nc.sbuf_tensor(f"land3_{i}", [128, 1024], F32)) for i in range(2)]
    wb3 = [stack_p6.enter_context(nc.sbuf_tensor(f"wb3_{i}", [128, 2048], BF16)) for i in range(2)]
    gsil = [stack_p6.enter_context(nc.sbuf_tensor(f"gsil{i}", [128, T], BF16)) for i in range(2)]
    htmp = [stack_p6.enter_context(nc.sbuf_tensor(f"htmp{i}", [128, T], BF16)) for i in range(2)]

    ctx6 = ExitStack()
    pg = [ctx6.enter_context(nc.psum_tensor(f"pg{i}", [128, T], F32)) for i in range(2)]
    pl = [ctx6.enter_context(nc.psum_tensor(f"pl{i}", [128, T], F32)) for i in range(2)]
    morder = [bitrev6(i) for i in range(64)]
    sched = fwht_schedule(morder)
    slot_of = list(range(64))
    free_slots = [64, 65]
    chunk_src = {}
    evacA_val = {}
    silu_val = {}
    mul_val = {}
    cast6_val = {}
    sa_pe_val = {}
    last_bf = 0
    gate_ffn = [("pe", rbc), ("d_s0", 16 * store_val[46]), ("d_s1", 16 * store_val[47]),
                ("d_q0", p.v("d_q0")), ("d_q1", p.v("d_q1")), ("d_qt2", 16 * qt2_dma)]
    w13_pe_val = {}
    cast1_val = {}
    cast3_val = {}

    def emit_w13_dma(mi):
        if mi >= 64:
            return
        mm_ = morder[mi]
        for half in range(2):
            p.dma("sp", land1[half][:, :], w1_h.ap()[mm_][:, half * 1024:(half + 1) * 1024],
                  waits=[("act", cast1_val.get((mi - 1, half), 0))] + (gate_ffn if mi == 0 else []),
                  inc=f"d_f1{half}")
            p.dma("sp", land3[half][:, :], w3_h.ap()[mm_][:, half * 1024:(half + 1) * 1024],
                  waits=[("act", cast3_val.get((mi - 1, half), 0))], inc=f"d_f3{half}")

    def emit_w13_cast(mi):
        if mi >= 64:
            return
        par = mi % 2
        for half in range(2):
            cast1_val[(mi, half)] = p.emit("act", lambda e, par=par, half=half: e.activation(
                out=wb1[par][:, half * 1024:(half + 1) * 1024], in_=land1[half][:, :], func=AF.Copy),
                waits=[(f"d_f1{half}", 16 * (mi + 1)), ("pe", w13_pe_val.get(mi - 2, 0))], inc="act")
            cast3_val[(mi, half)] = p.emit("act", lambda e, par=par, half=half: e.activation(
                out=wb3[par][:, half * 1024:(half + 1) * 1024], in_=land3[half][:, :], func=AF.Copy),
                waits=[(f"d_f3{half}", 16 * (mi + 1))], inc="act")

    emit_w13_dma(0)
    emit_w13_cast(0)
    emit_w13_dma(1)
    for mi in range(64):
        m = morder[mi]
        par = mi % 2
        w1_pe = None
        for k in range(16):
            for half in range(2):
                w1_pe = p.emit("pe", lambda e, par=par, k=k, half=half: e.matmul(
                    pg[par][:, half * 512:(half + 1) * 512],
                    wb1[par][:, k * 128:(k + 1) * 128],
                    actA[:, k * T + half * 512: k * T + (half + 1) * 512],
                    start=(k == 0), stop=(k == 15)),
                    waits=([("act", max(cast3_val[(mi, 1)], evacA_val.get(mi - 2, 0))),
                            ("dve", ln2_done if mi == 0 else 0)]
                           if (k == 0 and half == 0) else ()),
                    inc=("pe" if (k == 15 and half == 1) else None))
        w13_pe = None
        for k in range(16):
            for half in range(2):
                w13_pe = p.emit("pe", lambda e, par=par, k=k, half=half: e.matmul(
                    pl[par][:, half * 512:(half + 1) * 512],
                    wb3[par][:, k * 128:(k + 1) * 128],
                    actA[:, k * T + half * 512: k * T + (half + 1) * 512],
                    start=(k == 0), stop=(k == 15)),
                    waits=([("dve", mul_val.get(mi - 2, 0))] if (k == 0 and half == 0) else ()),
                    inc=("pe" if (k == 15 and half == 1) else None))
        w13_pe_val[mi] = w13_pe
        emit_w13_cast(mi + 1)
        emit_w13_dma(mi + 2)
        silu_val[mi] = p.emit("act", lambda e, par=par: e.activation(out=gsil[par][:, :],
                                                                     in_=pg[par][:, :], func=AF.Silu),
                              waits=[("pe", w1_pe)], inc="act")
        mul_val[mi] = p.emit("dve", lambda e, par=par: e.tensor_mul(htmp[par][:, :], gsil[par][:, :],
                                                                    pl[par][:, :]),
                             waits=[("act", silu_val[mi]), ("pe", w13_pe),
                                    ("pe", sa_pe_val.get(mi - 2, 0))], inc="dve")
        sa_pe = None
        for half in range(2):
            sa_pe = p.emit("pe", lambda e, par=par, half=half: e.matmul(
                pg[par][:, half * 512:(half + 1) * 512], H128s[:, :],
                htmp[par][:, half * 512:(half + 1) * 512], start=True, stop=True),
                waits=([("dve", mul_val[mi]), ("act", silu_val[mi])] if half == 0 else ()),
                inc=("pe" if half == 1 else None))
        sa_pe_val[mi] = sa_pe
        slot = slot_of[m]
        evacA_val[mi] = p.emit("act", lambda e, par=par, slot=slot: e.activation(
            out=hid[:, slot * T:(slot + 1) * T], in_=pg[par][:, :], func=AF.Copy),
            waits=[("pe", sa_pe), ("d_xp", 16 * xp_dma[15] if mi == 0 else 0)], inc="act")
        chunk_src[m] = ("act", evacA_val[mi])
        for (a, b) in sched[mi]:
            sa_slot, sb_slot = slot_of[a], slot_of[b]
            fa = free_slots.pop(0)
            fb = free_slots.pop(0)
            waits = [("act", val) for (kind, val) in (chunk_src[a], chunk_src[b]) if kind == "act"]
            p.emit("dve", lambda e, fa=fa, sa_slot=sa_slot, sb_slot=sb_slot: e.tensor_add(
                hid[:, fa * T:(fa + 1) * T], hid[:, sa_slot * T:(sa_slot + 1) * T],
                hid[:, sb_slot * T:(sb_slot + 1) * T]), waits=waits)
            last_bf = p.emit("dve", lambda e, fb=fb, sa_slot=sa_slot, sb_slot=sb_slot: e.tensor_sub(
                hid[:, fb * T:(fb + 1) * T], hid[:, sa_slot * T:(sa_slot + 1) * T],
                hid[:, sb_slot * T:(sb_slot + 1) * T]), inc="dve")
            free_slots.append(sa_slot)
            free_slots.append(sb_slot)
            slot_of[a], slot_of[b] = fa, fb
            chunk_src[a] = ("dve", last_bf)
            chunk_src[b] = ("dve", last_bf)
    ctx6.close()
    stack_p6.close()
    xpb = [stack_ffn.enter_context(nc.sbuf_tensor(f"xpb{i}", [128, T], F32)) for i in range(4)]
    if "hid" in tap_tensors:
        for c in range(64):
            p.dma("act", tap_tensors["hid"].ap()[c], hid[:, slot_of[c] * T:(slot_of[c] + 1) * T],
                  waits=[("dve", last_bf)], inc="d_tap")
    if upto == "p6":
        return _finish_early()

    ctx8 = ExitStack()
    p8 = [ctx8.enter_context(nc.psum_tensor(f"p8_{i}", [128, T], F32)) for i in range(4)]
    w2cast = {}
    evac8 = {}
    out_dma_val = {}
    blk_dve = {}
    p8_pe_val = {}
    xb_pref = {}
    for mblk in range(4):
        for m4 in range(4):
            mchunk = mblk * 4 + m4
            p.dma("pool", xpb[m4][:, :], xp_hbm.ap()[mchunk],
                  waits=[("d_xp", 16 * xp_dma[15]),
                         ("pe", sa_pe_val[63] if mblk == 0 else 0),
                         ("dve", mul_val[63] if mblk == 0 else 0),
                         ("d_out", 16 * out_dma_val.get((mblk - 1, m4), 0))],
                  inc=f"d_xb{m4}")
            xb_pref[(mblk, m4)] = p.v(f"d_xb{m4}")
        for cg in range(16):
            par = cg % 2
            gi = mblk * 16 + cg
            for half in range(2):
                p.dma("sp", land1[half][:, :].rearrange("p (c m) -> p c m", c=2),
                      w2_h.ap()[mblk, cg * 4 + 2 * half: cg * 4 + 2 * half + 2].transpose([1, 0, 2]),
                      waits=[("act", w2cast.get((gi - 1, half), cast1_val[(63, half)] if gi == 0 else 0))],
                      inc=f"d_f1{half}")
            for half in range(2):
                w2cast[(gi, half)] = p.emit("act", lambda e, par=par, half=half: e.activation(
                    out=wb1[par][:, half * 1024:(half + 1) * 1024], in_=land1[half][:, :],
                    func=AF.Copy, scale=W2SCALE),
                    waits=[(f"d_f1{half}", 16 * (64 + gi + 1)),
                           ("pe", p8_pe_val.get(gi - 2, w13_pe_val[62 + par]))], inc="act")
            for cc in range(4):
                c = cg * 4 + cc
                hs = slot_of[c]
                for m4 in range(4):
                    for half in range(2):
                        p8_pe_val[gi] = p.emit(
                            "pe", lambda e, par=par, cc=cc, m4=m4, half=half, hs=hs, c=c: e.matmul(
                                p8[m4][:, half * 512:(half + 1) * 512],
                                wb1[par][:, cc * 512 + m4 * 128: cc * 512 + (m4 + 1) * 128],
                                hid[:, hs * T + half * 512: hs * T + (half + 1) * 512],
                                start=(c == 0), stop=(c == 63)),
                            waits=(([("act", w2cast[(gi, 1)]),
                                     ("dve", last_bf if gi % 16 == 0 else 0)]
                                    if (cc == 0 and m4 == 0 and half == 0) else []) +
                                   ([("dve", evac8.get((mblk - 1, m4), 0))]
                                    if (cc == 0 and half == 0 and gi % 16 == 0) else [])),
                            inc=("pe" if (cc == 3 and m4 == 3 and half == 1) else None))
        blk_pe = p.v("pe")
        for m4 in range(4):
            mchunk = mblk * 4 + m4
            par = m4 % 2
            ev = p.emit("dve", lambda e, m4=m4: e.tensor_add(
                xpb[m4][:, :], xpb[m4][:, :], p8[m4][:, :]),
                waits=[("pe", blk_pe), (f"d_xb{m4}", xb_pref[(mblk, m4)])], inc="dve")
            evac8[(mblk, m4)] = ev
            dv = p.dma("act", out_h.ap().rearrange("(k p) t -> p k t", p=128)[:, mchunk, :],
                       xpb[m4][:, :], waits=[("dve", ev)], inc="d_out")
            out_dma_val[(mblk, m4)] = dv // 16
        blk_dve[mblk] = p.v("dve")
    ctx8.close()

    final_waits = [("d_out", 16 * out_dma_val[(3, 3)])]
    if p.v("d_tap"):
        final_waits.append(("d_tap", p.v("d_tap")))
    p.emit("sp", lambda e: e.nop(), waits=final_waits)

    stack_ffn.close()
    stack_persist.close()
    p.realize()
    return nc


def host_prep(inputs):
    x = np.asarray(inputs["L0_attn_x"], dtype=np.float32)
    wq_t = tile_lhsT(np.asarray(inputs["L0_attn_Wq"], dtype=np.float32))
    wk_t = tile_lhsT(np.asarray(inputs["L0_attn_Wk"], dtype=np.float32))
    wv_t = tile_lhsT(np.asarray(inputs["L0_attn_Wv"], dtype=np.float32))
    wo_t = tile_lhsT(np.asarray(inputs["L0_attn_Wo"], dtype=np.float32))
    w1_t = tile_lhsT(np.ascontiguousarray(np.asarray(inputs["L0_ffn_W1"], dtype=np.float32).T))
    w3_t = tile_lhsT(np.ascontiguousarray(np.asarray(inputs["L0_ffn_W3"], dtype=np.float32).T))
    w2T = np.ascontiguousarray(np.asarray(inputs["L0_ffn_W2"], dtype=np.float32).T)
    w2_t = np.ascontiguousarray(w2T.reshape(64, 128, 4, 512).transpose(2, 0, 1, 3))
    rms_cols = np.ascontiguousarray(np.asarray(inputs["L0_attn_rms_w"], dtype=np.float32).reshape(16, 128).T)
    in_maps = []
    zeros_xp = np.zeros((D, T), dtype=np.float32)
    for core in range(8):
        b, half = core // 2, core % 2
        xT = np.ascontiguousarray(x[b, half * T:(half + 1) * T, :].T)
        if half == 1:
            xp2 = np.ascontiguousarray(x[b, 0:T, :].T)
        else:
            xp2 = zeros_xp
        m = {"x": xT, "xp2": xp2, "wq_t": wq_t, "wk_t": wk_t, "wv_t": wv_t, "wo_t": wo_t,
             "w1_t": w1_t, "w3_t": w3_t, "w2_t": w2_t, "rms_cols": rms_cols}
        m.update(host_constants(core))
        in_maps.append(m)
    return in_maps


def kernel(**inputs):
    in_maps = host_prep(inputs)
    nc = build_program()
    res = run_bass_kernel_spmd(nc, in_maps, core_ids=list(range(8)))
    out = np.empty((4, 2048, D), dtype=np.float32)
    for core in range(8):
        b, half = core // 2, core % 2
        out[b, half * T:(half + 1) * T, :] = res.results[core]["out"].T
    return out
